# version 02
# baseline (speedup 1.0000x reference)
import numpy as np
import ml_dtypes
import concourse.bass as bass
import concourse.mybir as mybir
from concourse.bass_utils import run_bass_kernel_spmd
from contextlib import ExitStack

F32 = mybir.dt.float32
F16 = mybir.dt.float16
BF16 = mybir.dt.bfloat16
U8 = mybir.dt.uint8
I32 = mybir.dt.int32
AF = mybir.ActivationFunctionType
ALU = mybir.AluOpType

D = 4096
S = 4096
B = 2
NH, NKV, HD = 32, 8, 128
QO, KO, VO = 4096, 1024, 1024
INTER = 11008
GS = 128
NT_I = INTER // 128
NT_D = D // 128
TLOC = 1024
ROPE_BASE = 500000.0


def rope_tables():
    inv_freq = (1.0 / ROPE_BASE ** (np.arange(0, HD, 2, dtype=np.float32) / np.float32(HD))).astype(np.float32)
    t = np.arange(S, dtype=np.float32)
    freqs = (t[:, None] * inv_freq[None, :]).astype(np.float32)
    cos = np.cos(freqs.astype(np.float64)).astype(np.float32)
    sin = np.sin(freqs.astype(np.float64)).astype(np.float32)
    cosF = np.concatenate([cos.T, cos.T], axis=0)
    sinF = np.concatenate([sin.T, sin.T], axis=0)
    return np.ascontiguousarray(cosF), np.ascontiguousarray(sinF)

def qweight_bytesT(qw):
    KD8, N = qw.shape
    b = qw.view(np.uint8).reshape(KD8, N, 4)
    return np.ascontiguousarray(b.transpose(1, 0, 2).reshape(N, KD8 * 4))

def perm_sign_P64T():
    P = np.zeros((128, 128), np.float32)
    for i in range(64):
        P[i, i + 64] = -1.0
        P[i + 64, i] = 1.0
    return np.ascontiguousarray(P.T).astype(ml_dtypes.bfloat16)

def build_masks(g):
    out = np.zeros((32, 128, 512), np.float32)
    qlo = 512 * g
    qhi = 512 * (7 - g)
    kt = np.arange(128)
    qt = np.arange(512)
    for s_ in range(16):
        out[s_] = (128 * s_ + kt[:, None]) <= (qlo + qt[None, :])
    for j, s_ in enumerate(range(16, 32)):
        out[16 + j] = (128 * s_ + kt[:, None]) <= (qhi + qt[None, :])
    return np.ascontiguousarray(out).astype(ml_dtypes.bfloat16)

def onehot_lhsT(n):
    oh = np.zeros((n, 128, n), np.float32)
    for j in range(n):
        oh[j, :, j] = 1.0
    return oh.astype(ml_dtypes.bfloat16)

def stage_inputs(inputs):
    x = np.asarray(inputs["L0_attn_x"])
    w_qkv = np.asarray(inputs["L0_attn_w_qkv"])
    w_o = np.asarray(inputs["L0_attn_w_o"])
    st = {}
    st["xt0"] = np.ascontiguousarray(x[0].T)
    st["xt1"] = np.ascontiguousarray(x[1].T)
    st["wqkvT"] = np.ascontiguousarray(w_qkv.T)
    st["woT"] = np.ascontiguousarray(w_o.T)
    for nm in ("gate", "up", "down"):
        st[f"hb_{nm}"] = qweight_bytesT(np.asarray(inputs[f"L0_ffn_{nm}_qweight"]))
        sc = np.asarray(inputs[f"L0_ffn_{nm}_scales"])
        st[f"sT_{nm}"] = np.ascontiguousarray(sc.T)
        st[f"s_{nm}"] = sc
        st[f"z_{nm}"] = np.ascontiguousarray(
            np.asarray(inputs[f"L0_ffn_{nm}_zeros"]).view(np.uint8))
    cosF, sinF = rope_tables()
    st["cosK"] = cosF
    st["sinK"] = sinF
    scale = np.float32(1.0) / np.sqrt(np.float64(HD)).astype(np.float32)
    st["cosQ"] = (cosF.astype(np.float64) * float(scale)).astype(np.float32)
    st["sinQ"] = (sinF.astype(np.float64) * float(scale)).astype(np.float32)
    st["p64t"] = perm_sign_P64T()
    st["i128"] = np.eye(128, dtype=np.float32)
    st["i128b"] = np.eye(128, dtype=np.float32).astype(ml_dtypes.bfloat16)
    st["oh32"] = onehot_lhsT(32)
    st["oh86"] = onehot_lhsT(86)
    percore = []
    for c in range(8):
        b, g = c // 4, c % 4
        d = dict(st)
        d["xt"] = st["xt0"] if b == 0 else st["xt1"]
        del d["xt0"], d["xt1"]
        d["masks"] = build_masks(g)
        loc = np.r_[512 * g:512 * (g + 1), 512 * (7 - g):512 * (8 - g)]
        d["xtl"] = np.ascontiguousarray(d["xt"][:, loc])
        d["xn2t"] = np.zeros((4096, 1024), ml_dtypes.bfloat16)
        d["offs"] = np.array([512 * g, 512 * (7 - g)], dtype=np.int32)
        percore.append(d)
    return percore


class Plan:
    ENGINES = ("t", "v", "s", "g", "y")

    def __init__(self):
        self.ops = {e: [] for e in self.ENGINES}
        self.cnt = {}
        self.sem_names = set()
        self.phase_barrier = None
        self.phase_seen = set()

    def tok(self, sem):
        return (sem, self.cnt.get(sem, 0))

    def start_phase(self):
        self.phase_barrier = [(s, v) for s, v in self.cnt.items()]
        self.phase_seen = set()

    def op(self, eng, fn, waits=(), inc=None, amt=1, tag=""):
        waits = [w for w in waits if w is not None]
        if self.phase_barrier is not None and eng not in self.phase_seen:
            waits = list(waits) + self.phase_barrier
            self.phase_seen.add(eng)
        if inc is None:
            inc = eng
        self.sem_names.add(inc)
        for w in waits:
            self.sem_names.add(w[0])
        self.cnt[inc] = self.cnt.get(inc, 0) + amt
        self.ops[eng].append({"fn": fn, "waits": waits, "inc": inc,
                              "amt": amt, "tag": tag})
        return (inc, self.cnt[inc])

    def dma(self, eng, q, fn, waits=(), tag=""):
        return self.op(eng, fn, waits=waits, inc=q, amt=16, tag=tag)

    def barrier_token_all(self):
        return [self.tok(e) for e in ("t", "v", "s")]

    def emit(self, engines, semh):
        for name, eng in engines.items():
            last = {}
            for entry in self.ops[name]:
                for (sem, val) in entry["waits"]:
                    if last.get(sem, -1) >= val:
                        continue
                    eng.wait_ge(semh[sem], val)
                    last[sem] = val
                ins = entry["fn"](eng)
                if ins is not None:
                    ins.then_inc(semh[entry["inc"]], entry["amt"])
                else:
                    raise RuntimeError(f"op returned no instruction: {entry['tag']}")


class Arena:
    def __init__(self, handle, nbytes):
        self.h = handle
        self.nbytes = nbytes

    def view(self, off, dtype, shape):
        import math
        isz = {F32: 4, BF16: 2, U8: 1, I32: 4, F16: 2}[dtype]
        n = math.prod(shape)
        assert off % 64 == 0, off
        assert off + n * isz <= self.nbytes, (off, n * isz, self.nbytes)
        ap = self.h[:, off:off + n * isz]
        ap = ap.bitcast(dtype)
        if len(shape) > 1:
            dims = " ".join(f"d{i}" for i in range(len(shape)))
            kw = {f"d{i}": shape[i] for i in range(len(shape))}
            ap = ap.rearrange(f"p ({dims}) -> p {dims}", **kw)
        return ap




LNC = 128
SUPS = [(0, 1536), (1536, 1536), (3072, 1024)]


def build_nc(phases=("PD", "PK", "PQ", "PA", "PO", "PF1", "PF2", "PL"), debug=()):
    nc = bass.Bass("TRN2", target_bir_lowering=False, debug=False)

    def din(name, shape, dt):
        return nc.dram_tensor(name, shape, dt, kind="ExternalInput").ap()

    xt = din("xt", [D, S], F32)
    xtl = din("xtl", [D, TLOC], F32)
    xn2t = din("xn2t", [D, TLOC], BF16)
    wqkvT = din("wqkvT", [D, 6144], F32)
    woT = din("woT", [D, D], F32)
    hb_g = din("hb_gate", [INTER, D // 2], U8)
    hb_u = din("hb_up", [INTER, D // 2], U8)
    hb_d = din("hb_down", [D, INTER // 2], U8)
    sT_g = din("sT_gate", [INTER, 32], F16)
    sT_u = din("sT_up", [INTER, 32], F16)
    sT_d = din("sT_down", [D, 86], F16)
    s_g = din("s_gate", [32, INTER], F16)
    s_u = din("s_up", [32, INTER], F16)
    s_d = din("s_down", [86, D], F16)
    z_g = din("z_gate", [32, INTER // 2], U8)
    z_u = din("z_up", [32, INTER // 2], U8)
    z_d = din("z_down", [86, D // 2], U8)
    cosK = din("cosK", [128, S], F32)
    sinK = din("sinK", [128, S], F32)
    cosQ = din("cosQ", [128, S], F32)
    sinQ = din("sinQ", [128, S], F32)
    p64t = din("p64t", [128, 128], BF16)
    i128 = din("i128", [128, 128], F32)
    i128b = din("i128b", [128, 128], BF16)
    oh32 = din("oh32", [32, 128, 32], BF16)
    oh86 = din("oh86", [86, 128, 86], BF16)
    masks = din("masks", [32, 128, 512], BF16)
    offs = din("offs", [1, 2], I32)

    out_d = nc.dram_tensor("out", [D, TLOC], F32, kind="ExternalOutput").ap()

    def scratch(name, shape, dt):
        kind = "ExternalOutput" if name in debug else "Internal"
        return nc.dram_tensor(name, shape, dt, kind=kind).ap()

    xn_d = scratch("xn_d", [D, S], BF16)
    k_d = scratch("k_d", [1024, S], BF16)
    v_d = scratch("v_d", [S, 1024], BF16)
    x2_d = scratch("x2_d", [D, TLOC], F32)
    proj_d = scratch("proj_d", [D, TLOC], BF16)
    attn_d = scratch("attn_d", [D, TLOC], BF16)
    inter_d = scratch("inter_d", [INTER, TLOC], BF16)
    wtg_d = scratch("wtg_d", [32, 128, INTER], BF16)
    wtu_d = scratch("wtu_d", [32, 128, INTER], BF16)
    wtd_d = scratch("wtd_d", [86, 128, D], BF16)
    zpg_d = scratch("zpg_d", [32, INTER], BF16)
    zpu_d = scratch("zpu_d", [32, INTER], BF16)
    zpd_d = scratch("zpd_d", [86, D], BF16)
    q_d = scratch("q_d", [4096, TLOC], BF16)
    mdbg_d = scratch("mdbg", [16, 128, 512], BF16)
    wcdbg_d = scratch("wcdbg", [128, 8 * 384], BF16)
    kvdbg_d = scratch("kvdbg", [2, 128, 4096], BF16)
    xn2dbg_d = scratch("xn2dbg", [4096, 1024], BF16)
    r3dbg_d = scratch("r3dbg", [4, 128, 512], F32)

    P = Plan()
    AB = 200 * 1024

    stk = ExitStack()
    with stk:
        arena_t = stk.enter_context(nc.sbuf_tensor("arena", [128, AB], U8))
        A = Arena(arena_t, AB)
        psA = stk.enter_context(nc.psum_tensor("psA", [128, 4096], F32))

        def bank(i, cols=512, rows=128, row0=0):
            return psA[row0:row0 + rows, 512 * i:512 * i + cols]

        ones128 = A.view(0, BF16, (64,))[:, 0:1]
        onecol_t = A.view(128, BF16, (128,))
        onecol = onecol_t[0:1, :]
        p64_sb = A.view(384, BF16, (128,))
        i128_sb = A.view(640, F32, (128,))
        eps_sb = A.view(1152, F32, (16,))[0:1, 0:1]
        invS_sb = A.view(1216, F32, (16,))[0:1, 0:1]
        offs_sb = A.view(1280, I32, (16,))[0:1, 0:2]
        i128b_sb = A.view(1408, BF16, (128,))
        CONST_END = 2048

        P.op("g", lambda g: g.memset(ones128, 1.0), tag="c0")
        P.op("g", lambda g: g.memset(onecol_t[0:1, :], 1.0), tag="c1")
        P.op("g", lambda g: g.memset(eps_sb, 1e-5), tag="c2")
        P.op("g", lambda g: g.memset(invS_sb, 1.0 / 4096.0), tag="c3")
        P.dma("g", "gd0", lambda g: g.dma_start(out=p64_sb, in_=p64t), tag="c4")
        P.dma("g", "gd0", lambda g: g.dma_start(out=i128_sb, in_=i128), tag="c5")
        P.dma("g", "gd0", lambda g: g.dma_start(out=i128b_sb, in_=i128b), tag="c5b")
        P.dma("y", "yd0", lambda y: y.dma_start(out=offs_sb, in_=offs), tag="c6")

        def mkseg(base):
            cur = [base]
            def seg(nbytes):
                off = cur[0]
                cur[0] += (nbytes + 63) & ~63
                assert cur[0] <= AB, (off, nbytes, cur[0])
                return off
            return seg, cur

        if "PD" in phases:
            seg, cur = mkseg(CONST_END)
            HB = [A.view(seg(5504), U8, (5504,)) for _ in range(2)]
            NIB = [A.view(seg(11008), U8, (11008,)) for _ in range(2)]
            WBF = [A.view(seg(22016), BF16, (11008,)) for _ in range(2)]
            STS16 = A.view(seg(5504), F16, (2752,))
            STS = A.view(seg(11008), F32, (2752,))
            ZB = A.view(seg(5504), U8, (5504,))
            ZNIB = A.view(seg(11008), U8, (11008,))
            ZT = A.view(seg(22016), F32, (5504,))
            SN16 = A.view(seg(11008), F16, (5504,))
            SN = A.view(seg(22016), F32, (5504,))
            ZPB = A.view(seg(11008), BF16, (5504,))
            TEV = [A.view(seg(2048), BF16, (1024,)) for _ in range(2)]
            PSB = psA[:, 3584:4096].bitcast(BF16)

            hb_tok = [None, None]
            nib_tok = [None, None]
            wbf_tok = [None, None]
            tev_free = [None, None]
            psb_free = None
            last_sc = None
            st_cast = None
            for mi, (hb, sTp, wt, N, K) in enumerate([
                    (hb_g, sT_g, wtg_d, INTER, D),
                    (hb_u, sT_u, wtu_d, INTER, D),
                    (hb_d, sT_d, wtd_d, D, INTER)]):
                NT, G = N // 128, K // 128
                st_ld = P.dma("g", "pdt", lambda g, sTp=sTp, NT=NT, G=G: g.dma_start(
                    out=STS16[:, 0:NT * G].rearrange("p (n g) -> p n g", n=NT),
                    in_=sTp.rearrange("(n p) g -> p n g", p=128)),
                    waits=[st_cast], tag=f"pd_sT{mi}")
                st_cast = P.op("v", lambda v, NT=NT, G=G: v.tensor_copy(
                    STS[:, 0:NT * G], STS16[:, 0:NT * G]),
                    waits=[st_ld, last_sc], tag=f"pd_sTc{mi}")
                for nt in range(NT):
                    bi = nt % 2
                    hb_tok[bi] = P.dma("y", f"pdh{bi}",
                        lambda y, hb=hb, nt=nt, K=K, bi=bi: y.dma_start(
                            out=HB[bi][:, 0:K // 2], in_=hb[nt * 128:(nt + 1) * 128, :]),
                        waits=[nib_tok[bi]], tag=f"pd_hb{mi}_{nt}")
                    t1 = P.op("v", lambda v, bi=bi, K=K: v.tensor_scalar(
                        out=NIB[bi][:, 0:K:2], in0=HB[bi][:, 0:K // 2], scalar1=15,
                        scalar2=None, op0=ALU.bitwise_and),
                        waits=[hb_tok[bi], wbf_tok[bi]], tag=f"pd_lo{mi}_{nt}")
                    t2 = P.op("v", lambda v, bi=bi, K=K: v.tensor_scalar(
                        out=NIB[bi][:, 1:K:2], in0=HB[bi][:, 0:K // 2], scalar1=4,
                        scalar2=None, op0=ALU.logical_shift_right),
                        waits=[hb_tok[bi], wbf_tok[bi]], tag=f"pd_hi{mi}_{nt}")
                    nib_tok[bi] = t2
                    last_sc = P.op("v", lambda v, bi=bi, nt=nt, G=G, K=K, NT=NT:
                        v.tensor_tensor(
                            out=WBF[bi][:, 0:K].rearrange("p (g k) -> p g k", g=G),
                            in0=NIB[bi][:, 0:K].rearrange("p (g k) -> p g k", g=G),
                            in1=STS[:, 0:NT * G].rearrange("p (n g) -> p n g", n=NT)
                                [:, nt, :][:, :, None].broadcast_to([128, G, 128]),
                            op=ALU.mult),
                        waits=[t1, t2, st_cast], tag=f"pd_sc{mi}_{nt}")
                    ntr = K // 128
                    trt = None
                    for b in range((ntr + 7) // 8):
                        nk = min(8, ntr - b * 8)
                        for j in range(nk):
                            kt = b * 8 + j
                            trt = P.op("t", lambda t, bi=bi, kt=kt, j=j: t.transpose(
                                PSB[:, j * 128:(j + 1) * 128],
                                WBF[bi][:, kt * 128:(kt + 1) * 128], i128b_sb),
                                waits=([last_sc, psb_free] if j == 0 else []),
                                tag=f"pd_tr{mi}_{nt}_{kt}")
                        tsl = b % 2
                        ev = P.op("v", lambda v, tsl=tsl, nk=nk: v.tensor_copy(
                            TEV[tsl][:, 0:nk * 128], PSB[:, 0:nk * 128]),
                            waits=[trt, tev_free[tsl]], tag=f"pd_te{mi}_{nt}_{b}")
                        psb_free = ev
                        tev_free[tsl] = P.dma("s", f"pdw{tsl}",
                            lambda y, wt=wt, nt=nt, b=b, nk=nk, tsl=tsl: y.dma_start(
                                out=wt[b * 8:b * 8 + nk, :, nt * 128:(nt + 1) * 128]
                                    .rearrange("a p c -> p a c"),
                                in_=TEV[tsl][:, 0:nk * 128]
                                    .rearrange("p (a c) -> p a c", a=nk)),
                            waits=[ev], tag=f"pd_wst{mi}_{nt}_{b}")
                    wbf_tok[bi] = trt

            zb_tok = znib_tok = zt_tok = zpb_tok = None
            sn_cast = None
            for mi, (zby, sn, zp_o, N, G) in enumerate([
                    (z_g, s_g, zpg_d, INTER, 32),
                    (z_u, s_u, zpu_d, INTER, 32),
                    (z_d, s_d, zpd_d, D, 86)]):
                zb_tok = P.dma("y", "pdz", lambda y, zby=zby, G=G, N=N: y.dma_start(
                    out=ZB[0:G, 0:N // 2], in_=zby),
                    waits=[znib_tok], tag=f"pd_zb{mi}")
                e1 = P.op("v", lambda v, G=G, N=N: v.tensor_scalar(
                    out=ZNIB[0:G, 0:N:2], in0=ZB[0:G, 0:N // 2], scalar1=15,
                    scalar2=None, op0=ALU.bitwise_and),
                    waits=[zb_tok, zt_tok], tag=f"pd_zlo{mi}")
                e2 = P.op("v", lambda v, G=G, N=N: v.tensor_scalar(
                    out=ZNIB[0:G, 1:N:2], in0=ZB[0:G, 0:N // 2], scalar1=4,
                    scalar2=None, op0=ALU.logical_shift_right),
                    waits=[zb_tok, zt_tok], tag=f"pd_zhi{mi}")
                znib_tok = e2
                for h in range(2):
                    NH_ = N // 2
                    sn_ld = P.dma("g", "pdn", lambda g, sn=sn, G=G, NH_=NH_, h=h:
                        g.dma_start(out=SN16[0:G, 0:NH_], in_=sn[:, h * NH_:(h + 1) * NH_]),
                        waits=[sn_cast], tag=f"pd_sn{mi}_{h}")
                    sn_cast = P.op("v", lambda v, G=G, NH_=NH_: v.tensor_copy(
                        SN[0:G, 0:NH_], SN16[0:G, 0:NH_]),
                        waits=[sn_ld, zpb_tok], tag=f"pd_snc{mi}_{h}")
                    zt_tok = P.op("v", lambda v, G=G, NH_=NH_, h=h: v.tensor_scalar(
                        out=ZT[0:G, 0:NH_], in0=ZNIB[0:G, h * NH_:(h + 1) * NH_],
                        scalar1=-1.0, scalar2=-1.0, op0=ALU.mult, op1=ALU.add),
                        waits=[e1, e2, zpb_tok], tag=f"pd_zt{mi}_{h}")
                    zpv = P.op("v", lambda v, G=G, NH_=NH_: v.tensor_tensor(
                        out=ZPB[0:G, 0:NH_], in0=ZT[0:G, 0:NH_], in1=SN[0:G, 0:NH_],
                        op=ALU.mult),
                        waits=[zt_tok, sn_cast], tag=f"pd_zp{mi}_{h}")
                    zpb_tok = P.dma("y", "pdp", lambda y, zp_o=zp_o, G=G, NH_=NH_, h=h:
                        y.dma_start(out=zp_o[:, h * NH_:(h + 1) * NH_], in_=ZPB[0:G, 0:NH_]),
                        waits=[zpv], tag=f"pd_zpst{mi}_{h}")

        if "PK" in phases:
            P.start_phase()
            seg, cur = mkseg(CONST_END)
            XN = A.view(seg(98304), BF16, (32, 1536))
            XF = [A.view(seg(16384), F32, (32, LNC)) for _ in range(2)]
            WKVF = A.view(seg(16384), F32, (32, 128))
            WKV = [A.view(seg(8192), BF16, (32, 128)) for _ in range(2)]
            COSS = A.view(seg(6144), F32, (3, 512))
            SINS = A.view(seg(6144), F32, (3, 512))
            KSB = A.view(seg(1024), BF16, (512,))
            TMP1 = A.view(seg(2048), F32, (512,))
            TMP2 = A.view(seg(2048), F32, (512,))
            KR = A.view(seg(1024), BF16, (512,))
            VSB = A.view(seg(2048), F32, (512,))
            VT = A.view(seg(1024), BF16, (512,))
            RSTB = A.view(seg(512), F32, (LNC,))
            MRB = A.view(seg(512), F32, (LNC,))
            XSQ = A.view(seg(8192), BF16, (32, LNC))
            ROWM = A.view(seg(512), F32, (LNC,))[0:1, :]
            ROWS2 = A.view(seg(512), F32, (LNC,))[0:1, :]
            ROWT = A.view(seg(512), F32, (LNC,))[0:1, :]
            ROWRS = A.view(seg(512), F32, (LNC,))[0:1, :]
            ROWMR = A.view(seg(512), F32, (LNC,))[0:1, :]
            ROWRSB = A.view(seg(256), BF16, (LNC,))[0:1, :]
            ROWMRB = A.view(seg(256), BF16, (LNC,))[0:1, :]

            bank_tok = {}
            wkv_tok = [None, None]
            wcast_tok = [None, None]
            kv_w_free = {}
            sup_end_tok = []
            xf_free = [None, None]
            cs_use = []
            for sup, (s0, SW) in enumerate(SUPS):
                n_tt = SW // 512
                cs1 = P.dma("y", "pkc", lambda y, s0=s0, SW=SW, n_tt=n_tt: y.dma_start(
                    out=COSS[:, 0:n_tt, :], in_=cosK[:, s0:s0 + SW]
                    .rearrange("p (a b) -> p a b", b=512)),
                    waits=cs_use, tag=f"pk_cos{sup}")
                cs2 = P.dma("y", "pkc", lambda y, s0=s0, SW=SW, n_tt=n_tt: y.dma_start(
                    out=SINS[:, 0:n_tt, :], in_=sinK[:, s0:s0 + SW]
                    .rearrange("p (a b) -> p a b", b=512)),
                    waits=cs_use, tag=f"pk_sin{sup}")

                n_ch = SW // LNC
                m2_carry = [None, None]
                apply_last = {}
                for ch in range(n_ch):
                    c0 = ch * LNC
                    bi = ch % 2
                    xld = P.dma("g", f"pkx{bi}", lambda g, s0=s0, c0=c0, bi=bi: g.dma_start(
                        out=XF[bi][:],
                        in_=xt[:, s0 + c0:s0 + c0 + LNC].rearrange("(i p) t -> p i t", p=128)),
                        waits=[xf_free[bi]], tag=f"pk_x{sup}_{ch}")
                    cast = P.op("s", lambda s, bi=bi, c0=c0: s.activation(
                        out=XN[:, :, c0:c0 + LNC], in_=XF[bi][:], func=AF.Copy),
                        waits=[xld] + sup_end_tok +
                              ([apply_last[ch - 2]] if ch >= 2 else []),
                        tag=f"pk_cast{sup}_{ch}")
                    sq = P.op("s", lambda s, c0=c0: s.activation(
                        out=XSQ[:], in_=XN[:, :, c0:c0 + LNC], func=AF.Square),
                        waits=[cast, m2_carry[bi]], tag=f"pk_sq{sup}_{ch}")
                    mm_tok = None
                    for i in range(32):
                        P.op("t", lambda t, i=i, c0=c0: t.matmul(
                            bank(4, LNC, rows=1), ones128, XN[:, i, c0:c0 + LNC],
                            start=(i == 0), stop=(i == 31)),
                            waits=[cast, bank_tok.get(4) if i == 0 else None],
                            tag=f"pk_s1_{sup}_{ch}_{i}")
                        mm_tok = P.op("t", lambda t, i=i: t.matmul(
                            bank(5, LNC, rows=1), ones128, XSQ[:, i, :],
                            start=(i == 0), stop=(i == 31)),
                            waits=[sq, bank_tok.get(5) if i == 0 else None],
                            tag=f"pk_s2_{sup}_{ch}_{i}")
                    m2_carry[bi] = mm_tok
                    mm1_tok = P.tok("t")
                    e1 = P.op("s", lambda s: s.activation(
                        out=ROWM, in_=bank(4, LNC, rows=1), func=AF.Identity, scale=invS_sb),
                        waits=[mm_tok], tag=f"pk_m{sup}_{ch}")
                    e2 = P.op("s", lambda s: s.activation(
                        out=ROWS2, in_=bank(5, LNC, rows=1), func=AF.Identity, scale=invS_sb),
                        waits=[mm_tok], tag=f"pk_s2r{sup}_{ch}")
                    bank_tok[4] = e1
                    bank_tok[5] = e2
                    v1 = P.op("v", lambda v: v.tensor_tensor(out=ROWT, in0=ROWM, in1=ROWM,
                        op=ALU.mult), waits=[e1], tag=f"pk_mm{sup}_{ch}")
                    v2 = P.op("v", lambda v: v.tensor_tensor(out=ROWT, in0=ROWS2, in1=ROWT,
                        op=ALU.subtract), waits=[v1, e2], tag=f"pk_var{sup}_{ch}")
                    e3 = P.op("s", lambda s: s.activation(out=ROWRS, in_=ROWT, func=AF.Sqrt,
                        bias=eps_sb), waits=[v2], tag=f"pk_sqrt{sup}_{ch}")
                    v3 = P.op("v", lambda v: v.reciprocal(ROWRS, ROWRS), waits=[e3],
                        tag=f"pk_rcp{sup}_{ch}")
                    v4 = P.op("v", lambda v: v.tensor_tensor(out=ROWMR, in0=ROWM, in1=ROWRS,
                        op=ALU.mult), waits=[v3], tag=f"pk_mr{sup}_{ch}")
                    v5 = P.op("v", lambda v: v.tensor_copy(ROWRSB, ROWRS), waits=[v3],
                        tag=f"pk_cb1{sup}_{ch}")
                    v6 = P.op("v", lambda v: v.tensor_copy(ROWMRB, ROWMR), waits=[v4],
                        tag=f"pk_cb2{sup}_{ch}")
                    b1 = P.op("t", lambda t: t.matmul(bank(4, LNC), onecol, ROWRSB,
                        start=True, stop=True), waits=[v5, bank_tok.get(4)], tag=f"pk_b1{sup}_{ch}")
                    b2 = P.op("t", lambda t: t.matmul(bank(5, LNC), onecol, ROWMRB,
                        start=True, stop=True), waits=[v6, bank_tok.get(5)], tag=f"pk_b2{sup}_{ch}")
                    eb1 = P.op("s", lambda s: s.activation(out=RSTB, in_=bank(4, LNC),
                        func=AF.Copy), waits=[b1], tag=f"pk_eb1{sup}_{ch}")
                    eb2 = P.op("s", lambda s: s.activation(out=MRB, in_=bank(5, LNC),
                        func=AF.Copy), waits=[b2], tag=f"pk_eb2{sup}_{ch}")
                    bank_tok[4] = eb1
                    bank_tok[5] = eb2
                    a1 = P.op("v", lambda v, bi=bi, c0=c0: v.tensor_tensor(
                        out=XF[bi][:], in0=XN[:, :, c0:c0 + LNC],
                        in1=RSTB[:, None, :].broadcast_to([128, 32, LNC]), op=ALU.mult),
                        waits=[eb1, mm1_tok, sq], tag=f"pk_a1{sup}_{ch}")
                    a2 = P.op("v", lambda v, bi=bi, c0=c0: v.tensor_tensor(
                        out=XN[:, :, c0:c0 + LNC], in0=XF[bi][:],
                        in1=MRB[:, None, :].broadcast_to([128, 32, LNC]), op=ALU.subtract),
                        waits=[a1, eb2], tag=f"pk_a2{sup}_{ch}")
                    apply_last[ch] = a2
                    xf_free[bi] = a2
                    P.dma("y", "pkn", lambda y, s0=s0, c0=c0: y.dma_start(
                        out=xn_d[:, s0 + c0:s0 + c0 + LNC].rearrange("(i p) t -> p i t", p=128),
                        in_=XN[:, :, c0:c0 + LNC]),
                        waits=[a2], tag=f"pk_xnst{sup}_{ch}")
                apply_toks = [apply_last[n_ch - 1], apply_last[n_ch - 2]]

                for ot in range(16):
                    bi = ot % 2
                    wkvf_tok = P.dma("g", "pkwf", lambda y, ot=ot: y.dma_start(
                        out=WKVF[:],
                        in_=wqkvT[:, 4096 + ot * 128:4096 + (ot + 1) * 128]
                            .rearrange("(i p) c -> p i c", p=128)),
                        waits=[wcast_tok[1 - bi]], tag=f"pk_wf{sup}_{ot}")
                    wcast_tok[bi] = P.op("s", lambda s, bi=bi: s.activation(
                        out=WKV[bi][:], in_=WKVF[:], func=AF.Copy),
                        waits=[wkvf_tok, kv_w_free.get(bi)], tag=f"pk_wc{sup}_{ot}")
                    wkv_tok[bi] = wcast_tok[bi]
                    for tt in range(n_tt):
                        bq = (ot * n_tt + tt) % 4
                        mm = None
                        for i in range(32):
                            mm = P.op("t", lambda t, bi=bi, i=i, tt=tt, bq=bq: t.matmul(
                                bank(bq), WKV[bi][:, i, :], XN[:, i, tt * 512:(tt + 1) * 512],
                                start=(i == 0), stop=(i == 31)),
                                waits=([wkv_tok[bi], bank_tok.get(bq)] +
                                       (apply_toks if i == 0 else [])),
                                tag=f"pk_mm{sup}_{ot}_{tt}_{i}")
                        kv_w_free[bi] = mm
                        if ot < 8:
                            ev = P.op("s", lambda s, bq=bq: s.activation(
                                out=KSB, in_=bank(bq), func=AF.Copy),
                                waits=[mm, bank_tok.get("ksb")], tag=f"pk_kev{sup}_{ot}_{tt}")
                            bank_tok[bq] = ev
                            pm = P.op("t", lambda t: t.matmul(bank(6), p64_sb, KSB,
                                start=True, stop=True),
                                waits=[ev, bank_tok.get(6)], tag=f"pk_pm{sup}_{ot}_{tt}")
                            r1 = P.op("v", lambda v, tt=tt: v.tensor_tensor(
                                out=TMP1, in0=KSB, in1=COSS[:, tt, :], op=ALU.mult),
                                waits=[ev, cs1], tag=f"pk_r1{sup}_{ot}_{tt}")
                            r2 = P.op("v", lambda v, tt=tt: v.tensor_tensor(
                                out=TMP2, in0=bank(6), in1=SINS[:, tt, :], op=ALU.mult),
                                waits=[pm, cs2], tag=f"pk_r2{sup}_{ot}_{tt}")
                            bank_tok[6] = r2
                            r3 = P.op("v", lambda v: v.tensor_tensor(
                                out=KR, in0=TMP1, in1=TMP2, op=ALU.add),
                                waits=[r1, r2, bank_tok.get("kr")], tag=f"pk_r3{sup}_{ot}_{tt}")
                            bank_tok["ksb"] = r3
                            kst = P.dma("y", "pkk", lambda y, ot=ot, s0=s0, tt=tt: y.dma_start(
                                out=k_d[ot * 128:(ot + 1) * 128, s0 + tt * 512:s0 + (tt + 1) * 512],
                                in_=KR), waits=[r3], tag=f"pk_kst{sup}_{ot}_{tt}")
                            bank_tok["kr"] = kst
                        else:
                            ev = P.op("s", lambda s, bq=bq: s.activation(
                                out=VSB, in_=bank(bq), func=AF.Copy),
                                waits=[mm, bank_tok.get("vsb")], tag=f"pk_vev{sup}_{ot}_{tt}")
                            bank_tok[bq] = ev
                            trt = None
                            for j in range(4):
                                trt = P.op("t", lambda t, j=j: t.transpose(
                                    bank(7)[:, j * 128:(j + 1) * 128],
                                    VSB[:, j * 128:(j + 1) * 128], i128_sb),
                                    waits=[ev, bank_tok.get(7) if j == 0 else None],
                                    tag=f"pk_tr{sup}_{ot}_{tt}_{j}")
                            vt = P.op("s", lambda s: s.activation(
                                out=VT, in_=bank(7), func=AF.Copy),
                                waits=[trt, bank_tok.get("vt")], tag=f"pk_vt{sup}_{ot}_{tt}")
                            bank_tok[7] = vt
                            bank_tok["vsb"] = vt
                            vst = P.dma("y", "pkv", lambda y, ot=ot, s0=s0, tt=tt: y.dma_start(
                                out=v_d[s0 + tt * 512:s0 + (tt + 1) * 512,
                                        (ot - 8) * 128:(ot - 7) * 128]
                                    .rearrange("(j p) c -> p j c", p=128),
                                in_=VT[:].rearrange("p (j c) -> p j c", j=4)),
                                waits=[vt], tag=f"pk_vst{sup}_{ot}_{tt}")
                            bank_tok["vt"] = vst
                cs_use = [P.tok("v")]
                sup_end_tok = [P.tok("t"), P.tok("pkn")]

        QRES_OFF = AB - 65536
        if "PQ" in phases:
            P.start_phase()
            seg, cur = mkseg(CONST_END)
            QRES = A.view(QRES_OFF, BF16, (32, 1024))
            XNL = A.view(seg(65536), BF16, (32, 1024))
            WQF = A.view(seg(16384), F32, (32, 128))
            WQ = [A.view(seg(8192), BF16, (32, 128)) for _ in range(2)]
            CQ = A.view(seg(4096), F32, (2, 512))
            SQ_ = A.view(seg(4096), F32, (2, 512))
            KSB2 = A.view(seg(1024), BF16, (512,))
            TMQ1 = A.view(seg(2048), F32, (512,))
            TMQ2 = A.view(seg(2048), F32, (512,))

            rlo = nc.sync.alloc_register("rlo")
            rhi = nc.sync.alloc_register("rhi")
            rg1 = P.op("y", lambda y: y.reg_load(rlo, offs_sb[0:1, 0:1]), tag="pq_rl")
            rg2 = P.op("y", lambda y: y.reg_load(rhi, offs_sb[0:1, 1:2]), tag="pq_rh")
            xnl_toks = []
            for hf, reg, tok in ((0, rlo, rg1), (1, rhi, rg2)):
                xnl_toks.append(P.dma("y", "pql", lambda y, hf=hf, reg=reg: y.dma_start(
                    out=XNL[:, :, hf * 512:(hf + 1) * 512],
                    in_=bass.AP(xn_d.tensor, reg, [[S, 128], [128 * S, 32], [1, 512]])),
                    waits=[tok], tag=f"pq_xnl{hf}"))
                xnl_toks.append(P.dma("y", "pql", lambda y, hf=hf, reg=reg: y.dma_start(
                    out=CQ[:, hf, :],
                    in_=bass.AP(cosQ.tensor, reg, [[S, 128], [1, 512]])),
                    waits=[tok], tag=f"pq_cq{hf}"))
                xnl_toks.append(P.dma("y", "pql", lambda y, hf=hf, reg=reg: y.dma_start(
                    out=SQ_[:, hf, :],
                    in_=bass.AP(sinQ.tensor, reg, [[S, 128], [1, 512]])),
                    waits=[tok], tag=f"pq_sq{hf}"))
            wq_tok = [None, None]
            qcast_tok = [None, None]
            qb_tok = {}
            q_w_free = {}
            for ot in range(32):
                bi = ot % 2
                wqf_tok = P.dma("g", "pqwf", lambda y, ot=ot: y.dma_start(
                    out=WQF[:],
                    in_=wqkvT[:, ot * 128:(ot + 1) * 128].rearrange("(i p) c -> p i c", p=128)),
                    waits=[qcast_tok[1 - bi]], tag=f"pq_wf{ot}")
                qcast_tok[bi] = P.op("s", lambda s, bi=bi: s.activation(
                    out=WQ[bi][:], in_=WQF[:], func=AF.Copy),
                    waits=[wqf_tok, q_w_free.get(bi)], tag=f"pq_wc{ot}")
                wq_tok[bi] = qcast_tok[bi]
                for th in range(2):
                    bq = (ot * 2 + th) % 4
                    mm = None
                    for i in range(32):
                        mm = P.op("t", lambda t, bi=bi, i=i, th=th, bq=bq: t.matmul(
                            bank(bq), WQ[bi][:, i, :], XNL[:, i, th * 512:(th + 1) * 512],
                            start=(i == 0), stop=(i == 31)),
                            waits=([wq_tok[bi], qb_tok.get(bq)] +
                                   (xnl_toks if ot == 0 and i == 0 else [])),
                            tag=f"pq_mm{ot}_{th}_{i}")
                    q_w_free[bi] = mm
                    ev = P.op("s", lambda s, bq=bq: s.activation(
                        out=KSB2, in_=bank(bq), func=AF.Copy),
                        waits=[mm, qb_tok.get("ksb")], tag=f"pq_ev{ot}_{th}")
                    qb_tok[bq] = ev
                    pm = P.op("t", lambda t: t.matmul(bank(6), p64_sb, KSB2,
                        start=True, stop=True), waits=[ev, qb_tok.get(6)], tag=f"pq_pm{ot}_{th}")
                    r1 = P.op("v", lambda v, th=th: v.tensor_tensor(
                        out=TMQ1, in0=KSB2, in1=CQ[:, th, :], op=ALU.mult),
                        waits=[ev], tag=f"pq_r1{ot}_{th}")
                    r2 = P.op("v", lambda v, th=th: v.tensor_tensor(
                        out=TMQ2, in0=bank(6), in1=SQ_[:, th, :], op=ALU.mult),
                        waits=[pm], tag=f"pq_r2{ot}_{th}")
                    qb_tok[6] = r2
                    r3 = P.op("v", lambda v, ot=ot, th=th: v.tensor_tensor(
                        out=QRES[:, ot, th * 512:(th + 1) * 512], in0=TMQ1, in1=TMQ2,
                        op=ALU.add), waits=[r1, r2], tag=f"pq_r3{ot}_{th}")
                    qb_tok["ksb"] = r3
            if "q_d" in debug:
                P.dma("y", "pqd", lambda y: y.dma_start(
                    out=q_d.rearrange("(i p) t -> p i t", p=128), in_=QRES[:]),
                    waits=[P.tok("v")], tag="pq_qdump")

        if "PA" in phases:
            P.start_phase()
            seg, cur = mkseg(CONST_END)
            QRES = A.view(QRES_OFF, BF16, (32, 1024))
            KW = [A.view(seg(8192), BF16, (4096,)) for _ in range(2)]
            VW = [A.view(seg(8192), BF16, (32, 128)) for _ in range(2)]
            MSK = A.view(seg(16384), BF16, (16, 512))
            ESB = [A.view(seg(1024), BF16, (512,)) for _ in range(2)]
            DENR = A.view(seg(2048), F32, (512,))[0:1, :]
            DENB = A.view(seg(1024), BF16, (512,))[0:1, :]
            REC = A.view(seg(2048), F32, (512,))
            AVN = A.view(seg(1024), BF16, (512,))

            kw_tok = [None, None]
            vw_tok = [None, None]
            kw_free = [None, None]
            vw_free = [None, None]
            esb_free = [None, None]
            bankA = {}
            denb_tok = denr_tok = rec_tok = avn_tok = ast_tok = None
            msk_tok = None
            for chunk in range(2):
                bound = 16 if chunk == 0 else 32
                mstep0 = 0 if chunk == 0 else 16
                msk_tok = P.dma("y", "pam", lambda y, chunk=chunk: y.dma_start(
                    out=MSK[:], in_=masks[chunk * 16:(chunk + 1) * 16]
                    .rearrange("m p c -> p m c")),
                    waits=[P.tok("v")] if chunk == 1 else [], tag=f"pa_m{chunk}")
                if chunk == 0 and "mdbg" in debug:
                    P.dma("y", "pamd", lambda y: y.dma_start(
                        out=mdbg_d.rearrange("m p c -> p m c"), in_=MSK[:]),
                        waits=[msk_tok], tag="pa_mdump")
                for kvh in range(8):
                    bi = kvh % 2
                    kw_tok[bi] = P.dma("y", f"pak{bi}", lambda y, kvh=kvh, bound=bound, bi=bi:
                        y.dma_start(out=KW[bi][:, 0:bound * 128],
                                    in_=k_d[kvh * 128:(kvh + 1) * 128, 0:bound * 128]),
                        waits=[kw_free[bi]], tag=f"pa_kw{chunk}_{kvh}")
                    vw_tok[bi] = P.dma("y", f"pav{bi}", lambda y, kvh=kvh, bound=bound, bi=bi:
                        y.dma_start(out=VW[bi][:, 0:bound, :],
                                    in_=v_d[0:bound * 128, kvh * 128:(kvh + 1) * 128]
                                    .rearrange("(j p) c -> p j c", p=128)),
                        waits=[vw_free[bi]], tag=f"pa_vw{chunk}_{kvh}")
                    if chunk == 0 and kvh == 0 and "kvdbg" in debug:
                        DBGK = A.view(65536, BF16, (4096,))
                        DBGV = A.view(65536 + 8192, BF16, (32, 128))
                        dc1 = P.op("s", lambda s: s.activation(out=DBGK[:], in_=KW[0][:],
                            func=AF.Copy), waits=[kw_tok[0]], tag="dbg_k")
                        dc2 = P.op("s", lambda s: s.activation(out=DBGV[:], in_=VW[0][:],
                            func=AF.Copy), waits=[vw_tok[0]], tag="dbg_v")
                        P.dma("y", "padbg", lambda y: y.dma_start(
                            out=kvdbg_d[0].rearrange("p (a c) -> p a c", a=32),
                            in_=DBGK[:].rearrange("p (a c) -> p a c", a=32)),
                            waits=[dc1], tag="dbg_kst")
                        P.dma("y", "padbg", lambda y: y.dma_start(
                            out=kvdbg_d[1].rearrange("p (a c) -> p a c", a=32),
                            in_=DBGV[:].rearrange("p a c -> p a c")),
                            waits=[dc2], tag="dbg_vst")
                    for qh in range(4):
                        head = kvh * 4 + qh
                        for st_ in range(bound):
                            sb = st_ % 2
                            sc = P.op("t", lambda t, bi=bi, st_=st_, sb=sb, head=head,
                                      chunk=chunk: t.matmul(
                                bank(sb), KW[bi][:, st_ * 128:(st_ + 1) * 128],
                                QRES[:, head, chunk * 512:(chunk + 1) * 512],
                                start=True, stop=True),
                                waits=[kw_tok[bi], bankA.get(sb)], tag=f"pa_sc{chunk}_{head}_{st_}")
                            ex = P.op("s", lambda s, sb=sb: s.activation(
                                out=ESB[sb][:], in_=bank(sb), func=AF.Exp),
                                waits=[sc, esb_free[sb]], tag=f"pa_ex{chunk}_{head}_{st_}")
                            bankA[sb] = ex
                            dep = ex
                            if st_ >= mstep0:
                                dep = P.op("v", lambda v, sb=sb, st_=st_, mstep0=mstep0:
                                    v.tensor_tensor(out=ESB[sb][:], in0=ESB[sb][:],
                                                    in1=MSK[:, st_ - mstep0, :], op=ALU.mult),
                                    waits=[ex, msk_tok], tag=f"pa_mk{chunk}_{head}_{st_}")
                            av = P.op("t", lambda t, bi=bi, st_=st_, sb=sb, bound=bound:
                                t.matmul(bank(2), VW[bi][:, st_, :], ESB[sb][:],
                                         start=(st_ == 0), stop=(st_ == bound - 1),
                                         skip_group_check=True),
                                waits=[dep, vw_tok[bi],
                                       bankA.get(2) if st_ == 0 else None],
                                tag=f"pa_av{chunk}_{head}_{st_}")
                            dn = P.op("t", lambda t, st_=st_, sb=sb, bound=bound:
                                t.matmul(bank(3, rows=1), ones128, ESB[sb][:],
                                         start=(st_ == 0), stop=(st_ == bound - 1),
                                         skip_group_check=True),
                                waits=[dep, bankA.get(3) if st_ == 0 else None],
                                tag=f"pa_dn{chunk}_{head}_{st_}")
                            esb_free[sb] = dn
                        kw_free[bi] = P.tok("t")
                        vw_free[bi] = P.tok("t")
                        denr_tok = P.op("s", lambda s: s.activation(
                            out=DENR, in_=bank(3, rows=1), func=AF.Copy),
                            waits=[dn, denb_tok], tag=f"pa_de{chunk}_{head}")
                        bankA[3] = denr_tok
                        denb_tok = P.op("v", lambda v: v.tensor_copy(DENB, DENR),
                            waits=[denr_tok, bankA.get("bc")], tag=f"pa_db{chunk}_{head}")
                        bc = P.op("t", lambda t: t.matmul(bank(4), onecol, DENB,
                            start=True, stop=True),
                            waits=[denb_tok, bankA.get(4)], tag=f"pa_bc{chunk}_{head}")
                        bankA["bc"] = bc
                        rec_tok = P.op("v", lambda v: v.reciprocal(REC[:], bank(4)),
                            waits=[bc, avn_tok], tag=f"pa_rc{chunk}_{head}")
                        bankA[4] = rec_tok
                        avn_tok = P.op("v", lambda v: v.tensor_tensor(
                            out=AVN[:], in0=bank(2), in1=REC[:], op=ALU.mult),
                            waits=[rec_tok, av, ast_tok], tag=f"pa_an{chunk}_{head}")
                        bankA[2] = avn_tok
                        ast_tok = P.dma("y", "pas", lambda y, head=head, chunk=chunk:
                            y.dma_start(out=attn_d[head * 128:(head + 1) * 128,
                                                   chunk * 512:(chunk + 1) * 512],
                                        in_=AVN[:]),
                            waits=[avn_tok], tag=f"pa_as{chunk}_{head}")

        if "PO" in phases:
            P.start_phase()
            seg, cur = mkseg(67584)
            ATT = A.view(2048, BF16, (32, 1024))
            X2B = A.view(QRES_OFF, BF16, (32, 1024))
            XN2 = A.view(2048, BF16, (32, 1024))
            WOF = A.view(seg(8192), F32, (16, 128))
            WO = [A.view(seg(8192), BF16, (32, 128)) for _ in range(2)]
            PRJ = [A.view(seg(1024), BF16, (512,)) for _ in range(2)]
            PSQ = A.view(seg(1024), BF16, (512,))
            PRL = [A.view(seg(1024), BF16, (512,)) for _ in range(2)]
            XLF = [A.view(seg(2048), F32, (512,)) for _ in range(2)]
            TMPO = A.view(seg(2048), F32, (512,))
            X2T = [A.view(seg(2048), F32, (512,)) for _ in range(2)]
            RSTB2 = [A.view(seg(2048), F32, (512,)) for _ in range(2)]
            MRB2 = [A.view(seg(2048), F32, (512,)) for _ in range(2)]
            RSTB3 = [A.view(seg(2048), F32, (512,)) for _ in range(2)]
            MRB3 = [A.view(seg(2048), F32, (512,)) for _ in range(2)]
            OROWM = A.view(seg(2048), F32, (512,))[0:1, :]
            OROWS2 = A.view(seg(2048), F32, (512,))[0:1, :]
            OROWT = A.view(seg(2048), F32, (512,))[0:1, :]
            OROWRS = A.view(seg(2048), F32, (512,))[0:1, :]
            OROWMR = A.view(seg(2048), F32, (512,))[0:1, :]
            OROWRSB = A.view(seg(1024), BF16, (512,))[0:1, :]
            OROWMRB = A.view(seg(1024), BF16, (512,))[0:1, :]
            assert cur[0] <= QRES_OFF, ("PO segs overlap X2B", cur[0], QRES_OFF)

            att_tok = P.dma("y", "poa", lambda y: y.dma_start(
                out=ATT[:], in_=attn_d.rearrange("(i p) t -> p i t", p=128)),
                tag="po_att")
            wo_tok = [None, None]
            wocast = [None, None]
            ob_tok = {}
            wo_free = {}
            prj_free = [None, None]
            for dt in range(32):
                bi = dt % 2
                for hf in range(2):
                    wof_tok = P.dma("g", "powf", lambda y, dt=dt, hf=hf: y.dma_start(
                        out=WOF[:],
                        in_=woT[hf * 2048:(hf + 1) * 2048, dt * 128:(dt + 1) * 128]
                        .rearrange("(i p) c -> p i c", p=128)),
                        waits=[wocast[1 - bi] if hf == 0 else wocast[bi]],
                        tag=f"po_wf{dt}_{hf}")
                    wocast[bi] = P.op("s", lambda s, bi=bi, hf=hf: s.activation(
                        out=WO[bi][:, hf * 16:(hf + 1) * 16, :], in_=WOF[:], func=AF.Copy),
                        waits=[wof_tok, wo_free.get(bi) if hf == 0 else None],
                        tag=f"po_wc{dt}_{hf}")
                wo_tok[bi] = wocast[bi]
                for th in range(2):
                    bq = (dt * 2 + th) % 4
                    ps_ = bq % 2
                    mm = None
                    for ot in range(32):
                        mm = P.op("t", lambda t, bi=bi, ot=ot, th=th, bq=bq: t.matmul(
                            bank(bq), WO[bi][:, ot, :], ATT[:, ot, th * 512:(th + 1) * 512],
                            start=(ot == 0), stop=(ot == 31)),
                            waits=([wo_tok[bi], ob_tok.get(bq)] +
                                   ([att_tok] if dt == 0 and ot == 0 else [])),
                            tag=f"po_mm{dt}_{th}_{ot}")
                    wo_free[bi] = mm
                    ev = P.op("s", lambda s, bq=bq, ps_=ps_: s.activation(
                        out=PRJ[ps_][:], in_=bank(bq), func=AF.Copy),
                        waits=[mm, prj_free[ps_], ob_tok.get(("prjt", ps_))],
                        tag=f"po_ev{dt}_{th}")
                    ob_tok[bq] = ev
                    pst = P.dma("y", f"pop{ps_}", lambda y, dt=dt, th=th, ps_=ps_:
                        y.dma_start(out=proj_d[dt * 128:(dt + 1) * 128,
                                               th * 512:(th + 1) * 512],
                                    in_=PRJ[ps_][:]),
                        waits=[ev], tag=f"po_ps{dt}_{th}")
                    sq = P.op("s", lambda s, ps_=ps_: s.activation(
                        out=PSQ[:], in_=PRJ[ps_][:], func=AF.Square),
                        waits=[ev, ob_tok.get("psq")], tag=f"po_sq{dt}_{th}")
                    m1 = P.op("t", lambda t, th=th, dt=dt, ps_=ps_: t.matmul(
                        bank(4 + th, rows=1), ones128, PRJ[ps_][:],
                        start=(dt == 0), stop=(dt == 31)),
                        waits=[ev, ob_tok.get(4 + th) if dt == 0 else None],
                        tag=f"po_s1{dt}_{th}")
                    m2 = P.op("t", lambda t, th=th, dt=dt: t.matmul(
                        bank(6 + th, rows=1), ones128, PSQ[:],
                        start=(dt == 0), stop=(dt == 31)),
                        waits=[sq, ob_tok.get(6 + th) if dt == 0 else None],
                        tag=f"po_s2{dt}_{th}")
                    ob_tok["psq"] = m2
                    prj_free[ps_] = pst
                    ob_tok[("prjt", ps_)] = P.tok("t")
                    ob_tok[("stat", th)] = (m1, m2)
            oproj_done = P.tok("t")

            def ln_rows(pfx, sum_bank, sq_bank, RSB, MRB_, waits_extra, nw=512):
                e1 = P.op("s", lambda s: s.activation(out=OROWM[:, 0:nw],
                    in_=bank(sum_bank, nw, rows=1), func=AF.Identity, scale=invS_sb),
                    waits=waits_extra, tag=f"{pfx}_m")
                e2 = P.op("s", lambda s: s.activation(out=OROWS2[:, 0:nw],
                    in_=bank(sq_bank, nw, rows=1), func=AF.Identity, scale=invS_sb),
                    waits=waits_extra, tag=f"{pfx}_s2")
                v1 = P.op("v", lambda v: v.tensor_tensor(out=OROWT[:, 0:nw],
                    in0=OROWM[:, 0:nw], in1=OROWM[:, 0:nw], op=ALU.mult),
                    waits=[e1], tag=f"{pfx}_mm")
                v2 = P.op("v", lambda v: v.tensor_tensor(out=OROWT[:, 0:nw],
                    in0=OROWS2[:, 0:nw], in1=OROWT[:, 0:nw], op=ALU.subtract),
                    waits=[v1, e2], tag=f"{pfx}_var")
                e3 = P.op("s", lambda s: s.activation(out=OROWRS[:, 0:nw],
                    in_=OROWT[:, 0:nw], func=AF.Sqrt, bias=eps_sb),
                    waits=[v2], tag=f"{pfx}_sqrt")
                v3 = P.op("v", lambda v: v.reciprocal(OROWRS[:, 0:nw], OROWRS[:, 0:nw]),
                    waits=[e3], tag=f"{pfx}_rcp")
                v4 = P.op("v", lambda v: v.tensor_tensor(out=OROWMR[:, 0:nw],
                    in0=OROWM[:, 0:nw], in1=OROWRS[:, 0:nw], op=ALU.mult),
                    waits=[v3], tag=f"{pfx}_mr")
                v5 = P.op("v", lambda v: v.tensor_copy(OROWRSB[:, 0:nw], OROWRS[:, 0:nw]),
                    waits=[v3], tag=f"{pfx}_cb1")
                v6 = P.op("v", lambda v: v.tensor_copy(OROWMRB[:, 0:nw], OROWMR[:, 0:nw]),
                    waits=[v4], tag=f"{pfx}_cb2")
                b1 = P.op("t", lambda t: t.matmul(bank(sum_bank, nw), onecol,
                    OROWRSB[:, 0:nw], start=True, stop=True),
                    waits=[v5, e1], tag=f"{pfx}_b1")
                b2 = P.op("t", lambda t: t.matmul(bank(sq_bank, nw), onecol,
                    OROWMRB[:, 0:nw], start=True, stop=True),
                    waits=[v6, e2], tag=f"{pfx}_b2")
                eb1 = P.op("s", lambda s: s.activation(out=RSB[:, 0:nw],
                    in_=bank(sum_bank, nw), func=AF.Copy), waits=[b1], tag=f"{pfx}_eb1")
                eb2 = P.op("s", lambda s: s.activation(out=MRB_[:, 0:nw],
                    in_=bank(sq_bank, nw), func=AF.Copy), waits=[b2], tag=f"{pfx}_eb2")
                return eb1, eb2

            ln2_toks = {}
            for th in range(2):
                m1, m2 = ob_tok[("stat", th)]
                ln2_toks[th] = ln_rows(f"po_l2{th}", 4 + th, 6 + th,
                                       RSTB2[th], MRB2[th], [m1, m2])

            x2_toks = {}
            prl_tok = [None, None]
            xl_tok = [None, None]
            x2st_tok = [None, None]
            tmpo_tok = None
            x2b_tok = {}
            for dt in range(32):
                for th in range(2):
                    sl = (dt * 2 + th) % 2
                    prl_tok[sl] = P.dma("y", f"por{sl}", lambda y, dt=dt, th=th, sl=sl:
                        y.dma_start(out=PRL[sl][:],
                                    in_=proj_d[dt * 128:(dt + 1) * 128,
                                               th * 512:(th + 1) * 512]),
                        waits=[x2b_tok.get(("prl", sl)),
                               P.tok("pop0"), P.tok("pop1")],
                        tag=f"po_rl{dt}_{th}")
                    xl_tok[sl] = P.dma("y", f"pox{sl}", lambda y, sl=sl, dt=dt, th=th:
                        y.dma_start(out=XLF[sl][:],
                                    in_=xtl[dt * 128:(dt + 1) * 128,
                                            th * 512:(th + 1) * 512]),
                        waits=[x2b_tok.get(("xlf", sl))],
                        tag=f"po_xl{dt}_{th}")
                    t1 = P.op("v", lambda v, sl=sl, th=th: v.tensor_tensor(
                        out=TMPO[:], in0=PRL[sl][:], in1=RSTB2[th][:], op=ALU.mult),
                        waits=[prl_tok[sl], ln2_toks[th][0], tmpo_tok], tag=f"po_t1{dt}_{th}")
                    t2 = P.op("v", lambda v, th=th: v.tensor_tensor(
                        out=TMPO[:], in0=TMPO[:], in1=MRB2[th][:], op=ALU.subtract),
                        waits=[t1, ln2_toks[th][1]], tag=f"po_t2{dt}_{th}")
                    x2v = P.op("v", lambda v, sl=sl: v.tensor_tensor(
                        out=X2T[sl][:], in0=TMPO[:], in1=XLF[sl][:], op=ALU.add),
                        waits=[t2, xl_tok[sl], x2st_tok[sl],
                               x2b_tok.get(("xlf", sl))], tag=f"po_x2{dt}_{th}")
                    tmpo_tok = x2v
                    x2b_tok[("prl", sl)] = x2v
                    x2st_tok[sl] = P.dma("y", f"po2{sl}", lambda y, dt=dt, th=th, sl=sl:
                        y.dma_start(out=x2_d[dt * 128:(dt + 1) * 128,
                                             th * 512:(th + 1) * 512],
                                    in_=X2T[sl][:]),
                        waits=[x2v], tag=f"po_2s{dt}_{th}")
                    cb = P.op("s", lambda s, dt=dt, th=th, sl=sl: s.activation(
                        out=X2B[:, dt, th * 512:(th + 1) * 512], in_=X2T[sl][:],
                        func=AF.Copy), waits=[x2v], tag=f"po_cb{dt}_{th}")
                    x2b_tok[("xlf", sl)] = cb
                    sq = P.op("s", lambda s, dt=dt, th=th: s.activation(
                        out=PSQ[:], in_=X2B[:, dt, th * 512:(th + 1) * 512],
                        func=AF.Square), waits=[cb, x2b_tok.get("psq3")],
                        tag=f"po_q3{dt}_{th}")
                    m1 = P.op("t", lambda t, th=th, dt=dt: t.matmul(
                        bank(4 + th, rows=1), ones128,
                        X2B[:, dt, th * 512:(th + 1) * 512],
                        start=(dt == 0), stop=(dt == 31)),
                        waits=[cb, ln2_toks[th][0] if dt == 0 else None],
                        tag=f"po_31{dt}_{th}")
                    m2 = P.op("t", lambda t, th=th, dt=dt: t.matmul(
                        bank(6 + th, rows=1), ones128, PSQ[:],
                        start=(dt == 0), stop=(dt == 31)),
                        waits=[sq, ln2_toks[th][1] if dt == 0 else None],
                        tag=f"po_32{dt}_{th}")
                    x2b_tok["psq3"] = m2
                    x2b_tok[("stat", th)] = (m1, m2)
            ln3_toks = {}
            for th in range(2):
                m1, m2 = x2b_tok[("stat", th)]
                ln3_toks[th] = ln_rows(f"po_l3{th}", 4 + th, 6 + th,
                                       RSTB3[th], MRB3[th], [m1, m2])
            xn2_tok = None
            for dt in range(32):
                for th in range(2):
                    t1 = P.op("v", lambda v, dt=dt, th=th: v.tensor_tensor(
                        out=TMPO[:], in0=X2B[:, dt, th * 512:(th + 1) * 512],
                        in1=RSTB3[th][:], op=ALU.mult),
                        waits=[ln3_toks[th][0], tmpo_tok,
                               oproj_done if dt == 0 and th == 0 else None],
                        tag=f"po_n1{dt}_{th}")
                    xn2_tok = P.op("v", lambda v, dt=dt, th=th: v.tensor_tensor(
                        out=XN2[:, dt, th * 512:(th + 1) * 512], in0=TMPO[:],
                        in1=MRB3[th][:], op=ALU.subtract),
                        waits=[t1, ln3_toks[th][1]], tag=f"po_n2{dt}_{th}")
                    tmpo_tok = xn2_tok
            if "xn2dbg" in debug:
                P.dma("y", "pod1", lambda y: y.dma_start(
                    out=xn2dbg_d.rearrange("(i p) t -> p i t", p=128), in_=XN2[:]),
                    waits=[P.tok("v")], tag="po_xn2dump")
                P.dma("y", "pod2", lambda y: y.dma_start(
                    out=r3dbg_d[0].rearrange("(z p) c -> p z c", p=128) if False else
                        r3dbg_d[0], in_=RSTB3[0][:]), waits=[P.tok("v")], tag="po_r3a")
                P.dma("y", "pod2", lambda y: y.dma_start(
                    out=r3dbg_d[1], in_=RSTB3[1][:]), waits=[P.tok("v")], tag="po_r3b")
                P.dma("y", "pod2", lambda y: y.dma_start(
                    out=r3dbg_d[2], in_=MRB3[0][:]), waits=[P.tok("v")], tag="po_r3c")
                P.dma("y", "pod2", lambda y: y.dma_start(
                    out=r3dbg_d[3], in_=MRB3[1][:]), waits=[P.tok("v")], tag="po_r3d")

        if "PF1" in phases:
            P.start_phase()
            seg, cur = mkseg(67584)
            XN2 = A.view(2048, BF16, (32, 1024))
            if "PO" not in phases or "force_xn2t" in debug:
                P.dma("y", "f1x", lambda y: y.dma_start(
                    out=XN2[:], in_=xn2t.rearrange("(i p) t -> p i t", p=128)),
                    tag="f1_xn2t")
                P.start_phase()
            WCAR = [A.view(seg(16384), BF16, (32, 256)) for _ in range(3)]
            OH32 = A.view(seg(2048), BF16, (32, 32))
            ZPG = A.view(seg(22016), BF16, (11008,))
            ZPU = A.view(seg(22016), BF16, (11008,))
            CSX = A.view(seg(2048), BF16, (2, 512))
            GS = [A.view(seg(1024), BF16, (512,)) for _ in range(2)]
            INTO = [A.view(seg(2048), BF16, (2, 512)) for _ in range(2)]

            oh_tok = P.dma("y", "f1o", lambda y: y.dma_start(
                out=OH32[:], in_=oh32.rearrange("m p c -> p m c")), tag="f1_oh")
            zpg_tok = P.dma("y", "f1z", lambda y: y.dma_start(
                out=ZPG[0:32, :], in_=zpg_d), tag="f1_zg")
            zpu_tok = P.dma("y", "f1z", lambda y: y.dma_start(
                out=ZPU[0:32, :], in_=zpu_d), tag="f1_zu")
            fb_tok = {}
            cs_ev = {}
            for th in range(2):
                mm = None
                for i in range(32):
                    mm = P.op("t", lambda t, i=i, th=th: t.matmul(
                        bank(4 + th, rows=32), OH32[:, i, :],
                        XN2[:, i, th * 512:(th + 1) * 512],
                        start=(i == 0), stop=(i == 31)),
                        waits=[oh_tok] if i == 0 else [], tag=f"f1_cs{th}_{i}")
                cs_ev[th] = P.op("s", lambda s, th=th: s.activation(
                    out=CSX[0:32, th, :], in_=bank(4 + th, rows=32), func=AF.Copy),
                    waits=[mm], tag=f"f1_cse{th}")
            spans = [(i0, 2) for i0 in range(0, NT_I, 2)]
            slot_free = [None, None, None]
            slot_tok = {}
            gs_free = [None, None]
            into_free = [None, None]
            bank_f = {}
            sidx = 0
            for (i0, ni) in spans:
                gslot = sidx % 3
                uslot = (sidx + 1) % 3
                sidx += 2
                gtok = P.dma("g", f"f1w{gslot}", lambda y, i0=i0, ni=ni, gslot=gslot:
                    y.dma_start(out=WCAR[gslot][:, :, 0:ni * 128],
                                in_=wtg_d[:, :, i0 * 128:(i0 + ni) * 128]
                                .rearrange("a p c -> p a c")),
                    waits=[slot_free[gslot]], tag=f"f1_wg{i0}")
                utok = P.dma("g", f"f1w{uslot}", lambda y, i0=i0, ni=ni, uslot=uslot:
                    y.dma_start(out=WCAR[uslot][:, :, 0:ni * 128],
                                in_=wtu_d[:, :, i0 * 128:(i0 + ni) * 128]
                                .rearrange("a p c -> p a c")),
                    waits=[slot_free[uslot]], tag=f"f1_wu{i0}")
                for ii in range(ni):
                    par = ii % 2
                    gb = par * 4
                    ub = par * 4 + 2
                    for th in range(2):
                        mm = None
                        for kt in range(32):
                            mm = P.op("t", lambda t, kt=kt, ii=ii, th=th, gb=gb, gslot=gslot:
                                t.matmul(bank(gb + th), WCAR[gslot][:, kt, ii * 128:(ii + 1) * 128],
                                         XN2[:, kt, th * 512:(th + 1) * 512],
                                         start=(kt == 0), stop=False),
                                waits=([gtok, bank_f.get(gb + th)] if kt == 0 else []),
                                tag=f"f1_gm{i0}_{ii}_{th}_{kt}")
                        gmm = P.op("t", lambda t, ii=ii, th=th, gb=gb, i0=i0: t.matmul(
                            bank(gb + th), ZPG[0:32, (i0 + ii) * 128:(i0 + ii + 1) * 128],
                            CSX[0:32, th, :], start=False, stop=True),
                            waits=[zpg_tok, cs_ev[th]], tag=f"f1_gz{i0}_{ii}_{th}")
                        mm = None
                        for kt in range(32):
                            mm = P.op("t", lambda t, kt=kt, ii=ii, th=th, ub=ub, uslot=uslot:
                                t.matmul(bank(ub + th), WCAR[uslot][:, kt, ii * 128:(ii + 1) * 128],
                                         XN2[:, kt, th * 512:(th + 1) * 512],
                                         start=(kt == 0), stop=False),
                                waits=([utok, bank_f.get(ub + th)] if kt == 0 else []),
                                tag=f"f1_um{i0}_{ii}_{th}_{kt}")
                        umm = P.op("t", lambda t, ii=ii, th=th, ub=ub, i0=i0: t.matmul(
                            bank(ub + th), ZPU[0:32, (i0 + ii) * 128:(i0 + ii + 1) * 128],
                            CSX[0:32, th, :], start=False, stop=True),
                            waits=[zpu_tok], tag=f"f1_uz{i0}_{ii}_{th}")
                        sil = P.op("s", lambda s, th=th, gb=gb, par=par: s.activation(
                            out=GS[par][:], in_=bank(gb + th), func=AF.Silu),
                            waits=[gmm, gs_free[par]], tag=f"f1_si{i0}_{ii}_{th}")
                        bank_f[gb + th] = sil
                        mul = P.op("v", lambda v, th=th, ub=ub, par=par: v.tensor_tensor(
                            out=INTO[par][:, th, :], in0=bank(ub + th), in1=GS[par][:],
                            op=ALU.mult),
                            waits=[umm, sil, into_free[par]], tag=f"f1_ml{i0}_{ii}_{th}")
                        bank_f[ub + th] = mul
                        gs_free[par] = mul
                        if th == 1:
                            into_free[par] = P.dma("y", f"f1i{par}",
                                lambda y, i0=i0, ii=ii, par=par: y.dma_start(
                                    out=inter_d[(i0 + ii) * 128:(i0 + ii + 1) * 128, :]
                                        .rearrange("p (a b) -> p a b", a=2),
                                    in_=INTO[par][:]),
                                waits=[mul], tag=f"f1_is{i0}_{ii}")
                    slot_tok[gslot] = P.tok("t")
                    slot_tok[uslot] = P.tok("t")
                slot_free[gslot] = slot_tok[gslot]
                slot_free[uslot] = slot_tok[uslot]
            pf1_done = P.tok("t")

        if "PF2" in phases:
            P.start_phase()
            seg, cur = mkseg(67584)
            D16 = A.view(2048, F16, (32, 1024))
            OH86 = A.view(seg(14792), BF16, (86, 86))
            ZPD = A.view(seg(8192), BF16, (4096,))
            CSI = A.view(seg(2048), BF16, (2, 512))
            INT2 = [A.view(seg(2048), BF16, (2, 512)) for _ in range(2)]
            WDT = [A.view(seg(1024), BF16, (512,)) for _ in range(2)]
            NRES = 43
            INTR = A.view(seg(NRES * 2048), BF16, (NRES, 2, 512))

            oh2_tok = P.dma("y", "f2o", lambda y: y.dma_start(
                out=OH86[:], in_=oh86.rearrange("m p c -> p m c")), tag="f2_oh")
            zpd_tok = P.dma("y", "f2z", lambda y: y.dma_start(
                out=ZPD[0:86, :], in_=zpd_d), tag="f2_zd")
            int_tok = [None, None]
            int_free = [None, None]
            res_gate = [None, None]
            intr_tok = {}
            csi_ev = {}
            mm_ = {0: None, 1: None}
            for i in range(NT_I):
                sl = i % 2
                if i < NRES:
                    intr_tok[i] = P.dma("g", f"f2i{sl}", lambda y, i=i: y.dma_start(
                        out=INTR[:, i, :, :], in_=inter_d[i * 128:(i + 1) * 128, :]
                        .rearrange("p (a b) -> p a b", a=2)),
                        waits=[res_gate[sl]], tag=f"f2_il{i}")
                    rhs_ = lambda th, i=i: INTR[:, i, th, :]
                    ld_tok = intr_tok[i]
                else:
                    int_tok[sl] = P.dma("g", f"f2i{sl}", lambda y, i=i, sl=sl: y.dma_start(
                        out=INT2[sl][:], in_=inter_d[i * 128:(i + 1) * 128, :]
                        .rearrange("p (a b) -> p a b", a=2)),
                        waits=[int_free[sl]], tag=f"f2_il{i}")
                    rhs_ = lambda th, sl=sl: INT2[sl][:, th, :]
                    ld_tok = int_tok[sl]
                for th in range(2):
                    mm_[th] = P.op("t", lambda t, i=i, th=th, rhs_=rhs_: t.matmul(
                        bank(4 + th, rows=86), OH86[:, i, :], rhs_(th),
                        start=(i == 0), stop=(i == NT_I - 1)),
                        waits=[ld_tok] + ([oh2_tok] if i == 0 else []),
                        tag=f"f2_cs{i}_{th}")
                res_gate[sl] = mm_[1]
                if i >= NRES:
                    int_free[sl] = mm_[1]
            for th in range(2):
                csi_ev[th] = P.op("s", lambda s, th=th: s.activation(
                    out=CSI[0:86, th, :], in_=bank(4 + th, rows=86), func=AF.Copy),
                    waits=[mm_[th]], tag=f"f2_cse{th}")
            wd_tok = [None, None]
            wd_free = [None, None]
            bank_g = {}
            for hg in range(8):
                mm_last = {}
                for i in range(NT_I):
                    sl = i % 2
                    if i >= NRES:
                        int_tok[sl] = P.dma("g", f"f2j{sl}", lambda y, i=i, sl=sl: y.dma_start(
                            out=INT2[sl][:], in_=inter_d[i * 128:(i + 1) * 128, :]
                            .rearrange("p (a b) -> p a b", a=2)),
                            waits=[int_free[sl]], tag=f"f2_jl{hg}_{i}")
                    wsl = i % 2
                    wd_tok[wsl] = P.dma("y", f"f2w{wsl}", lambda y, hg=hg, i=i, wsl=wsl:
                        y.dma_start(out=WDT[wsl][:],
                                    in_=wtd_d[i, :, hg * 512:(hg + 1) * 512]),
                        waits=[wd_free[wsl]], tag=f"f2_wl{hg}_{i}")
                    if i < NRES:
                        rhs2 = lambda th, i=i: INTR[:, i, th, :]
                        ld2 = intr_tok[i]
                    else:
                        rhs2 = lambda th, sl=sl: INT2[sl][:, th, :]
                        ld2 = int_tok[sl]
                    for ht in range(4):
                        for th in range(2):
                            bq = ht * 2 + th
                            mm_last[bq] = P.op("t", lambda t, i=i, ht=ht, th=th, bq=bq,
                                               rhs2=rhs2, wsl=wsl: t.matmul(
                                bank(bq), WDT[wsl][:, ht * 128:(ht + 1) * 128],
                                rhs2(th), start=(i == 0), stop=False),
                                waits=[ld2, wd_tok[wsl],
                                       bank_g.get(bq) if i == 0 else None],
                                tag=f"f2_dm{hg}_{i}_{bq}")
                    if i >= NRES:
                        int_free[sl] = mm_last[7]
                    wd_free[wsl] = mm_last[7]
                for ht in range(4):
                    for th in range(2):
                        bq = ht * 2 + th
                        zm = P.op("t", lambda t, hg=hg, ht=ht, th=th, bq=bq: t.matmul(
                            bank(bq), ZPD[0:86, (hg * 4 + ht) * 128:(hg * 4 + ht + 1) * 128],
                            CSI[0:86, th, :], start=False, stop=True),
                            waits=[zpd_tok, csi_ev[th], mm_last[bq]],
                            tag=f"f2_dz{hg}_{bq}")
                        bank_g[bq] = P.op("s", lambda s, hg=hg, ht=ht, th=th, bq=bq:
                            s.activation(out=D16[:, hg * 4 + ht, th * 512:(th + 1) * 512],
                                         in_=bank(bq), func=AF.Copy),
                            waits=[zm, pf1_done if hg == 0 and ht == 0 and th == 0 else None],
                            tag=f"f2_de{hg}_{bq}")

        if "PL" in phases:
            P.start_phase()
            seg, cur = mkseg(132 * 1024)
            D16 = A.view(2048, F16, (32, 1024))
            DBF = A.view(seg(1024), BF16, (512,))
            DSQ = A.view(seg(1024), BF16, (512,))
            X2L = [A.view(seg(2048), F32, (512,)) for _ in range(2)]
            OUTT = [A.view(seg(2048), F32, (512,)) for _ in range(2)]
            TMPL = A.view(seg(2048), F32, (512,))
            RSTB4 = [A.view(seg(2048), F32, (512,)) for _ in range(2)]
            MRB4 = [A.view(seg(2048), F32, (512,)) for _ in range(2)]
            LROWM = A.view(seg(2048), F32, (512,))[0:1, :]
            LROWS2 = A.view(seg(2048), F32, (512,))[0:1, :]
            LROWT = A.view(seg(2048), F32, (512,))[0:1, :]
            LROWRS = A.view(seg(2048), F32, (512,))[0:1, :]
            LROWMR = A.view(seg(2048), F32, (512,))[0:1, :]
            LROWRSB = A.view(seg(1024), BF16, (512,))[0:1, :]
            LROWMRB = A.view(seg(1024), BF16, (512,))[0:1, :]

            lb_tok = {}
            for dt in range(32):
                for th in range(2):
                    cb = P.op("s", lambda s, dt=dt, th=th: s.activation(
                        out=DBF[:], in_=D16[:, dt, th * 512:(th + 1) * 512], func=AF.Copy),
                        waits=[lb_tok.get("dbf")], tag=f"pl_cb{dt}_{th}")
                    sq = P.op("s", lambda s: s.activation(
                        out=DSQ[:], in_=DBF[:], func=AF.Square),
                        waits=[cb, lb_tok.get("dsq")], tag=f"pl_sq{dt}_{th}")
                    m1 = P.op("t", lambda t, th=th, dt=dt: t.matmul(
                        bank(0 + th, rows=1), ones128, DBF[:],
                        start=(dt == 0), stop=(dt == 31)),
                        waits=[cb, lb_tok.get(("b", th)) if dt == 0 else None],
                        tag=f"pl_s1{dt}_{th}")
                    lb_tok["dbf"] = m1
                    m2 = P.op("t", lambda t, th=th, dt=dt: t.matmul(
                        bank(2 + th, rows=1), ones128, DSQ[:],
                        start=(dt == 0), stop=(dt == 31)),
                        waits=[sq, lb_tok.get(("c", th)) if dt == 0 else None],
                        tag=f"pl_s2{dt}_{th}")
                    lb_tok["dsq"] = m2
                    lb_tok[("stat", th)] = (m1, m2)

            def ln_rows4(pfx, sum_bank, sq_bank, RSB, MRB_, waits_extra):
                e1 = P.op("s", lambda s: s.activation(out=LROWM,
                    in_=bank(sum_bank, 512, rows=1), func=AF.Identity, scale=invS_sb),
                    waits=waits_extra, tag=f"{pfx}_m")
                e2 = P.op("s", lambda s: s.activation(out=LROWS2,
                    in_=bank(sq_bank, 512, rows=1), func=AF.Identity, scale=invS_sb),
                    waits=waits_extra, tag=f"{pfx}_s2")
                v1 = P.op("v", lambda v: v.tensor_tensor(out=LROWT, in0=LROWM, in1=LROWM,
                    op=ALU.mult), waits=[e1], tag=f"{pfx}_mm")
                v2 = P.op("v", lambda v: v.tensor_tensor(out=LROWT, in0=LROWS2, in1=LROWT,
                    op=ALU.subtract), waits=[v1, e2], tag=f"{pfx}_var")
                e3 = P.op("s", lambda s: s.activation(out=LROWRS, in_=LROWT, func=AF.Sqrt,
                    bias=eps_sb), waits=[v2], tag=f"{pfx}_sqrt")
                v3 = P.op("v", lambda v: v.reciprocal(LROWRS, LROWRS), waits=[e3],
                    tag=f"{pfx}_rcp")
                v4 = P.op("v", lambda v: v.tensor_tensor(out=LROWMR, in0=LROWM, in1=LROWRS,
                    op=ALU.mult), waits=[v3], tag=f"{pfx}_mr")
                v5 = P.op("v", lambda v: v.tensor_copy(LROWRSB, LROWRS), waits=[v3],
                    tag=f"{pfx}_cb1")
                v6 = P.op("v", lambda v: v.tensor_copy(LROWMRB, LROWMR), waits=[v4],
                    tag=f"{pfx}_cb2")
                b1 = P.op("t", lambda t: t.matmul(bank(sum_bank), onecol, LROWRSB,
                    start=True, stop=True), waits=[v5, e1], tag=f"{pfx}_b1")
                b2 = P.op("t", lambda t: t.matmul(bank(sq_bank), onecol, LROWMRB,
                    start=True, stop=True), waits=[v6, e2], tag=f"{pfx}_b2")
                eb1 = P.op("s", lambda s: s.activation(out=RSB[:], in_=bank(sum_bank),
                    func=AF.Copy), waits=[b1], tag=f"{pfx}_eb1")
                eb2 = P.op("s", lambda s: s.activation(out=MRB_[:], in_=bank(sq_bank),
                    func=AF.Copy), waits=[b2], tag=f"{pfx}_eb2")
                return eb1, eb2

            ln4_toks = {}
            for th in range(2):
                m1, m2 = lb_tok[("stat", th)]
                ln4_toks[th] = ln_rows4(f"pl_l4{th}", 0 + th, 2 + th,
                                        RSTB4[th], MRB4[th], [m1, m2])
            x2l_tok = [None, None]
            outt_free = [None, None]
            ost_tok = [None, None]
            tmpl_tok = None
            for dt in range(32):
                for th in range(2):
                    sl = (dt * 2 + th) % 2
                    x2l_tok[sl] = P.dma("y", f"plx{sl}", lambda y, dt=dt, th=th, sl=sl:
                        y.dma_start(out=X2L[sl][:],
                                    in_=x2_d[dt * 128:(dt + 1) * 128,
                                             th * 512:(th + 1) * 512]),
                        waits=[outt_free[sl]], tag=f"pl_xl{dt}_{th}")
                    t1 = P.op("v", lambda v, dt=dt, th=th: v.tensor_tensor(
                        out=TMPL[:], in0=D16[:, dt, th * 512:(th + 1) * 512],
                        in1=RSTB4[th][:], op=ALU.mult),
                        waits=[ln4_toks[th][0], tmpl_tok], tag=f"pl_t1{dt}_{th}")
                    t2 = P.op("v", lambda v, th=th: v.tensor_tensor(
                        out=TMPL[:], in0=TMPL[:], in1=MRB4[th][:], op=ALU.subtract),
                        waits=[t1, ln4_toks[th][1]], tag=f"pl_t2{dt}_{th}")
                    ov = P.op("v", lambda v, sl=sl: v.tensor_tensor(
                        out=OUTT[sl][:], in0=TMPL[:], in1=X2L[sl][:], op=ALU.add),
                        waits=[t2, x2l_tok[sl], ost_tok[sl]], tag=f"pl_o{dt}_{th}")
                    tmpl_tok = ov
                    outt_free[sl] = ov
                    ost_tok[sl] = P.dma("y", f"pls{sl}", lambda y, dt=dt, th=th, sl=sl:
                        y.dma_start(out=out_d[dt * 128:(dt + 1) * 128,
                                              th * 512:(th + 1) * 512],
                                    in_=OUTT[sl][:]),
                        waits=[ov], tag=f"pl_os{dt}_{th}")

        fin = [P.tok(s) for s in list(P.cnt.keys())]
        P.op("y", lambda y: y.nop(), waits=fin, tag="final_fence")

        emit_all(nc, P)

    return nc, P


def emit_all(nc, P):
    semh = {}
    with ExitStack() as es:
        for name in sorted(P.sem_names):
            semh[name] = es.enter_context(nc.semaphore(name))
        with nc.Block() as block:
            @block.tensor
            def _(t):
                _emit_one(P, "t", t, semh)
            @block.vector
            def _(v):
                _emit_one(P, "v", v, semh)
            @block.scalar
            def _(s):
                _emit_one(P, "s", s, semh)
            @block.gpsimd
            def _(g):
                _emit_one(P, "g", g, semh)
            @block.sync
            def _(y):
                _emit_one(P, "y", y, semh)


def _emit_one(P, name, eng, semh):
    last = {}
    for entry in P.ops[name]:
        for (sem, val) in entry["waits"]:
            if last.get(sem, -1) >= val:
                continue
            eng.wait_ge(semh[sem], val)
            last[sem] = val
        ins = entry["fn"](eng)
        if ins is None:
            raise RuntimeError(f"op emitted nothing: {entry['tag']}")
        ins.then_inc(semh[entry["inc"]], entry["amt"])



def kernel(**inputs):
    percore = stage_inputs(inputs)
    nc, _plan = build_nc()
    res = run_bass_kernel_spmd(nc, percore, core_ids=list(range(8)))
    out = np.empty((2, 4096, 4096), np.float32)
    for c in range(8):
        b, g = c // 4, c % 4
        o = np.asarray(res.results[c]["out"])
        out[b, 512 * g:512 * (g + 1)] = o[:, 0:512].T
        out[b, 512 * (7 - g):512 * (8 - g)] = o[:, 512:1024].T
    return out


# version 03
# speedup vs baseline: 1.4639x; 1.4639x over previous
import numpy as np
import ml_dtypes
import concourse.bass as bass
import concourse.mybir as mybir
from concourse.bass_utils import run_bass_kernel_spmd
from contextlib import ExitStack

F32 = mybir.dt.float32
F16 = mybir.dt.float16
BF16 = mybir.dt.bfloat16
U8 = mybir.dt.uint8
I32 = mybir.dt.int32
AF = mybir.ActivationFunctionType
ALU = mybir.AluOpType

D = 4096
S = 4096
B = 2
NH, NKV, HD = 32, 8, 128
QO, KO, VO = 4096, 1024, 1024
INTER = 11008
GS = 128
NT_I = INTER // 128
NT_D = D // 128
TLOC = 1024
ROPE_BASE = 500000.0


def rope_tables():
    inv_freq = (1.0 / ROPE_BASE ** (np.arange(0, HD, 2, dtype=np.float32) / np.float32(HD))).astype(np.float32)
    t = np.arange(S, dtype=np.float32)
    freqs = (t[:, None] * inv_freq[None, :]).astype(np.float32)
    cos = np.cos(freqs.astype(np.float64)).astype(np.float32)
    sin = np.sin(freqs.astype(np.float64)).astype(np.float32)
    cosF = np.concatenate([cos.T, cos.T], axis=0)
    sinF = np.concatenate([sin.T, sin.T], axis=0)
    return np.ascontiguousarray(cosF), np.ascontiguousarray(sinF)

def qweight_bytesT(qw):
    KD8, N = qw.shape
    b = qw.view(np.uint8).reshape(KD8, N, 4)
    return np.ascontiguousarray(b.transpose(1, 0, 2).reshape(N, KD8 * 4))

def perm_sign_P64T():
    P = np.zeros((128, 128), np.float32)
    for i in range(64):
        P[i, i + 64] = -1.0
        P[i + 64, i] = 1.0
    return np.ascontiguousarray(P.T).astype(ml_dtypes.bfloat16)

def build_masks(g):
    out = np.zeros((32, 128, 512), np.float32)
    qlo = 512 * g
    qhi = 512 * (7 - g)
    kt = np.arange(128)
    qt = np.arange(512)
    for s_ in range(16):
        out[s_] = (128 * s_ + kt[:, None]) <= (qlo + qt[None, :])
    for j, s_ in enumerate(range(16, 32)):
        out[16 + j] = (128 * s_ + kt[:, None]) <= (qhi + qt[None, :])
    return np.ascontiguousarray(out).astype(ml_dtypes.bfloat16)

def onehot_lhsT(n):
    oh = np.zeros((n, 128, n), np.float32)
    for j in range(n):
        oh[j, :, j] = 1.0
    return oh.astype(ml_dtypes.bfloat16)

def stage_inputs(inputs):
    x = np.asarray(inputs["L0_attn_x"])
    w_qkv = np.asarray(inputs["L0_attn_w_qkv"])
    w_o = np.asarray(inputs["L0_attn_w_o"])
    st = {}
    st["xt0"] = np.ascontiguousarray(x[0].T)
    st["xt1"] = np.ascontiguousarray(x[1].T)
    st["wqkvT"] = np.ascontiguousarray(w_qkv.T)
    st["woT"] = np.ascontiguousarray(w_o.T)
    for nm in ("gate", "up", "down"):
        st[f"hb_{nm}"] = qweight_bytesT(np.asarray(inputs[f"L0_ffn_{nm}_qweight"]))
        sc = np.asarray(inputs[f"L0_ffn_{nm}_scales"])
        st[f"sT_{nm}"] = np.ascontiguousarray(sc.T)
        st[f"s_{nm}"] = sc
        st[f"z_{nm}"] = np.ascontiguousarray(
            np.asarray(inputs[f"L0_ffn_{nm}_zeros"]).view(np.uint8))
    cosF, sinF = rope_tables()
    st["cosK"] = cosF
    st["sinK"] = sinF
    scale = np.float32(1.0) / np.sqrt(np.float64(HD)).astype(np.float32)
    st["cosQ"] = (cosF.astype(np.float64) * float(scale)).astype(np.float32)
    st["sinQ"] = (sinF.astype(np.float64) * float(scale)).astype(np.float32)
    st["p64t"] = perm_sign_P64T()
    st["i128"] = np.eye(128, dtype=np.float32)
    st["i128b"] = np.eye(128, dtype=np.float32).astype(ml_dtypes.bfloat16)
    st["oh32"] = onehot_lhsT(32)
    st["oh86"] = onehot_lhsT(86)
    percore = []
    for c in range(8):
        b, g = c // 4, c % 4
        d = dict(st)
        d["xt"] = st["xt0"] if b == 0 else st["xt1"]
        del d["xt0"], d["xt1"]
        d["masks"] = build_masks(g)
        loc = np.r_[512 * g:512 * (g + 1), 512 * (7 - g):512 * (8 - g)]
        d["xtl"] = np.ascontiguousarray(d["xt"][:, loc])
        d["xn2t"] = np.zeros((4096, 1024), ml_dtypes.bfloat16)
        d["offs"] = np.array([512 * g, 512 * (7 - g)], dtype=np.int32)
        percore.append(d)
    return percore


class Plan:
    ENGINES = ("t", "v", "s", "g", "y")

    def __init__(self):
        self.ops = {e: [] for e in self.ENGINES}
        self.cnt = {}
        self.sem_names = set()
        self.phase_barrier = None
        self.phase_seen = set()

    def tok(self, sem):
        return (sem, self.cnt.get(sem, 0))

    def start_phase(self):
        self.phase_barrier = [(s, v) for s, v in self.cnt.items()]
        self.phase_seen = set()

    def op(self, eng, fn, waits=(), inc=None, amt=1, tag=""):
        waits = [w for w in waits if w is not None]
        if self.phase_barrier is not None and eng not in self.phase_seen:
            waits = list(waits) + self.phase_barrier
            self.phase_seen.add(eng)
        if inc is None:
            inc = eng
        self.sem_names.add(inc)
        for w in waits:
            self.sem_names.add(w[0])
        self.cnt[inc] = self.cnt.get(inc, 0) + amt
        self.ops[eng].append({"fn": fn, "waits": waits, "inc": inc,
                              "amt": amt, "tag": tag})
        return (inc, self.cnt[inc])

    def dma(self, eng, q, fn, waits=(), tag=""):
        return self.op(eng, fn, waits=waits, inc=q, amt=16, tag=tag)

    def barrier_token_all(self):
        return [self.tok(e) for e in ("t", "v", "s")]

    def emit(self, engines, semh):
        for name, eng in engines.items():
            last = {}
            for entry in self.ops[name]:
                for (sem, val) in entry["waits"]:
                    if last.get(sem, -1) >= val:
                        continue
                    eng.wait_ge(semh[sem], val)
                    last[sem] = val
                ins = entry["fn"](eng)
                if ins is not None:
                    ins.then_inc(semh[entry["inc"]], entry["amt"])
                else:
                    raise RuntimeError(f"op returned no instruction: {entry['tag']}")


class Arena:
    def __init__(self, handle, nbytes):
        self.h = handle
        self.nbytes = nbytes

    def view(self, off, dtype, shape):
        import math
        isz = {F32: 4, BF16: 2, U8: 1, I32: 4, F16: 2}[dtype]
        n = math.prod(shape)
        assert off % 64 == 0, off
        assert off + n * isz <= self.nbytes, (off, n * isz, self.nbytes)
        ap = self.h[:, off:off + n * isz]
        ap = ap.bitcast(dtype)
        if len(shape) > 1:
            dims = " ".join(f"d{i}" for i in range(len(shape)))
            kw = {f"d{i}": shape[i] for i in range(len(shape))}
            ap = ap.rearrange(f"p ({dims}) -> p {dims}", **kw)
        return ap


import numpy as np
from contextlib import ExitStack


LNC = 128
SUPS = [(0, 1536), (1536, 1536), (3072, 1024)]


def build_nc(phases=("PD", "PK", "PQ", "PA", "PO", "PF1", "PF2", "PL"), debug=()):
    nc = bass.Bass("TRN2", target_bir_lowering=False, debug=False)

    def din(name, shape, dt):
        return nc.dram_tensor(name, shape, dt, kind="ExternalInput").ap()

    xt = din("xt", [D, S], F32)
    xtl = din("xtl", [D, TLOC], F32)
    xn2t = din("xn2t", [D, TLOC], BF16)
    wqkvT = din("wqkvT", [D, 6144], F32)
    woT = din("woT", [D, D], F32)
    hb_g = din("hb_gate", [INTER, D // 2], U8)
    hb_u = din("hb_up", [INTER, D // 2], U8)
    hb_d = din("hb_down", [D, INTER // 2], U8)
    sT_g = din("sT_gate", [INTER, 32], F16)
    sT_u = din("sT_up", [INTER, 32], F16)
    sT_d = din("sT_down", [D, 86], F16)
    s_g = din("s_gate", [32, INTER], F16)
    s_u = din("s_up", [32, INTER], F16)
    s_d = din("s_down", [86, D], F16)
    z_g = din("z_gate", [32, INTER // 2], U8)
    z_u = din("z_up", [32, INTER // 2], U8)
    z_d = din("z_down", [86, D // 2], U8)
    cosK = din("cosK", [128, S], F32)
    sinK = din("sinK", [128, S], F32)
    cosQ = din("cosQ", [128, S], F32)
    sinQ = din("sinQ", [128, S], F32)
    p64t = din("p64t", [128, 128], BF16)
    i128 = din("i128", [128, 128], F32)
    i128b = din("i128b", [128, 128], BF16)
    oh32 = din("oh32", [32, 128, 32], BF16)
    oh86 = din("oh86", [86, 128, 86], BF16)
    masks = din("masks", [32, 128, 512], BF16)
    offs = din("offs", [1, 2], I32)

    out_d = nc.dram_tensor("out", [D, TLOC], F32, kind="ExternalOutput").ap()

    def scratch(name, shape, dt):
        kind = "ExternalOutput" if name in debug else "Internal"
        return nc.dram_tensor(name, shape, dt, kind=kind).ap()

    xn_d = scratch("xn_d", [D, S], BF16)
    k_d = scratch("k_d", [1024, S], BF16)
    v_d = scratch("v_d", [S, 1024], BF16)
    x2_d = scratch("x2_d", [D, TLOC], F32)
    proj_d = scratch("proj_d", [D, TLOC], BF16)
    attn_d = scratch("attn_d", [D, TLOC], BF16)
    inter_d = scratch("inter_d", [INTER, TLOC], BF16)
    wtg_d = scratch("wtg_d", [32, 128, INTER], BF16)
    wtu_d = scratch("wtu_d", [32, 128, INTER], BF16)
    wtd_d = scratch("wtd_d", [86, 128, D], BF16)
    zpg_d = scratch("zpg_d", [32, INTER], BF16)
    zpu_d = scratch("zpu_d", [32, INTER], BF16)
    zpd_d = scratch("zpd_d", [86, D], BF16)
    q_d = scratch("q_d", [4096, TLOC], BF16)
    mdbg_d = scratch("mdbg", [16, 128, 512], BF16)
    wcdbg_d = scratch("wcdbg", [128, 8 * 384], BF16)
    kvdbg_d = scratch("kvdbg", [2, 128, 4096], BF16)
    xn2dbg_d = scratch("xn2dbg", [4096, 1024], BF16)
    r3dbg_d = scratch("r3dbg", [4, 128, 512], F32)

    P = Plan()
    AB = 200 * 1024

    stk = ExitStack()
    with stk:
        arena_t = stk.enter_context(nc.sbuf_tensor("arena", [128, AB], U8))
        A = Arena(arena_t, AB)
        psA = stk.enter_context(nc.psum_tensor("psA", [128, 4096], F32))

        def bank(i, cols=512, rows=128, row0=0):
            return psA[row0:row0 + rows, 512 * i:512 * i + cols]

        ones128 = A.view(0, BF16, (64,))[:, 0:1]
        onecol_t = A.view(128, BF16, (128,))
        onecol = onecol_t[0:1, :]
        p64_sb = A.view(384, BF16, (128,))
        i128_sb = A.view(640, F32, (128,))
        eps_sb = A.view(1152, F32, (16,))[0:1, 0:1]
        invS_sb = A.view(1216, F32, (16,))[0:1, 0:1]
        offs_sb = A.view(1280, I32, (16,))[0:1, 0:2]
        i128b_sb = A.view(1408, BF16, (128,))
        CONST_END = 2048

        P.op("g", lambda g: g.memset(ones128, 1.0), tag="c0")
        P.op("g", lambda g: g.memset(onecol_t[0:1, :], 1.0), tag="c1")
        P.op("g", lambda g: g.memset(eps_sb, 1e-5), tag="c2")
        P.op("g", lambda g: g.memset(invS_sb, 1.0 / 4096.0), tag="c3")
        P.dma("g", "gd0", lambda g: g.dma_start(out=p64_sb, in_=p64t), tag="c4")
        P.dma("g", "gd0", lambda g: g.dma_start(out=i128_sb, in_=i128), tag="c5")
        P.dma("g", "gd0", lambda g: g.dma_start(out=i128b_sb, in_=i128b), tag="c5b")
        P.dma("y", "yd0", lambda y: y.dma_start(out=offs_sb, in_=offs), tag="c6")

        def mkseg(base):
            cur = [base]
            def seg(nbytes):
                off = cur[0]
                cur[0] += (nbytes + 63) & ~63
                assert cur[0] <= AB, (off, nbytes, cur[0])
                return off
            return seg, cur

        if "PD" in phases:
            seg, cur = mkseg(CONST_END)
            HB = [A.view(seg(5504), U8, (5504,)) for _ in range(2)]
            NIB = [A.view(seg(11008), U8, (11008,)) for _ in range(2)]
            WBF = [A.view(seg(22016), BF16, (11008,)) for _ in range(2)]
            STS16 = A.view(seg(5504), F16, (2752,))
            STS = A.view(seg(11008), F32, (2752,))
            ZB = A.view(seg(5504), U8, (5504,))
            ZNIB = A.view(seg(11008), U8, (11008,))
            ZT = A.view(seg(22016), F32, (5504,))
            SN16 = A.view(seg(11008), F16, (5504,))
            SN = A.view(seg(22016), F32, (5504,))
            ZPB = A.view(seg(11008), BF16, (5504,))
            TEV = [A.view(seg(2048), BF16, (1024,)) for _ in range(2)]
            PSB = psA[:, 3584:4096].bitcast(BF16)

            hb_tok = [None, None]
            nib_tok = [None, None]
            wbf_tok = [None, None]
            tev_free = [None, None]
            psb_free = None
            last_sc = None
            st_cast = None
            for mi, (hb, sTp, wt, N, K) in enumerate([
                    (hb_g, sT_g, wtg_d, INTER, D),
                    (hb_u, sT_u, wtu_d, INTER, D),
                    (hb_d, sT_d, wtd_d, D, INTER)]):
                NT, G = N // 128, K // 128
                st_ld = P.dma("g", "pdt", lambda g, sTp=sTp, NT=NT, G=G: g.dma_start(
                    out=STS16[:, 0:NT * G].rearrange("p (n g) -> p n g", n=NT),
                    in_=sTp.rearrange("(n p) g -> p n g", p=128)),
                    waits=[st_cast], tag=f"pd_sT{mi}")
                st_cast = P.op("v", lambda v, NT=NT, G=G: v.tensor_copy(
                    STS[:, 0:NT * G], STS16[:, 0:NT * G]),
                    waits=[st_ld, last_sc], tag=f"pd_sTc{mi}")
                for nt in range(NT):
                    bi = nt % 2
                    hb_tok[bi] = P.dma("y", f"pdh{bi}",
                        lambda y, hb=hb, nt=nt, K=K, bi=bi: y.dma_start(
                            out=HB[bi][:, 0:K // 2], in_=hb[nt * 128:(nt + 1) * 128, :]),
                        waits=[nib_tok[bi]], tag=f"pd_hb{mi}_{nt}")
                    t1 = P.op("v", lambda v, bi=bi, K=K: v.tensor_scalar(
                        out=NIB[bi][:, 0:K:2], in0=HB[bi][:, 0:K // 2], scalar1=15,
                        scalar2=None, op0=ALU.bitwise_and),
                        waits=[hb_tok[bi], wbf_tok[bi]], tag=f"pd_lo{mi}_{nt}")
                    t2 = P.op("v", lambda v, bi=bi, K=K: v.tensor_scalar(
                        out=NIB[bi][:, 1:K:2], in0=HB[bi][:, 0:K // 2], scalar1=4,
                        scalar2=None, op0=ALU.logical_shift_right),
                        waits=[hb_tok[bi], wbf_tok[bi]], tag=f"pd_hi{mi}_{nt}")
                    nib_tok[bi] = t2
                    last_sc = P.op("v", lambda v, bi=bi, nt=nt, G=G, K=K, NT=NT:
                        v.tensor_tensor(
                            out=WBF[bi][:, 0:K].rearrange("p (g k) -> p g k", g=G),
                            in0=NIB[bi][:, 0:K].rearrange("p (g k) -> p g k", g=G),
                            in1=STS[:, 0:NT * G].rearrange("p (n g) -> p n g", n=NT)
                                [:, nt, :][:, :, None].broadcast_to([128, G, 128]),
                            op=ALU.mult),
                        waits=[t1, t2, st_cast], tag=f"pd_sc{mi}_{nt}")
                    ntr = K // 128
                    trt = None
                    for b in range((ntr + 7) // 8):
                        nk = min(8, ntr - b * 8)
                        for j in range(nk):
                            kt = b * 8 + j
                            trt = P.op("t", lambda t, bi=bi, kt=kt, j=j: t.transpose(
                                PSB[:, j * 128:(j + 1) * 128],
                                WBF[bi][:, kt * 128:(kt + 1) * 128], i128b_sb),
                                waits=([last_sc, psb_free] if j == 0 else []),
                                tag=f"pd_tr{mi}_{nt}_{kt}")
                        tsl = b % 2
                        ev = P.op("s", lambda s, tsl=tsl, nk=nk: s.activation(
                            out=TEV[tsl][:, 0:nk * 128], in_=PSB[:, 0:nk * 128],
                            func=AF.Copy),
                            waits=[trt, tev_free[tsl]], tag=f"pd_te{mi}_{nt}_{b}")
                        psb_free = ev
                        tev_free[tsl] = P.dma("s", f"pdw{tsl}",
                            lambda y, wt=wt, nt=nt, b=b, nk=nk, tsl=tsl: y.dma_start(
                                out=wt[b * 8:b * 8 + nk, :, nt * 128:(nt + 1) * 128]
                                    .rearrange("a p c -> p a c"),
                                in_=TEV[tsl][:, 0:nk * 128]
                                    .rearrange("p (a c) -> p a c", a=nk)),
                            waits=[ev], tag=f"pd_wst{mi}_{nt}_{b}")
                    wbf_tok[bi] = trt

            zb_tok = znib_tok = zt_tok = zpb_tok = None
            sn_cast = None
            for mi, (zby, sn, zp_o, N, G) in enumerate([
                    (z_g, s_g, zpg_d, INTER, 32),
                    (z_u, s_u, zpu_d, INTER, 32),
                    (z_d, s_d, zpd_d, D, 86)]):
                zb_tok = P.dma("y", "pdz", lambda y, zby=zby, G=G, N=N: y.dma_start(
                    out=ZB[0:G, 0:N // 2], in_=zby),
                    waits=[znib_tok], tag=f"pd_zb{mi}")
                e1 = P.op("v", lambda v, G=G, N=N: v.tensor_scalar(
                    out=ZNIB[0:G, 0:N:2], in0=ZB[0:G, 0:N // 2], scalar1=15,
                    scalar2=None, op0=ALU.bitwise_and),
                    waits=[zb_tok, zt_tok], tag=f"pd_zlo{mi}")
                e2 = P.op("v", lambda v, G=G, N=N: v.tensor_scalar(
                    out=ZNIB[0:G, 1:N:2], in0=ZB[0:G, 0:N // 2], scalar1=4,
                    scalar2=None, op0=ALU.logical_shift_right),
                    waits=[zb_tok, zt_tok], tag=f"pd_zhi{mi}")
                znib_tok = e2
                for h in range(2):
                    NH_ = N // 2
                    sn_ld = P.dma("g", "pdn", lambda g, sn=sn, G=G, NH_=NH_, h=h:
                        g.dma_start(out=SN16[0:G, 0:NH_], in_=sn[:, h * NH_:(h + 1) * NH_]),
                        waits=[sn_cast], tag=f"pd_sn{mi}_{h}")
                    sn_cast = P.op("v", lambda v, G=G, NH_=NH_: v.tensor_copy(
                        SN[0:G, 0:NH_], SN16[0:G, 0:NH_]),
                        waits=[sn_ld, zpb_tok], tag=f"pd_snc{mi}_{h}")
                    zt_tok = P.op("v", lambda v, G=G, NH_=NH_, h=h: v.tensor_scalar(
                        out=ZT[0:G, 0:NH_], in0=ZNIB[0:G, h * NH_:(h + 1) * NH_],
                        scalar1=-1.0, scalar2=-1.0, op0=ALU.mult, op1=ALU.add),
                        waits=[e1, e2, zpb_tok], tag=f"pd_zt{mi}_{h}")
                    zpv = P.op("v", lambda v, G=G, NH_=NH_: v.tensor_tensor(
                        out=ZPB[0:G, 0:NH_], in0=ZT[0:G, 0:NH_], in1=SN[0:G, 0:NH_],
                        op=ALU.mult),
                        waits=[zt_tok, sn_cast], tag=f"pd_zp{mi}_{h}")
                    zpb_tok = P.dma("y", "pdp", lambda y, zp_o=zp_o, G=G, NH_=NH_, h=h:
                        y.dma_start(out=zp_o[:, h * NH_:(h + 1) * NH_], in_=ZPB[0:G, 0:NH_]),
                        waits=[zpv], tag=f"pd_zpst{mi}_{h}")

        if "PK" in phases:
            P.start_phase()
            seg, cur = mkseg(CONST_END)
            XN = A.view(seg(98304), BF16, (32, 1536))
            XF = [A.view(seg(16384), F32, (32, LNC)) for _ in range(2)]
            WKVF = A.view(seg(16384), F32, (32, 128))
            WKV = [A.view(seg(8192), BF16, (32, 128)) for _ in range(2)]
            COSS = A.view(seg(6144), F32, (3, 512))
            SINS = A.view(seg(6144), F32, (3, 512))
            KSB = A.view(seg(1024), BF16, (512,))
            TMP1 = A.view(seg(2048), F32, (512,))
            TMP2 = A.view(seg(2048), F32, (512,))
            KR = A.view(seg(1024), BF16, (512,))
            VSB = A.view(seg(2048), F32, (512,))
            VT = A.view(seg(1024), BF16, (512,))
            RSTB = A.view(seg(512), F32, (LNC,))
            MRB = A.view(seg(512), F32, (LNC,))
            XSQ = A.view(seg(8192), BF16, (32, LNC))
            ROWM = A.view(seg(512), F32, (LNC,))[0:1, :]
            ROWS2 = A.view(seg(512), F32, (LNC,))[0:1, :]
            ROWT = A.view(seg(512), F32, (LNC,))[0:1, :]
            ROWRS = A.view(seg(512), F32, (LNC,))[0:1, :]
            ROWMR = A.view(seg(512), F32, (LNC,))[0:1, :]
            ROWRSB = A.view(seg(256), BF16, (LNC,))[0:1, :]
            ROWMRB = A.view(seg(256), BF16, (LNC,))[0:1, :]

            bank_tok = {}
            wkv_tok = [None, None]
            wcast_tok = [None, None]
            kv_w_free = {}
            sup_end_tok = []
            xf_free = [None, None]
            cs_use = []
            for sup, (s0, SW) in enumerate(SUPS):
                n_tt = SW // 512
                cs1 = P.dma("y", "pkc", lambda y, s0=s0, SW=SW, n_tt=n_tt: y.dma_start(
                    out=COSS[:, 0:n_tt, :], in_=cosK[:, s0:s0 + SW]
                    .rearrange("p (a b) -> p a b", b=512)),
                    waits=cs_use, tag=f"pk_cos{sup}")
                cs2 = P.dma("y", "pkc", lambda y, s0=s0, SW=SW, n_tt=n_tt: y.dma_start(
                    out=SINS[:, 0:n_tt, :], in_=sinK[:, s0:s0 + SW]
                    .rearrange("p (a b) -> p a b", b=512)),
                    waits=cs_use, tag=f"pk_sin{sup}")

                n_ch = SW // LNC
                m2_carry = [None, None]
                apply_last = {}
                for ch in range(n_ch):
                    c0 = ch * LNC
                    bi = ch % 2
                    xld = P.dma("g", f"pkx{bi}", lambda g, s0=s0, c0=c0, bi=bi: g.dma_start(
                        out=XF[bi][:],
                        in_=xt[:, s0 + c0:s0 + c0 + LNC].rearrange("(i p) t -> p i t", p=128)),
                        waits=[xf_free[bi]], tag=f"pk_x{sup}_{ch}")
                    cast = P.op("s", lambda s, bi=bi, c0=c0: s.activation(
                        out=XN[:, :, c0:c0 + LNC], in_=XF[bi][:], func=AF.Copy),
                        waits=[xld] + sup_end_tok +
                              ([apply_last[ch - 2]] if ch >= 2 else []),
                        tag=f"pk_cast{sup}_{ch}")
                    sq = P.op("s", lambda s, c0=c0: s.activation(
                        out=XSQ[:], in_=XN[:, :, c0:c0 + LNC], func=AF.Square),
                        waits=[cast, m2_carry[bi]], tag=f"pk_sq{sup}_{ch}")
                    mm_tok = None
                    for i in range(32):
                        P.op("t", lambda t, i=i, c0=c0: t.matmul(
                            bank(4, LNC, rows=1), ones128, XN[:, i, c0:c0 + LNC],
                            start=(i == 0), stop=(i == 31)),
                            waits=[cast, bank_tok.get(4) if i == 0 else None],
                            tag=f"pk_s1_{sup}_{ch}_{i}")
                        mm_tok = P.op("t", lambda t, i=i: t.matmul(
                            bank(5, LNC, rows=1), ones128, XSQ[:, i, :],
                            start=(i == 0), stop=(i == 31)),
                            waits=[sq, bank_tok.get(5) if i == 0 else None],
                            tag=f"pk_s2_{sup}_{ch}_{i}")
                    m2_carry[bi] = mm_tok
                    mm1_tok = P.tok("t")
                    e1 = P.op("s", lambda s: s.activation(
                        out=ROWM, in_=bank(4, LNC, rows=1), func=AF.Identity, scale=invS_sb),
                        waits=[mm_tok], tag=f"pk_m{sup}_{ch}")
                    e2 = P.op("s", lambda s: s.activation(
                        out=ROWS2, in_=bank(5, LNC, rows=1), func=AF.Identity, scale=invS_sb),
                        waits=[mm_tok], tag=f"pk_s2r{sup}_{ch}")
                    bank_tok[4] = e1
                    bank_tok[5] = e2
                    v1 = P.op("v", lambda v: v.tensor_tensor(out=ROWT, in0=ROWM, in1=ROWM,
                        op=ALU.mult), waits=[e1], tag=f"pk_mm{sup}_{ch}")
                    v2 = P.op("v", lambda v: v.tensor_tensor(out=ROWT, in0=ROWS2, in1=ROWT,
                        op=ALU.subtract), waits=[v1, e2], tag=f"pk_var{sup}_{ch}")
                    e3 = P.op("s", lambda s: s.activation(out=ROWRS, in_=ROWT, func=AF.Sqrt,
                        bias=eps_sb), waits=[v2], tag=f"pk_sqrt{sup}_{ch}")
                    v3 = P.op("v", lambda v: v.reciprocal(ROWRS, ROWRS), waits=[e3],
                        tag=f"pk_rcp{sup}_{ch}")
                    v4 = P.op("v", lambda v: v.tensor_tensor(out=ROWMR, in0=ROWM, in1=ROWRS,
                        op=ALU.mult), waits=[v3], tag=f"pk_mr{sup}_{ch}")
                    v5 = P.op("v", lambda v: v.tensor_copy(ROWRSB, ROWRS), waits=[v3],
                        tag=f"pk_cb1{sup}_{ch}")
                    v6 = P.op("v", lambda v: v.tensor_copy(ROWMRB, ROWMR), waits=[v4],
                        tag=f"pk_cb2{sup}_{ch}")
                    b1 = P.op("t", lambda t: t.matmul(bank(4, LNC), onecol, ROWRSB,
                        start=True, stop=True), waits=[v5, bank_tok.get(4)], tag=f"pk_b1{sup}_{ch}")
                    b2 = P.op("t", lambda t: t.matmul(bank(5, LNC), onecol, ROWMRB,
                        start=True, stop=True), waits=[v6, bank_tok.get(5)], tag=f"pk_b2{sup}_{ch}")
                    eb1 = P.op("s", lambda s: s.activation(out=RSTB, in_=bank(4, LNC),
                        func=AF.Copy), waits=[b1], tag=f"pk_eb1{sup}_{ch}")
                    eb2 = P.op("s", lambda s: s.activation(out=MRB, in_=bank(5, LNC),
                        func=AF.Copy), waits=[b2], tag=f"pk_eb2{sup}_{ch}")
                    bank_tok[4] = eb1
                    bank_tok[5] = eb2
                    a1 = P.op("v", lambda v, bi=bi, c0=c0: v.tensor_tensor(
                        out=XF[bi][:], in0=XN[:, :, c0:c0 + LNC],
                        in1=RSTB[:, None, :].broadcast_to([128, 32, LNC]), op=ALU.mult),
                        waits=[eb1, mm1_tok, sq], tag=f"pk_a1{sup}_{ch}")
                    a2 = P.op("v", lambda v, bi=bi, c0=c0: v.tensor_tensor(
                        out=XN[:, :, c0:c0 + LNC], in0=XF[bi][:],
                        in1=MRB[:, None, :].broadcast_to([128, 32, LNC]), op=ALU.subtract),
                        waits=[a1, eb2], tag=f"pk_a2{sup}_{ch}")
                    apply_last[ch] = a2
                    xf_free[bi] = a2
                    P.dma("y", "pkn", lambda y, s0=s0, c0=c0: y.dma_start(
                        out=xn_d[:, s0 + c0:s0 + c0 + LNC].rearrange("(i p) t -> p i t", p=128),
                        in_=XN[:, :, c0:c0 + LNC]),
                        waits=[a2], tag=f"pk_xnst{sup}_{ch}")
                apply_toks = [apply_last[n_ch - 1], apply_last[n_ch - 2]]

                for ot in range(16):
                    bi = ot % 2
                    wkvf_tok = P.dma("g", "pkwf", lambda y, ot=ot: y.dma_start(
                        out=WKVF[:],
                        in_=wqkvT[:, 4096 + ot * 128:4096 + (ot + 1) * 128]
                            .rearrange("(i p) c -> p i c", p=128)),
                        waits=[wcast_tok[1 - bi]], tag=f"pk_wf{sup}_{ot}")
                    wcast_tok[bi] = P.op("s", lambda s, bi=bi: s.activation(
                        out=WKV[bi][:], in_=WKVF[:], func=AF.Copy),
                        waits=[wkvf_tok, kv_w_free.get(bi)], tag=f"pk_wc{sup}_{ot}")
                    wkv_tok[bi] = wcast_tok[bi]
                    for tt in range(n_tt):
                        bq = (ot * n_tt + tt) % 4
                        mm = None
                        for i in range(32):
                            mm = P.op("t", lambda t, bi=bi, i=i, tt=tt, bq=bq: t.matmul(
                                bank(bq), WKV[bi][:, i, :], XN[:, i, tt * 512:(tt + 1) * 512],
                                start=(i == 0), stop=(i == 31)),
                                waits=([wkv_tok[bi], bank_tok.get(bq)] +
                                       (apply_toks if i == 0 else [])),
                                tag=f"pk_mm{sup}_{ot}_{tt}_{i}")
                        kv_w_free[bi] = mm
                        if ot < 8:
                            ev = P.op("s", lambda s, bq=bq: s.activation(
                                out=KSB, in_=bank(bq), func=AF.Copy),
                                waits=[mm, bank_tok.get("ksb")], tag=f"pk_kev{sup}_{ot}_{tt}")
                            bank_tok[bq] = ev
                            pm = P.op("t", lambda t: t.matmul(bank(6), p64_sb, KSB,
                                start=True, stop=True),
                                waits=[ev, bank_tok.get(6)], tag=f"pk_pm{sup}_{ot}_{tt}")
                            r1 = P.op("v", lambda v, tt=tt: v.tensor_tensor(
                                out=TMP1, in0=KSB, in1=COSS[:, tt, :], op=ALU.mult),
                                waits=[ev, cs1], tag=f"pk_r1{sup}_{ot}_{tt}")
                            r2 = P.op("v", lambda v, tt=tt: v.tensor_tensor(
                                out=TMP2, in0=bank(6), in1=SINS[:, tt, :], op=ALU.mult),
                                waits=[pm, cs2], tag=f"pk_r2{sup}_{ot}_{tt}")
                            bank_tok[6] = r2
                            r3 = P.op("v", lambda v: v.tensor_tensor(
                                out=KR, in0=TMP1, in1=TMP2, op=ALU.add),
                                waits=[r1, r2, bank_tok.get("kr")], tag=f"pk_r3{sup}_{ot}_{tt}")
                            bank_tok["ksb"] = r3
                            kst = P.dma("y", "pkk", lambda y, ot=ot, s0=s0, tt=tt: y.dma_start(
                                out=k_d[ot * 128:(ot + 1) * 128, s0 + tt * 512:s0 + (tt + 1) * 512],
                                in_=KR), waits=[r3], tag=f"pk_kst{sup}_{ot}_{tt}")
                            bank_tok["kr"] = kst
                        else:
                            ev = P.op("s", lambda s, bq=bq: s.activation(
                                out=VSB, in_=bank(bq), func=AF.Copy),
                                waits=[mm, bank_tok.get("vsb")], tag=f"pk_vev{sup}_{ot}_{tt}")
                            bank_tok[bq] = ev
                            trt = None
                            for j in range(4):
                                trt = P.op("t", lambda t, j=j: t.transpose(
                                    bank(7)[:, j * 128:(j + 1) * 128],
                                    VSB[:, j * 128:(j + 1) * 128], i128_sb),
                                    waits=[ev, bank_tok.get(7) if j == 0 else None],
                                    tag=f"pk_tr{sup}_{ot}_{tt}_{j}")
                            vt = P.op("s", lambda s: s.activation(
                                out=VT, in_=bank(7), func=AF.Copy),
                                waits=[trt, bank_tok.get("vt")], tag=f"pk_vt{sup}_{ot}_{tt}")
                            bank_tok[7] = vt
                            bank_tok["vsb"] = vt
                            vst = P.dma("y", "pkv", lambda y, ot=ot, s0=s0, tt=tt: y.dma_start(
                                out=v_d[s0 + tt * 512:s0 + (tt + 1) * 512,
                                        (ot - 8) * 128:(ot - 7) * 128]
                                    .rearrange("(j p) c -> p j c", p=128),
                                in_=VT[:].rearrange("p (j c) -> p j c", j=4)),
                                waits=[vt], tag=f"pk_vst{sup}_{ot}_{tt}")
                            bank_tok["vt"] = vst
                cs_use = [P.tok("v")]
                sup_end_tok = [P.tok("t"), P.tok("pkn")]

        QRES_OFF = AB - 65536
        if "PQ" in phases:
            P.start_phase()
            seg, cur = mkseg(CONST_END)
            QRES = A.view(QRES_OFF, BF16, (32, 1024))
            XNL = A.view(seg(65536), BF16, (32, 1024))
            WQF = A.view(seg(16384), F32, (32, 128))
            WQ = [A.view(seg(8192), BF16, (32, 128)) for _ in range(2)]
            CQ = A.view(seg(4096), F32, (2, 512))
            SQ_ = A.view(seg(4096), F32, (2, 512))
            KSB2 = A.view(seg(1024), BF16, (512,))
            TMQ1 = A.view(seg(2048), F32, (512,))
            TMQ2 = A.view(seg(2048), F32, (512,))

            rlo = nc.sync.alloc_register("rlo")
            rhi = nc.sync.alloc_register("rhi")
            rg1 = P.op("y", lambda y: y.reg_load(rlo, offs_sb[0:1, 0:1]), tag="pq_rl")
            rg2 = P.op("y", lambda y: y.reg_load(rhi, offs_sb[0:1, 1:2]), tag="pq_rh")
            xnl_toks = []
            for hf, reg, tok in ((0, rlo, rg1), (1, rhi, rg2)):
                xnl_toks.append(P.dma("y", "pql", lambda y, hf=hf, reg=reg: y.dma_start(
                    out=XNL[:, :, hf * 512:(hf + 1) * 512],
                    in_=bass.AP(xn_d.tensor, reg, [[S, 128], [128 * S, 32], [1, 512]])),
                    waits=[tok], tag=f"pq_xnl{hf}"))
                xnl_toks.append(P.dma("y", "pql", lambda y, hf=hf, reg=reg: y.dma_start(
                    out=CQ[:, hf, :],
                    in_=bass.AP(cosQ.tensor, reg, [[S, 128], [1, 512]])),
                    waits=[tok], tag=f"pq_cq{hf}"))
                xnl_toks.append(P.dma("y", "pql", lambda y, hf=hf, reg=reg: y.dma_start(
                    out=SQ_[:, hf, :],
                    in_=bass.AP(sinQ.tensor, reg, [[S, 128], [1, 512]])),
                    waits=[tok], tag=f"pq_sq{hf}"))
            wq_tok = [None, None]
            qcast_tok = [None, None]
            qb_tok = {}
            q_w_free = {}
            for ot in range(32):
                bi = ot % 2
                wqf_tok = P.dma("g", "pqwf", lambda y, ot=ot: y.dma_start(
                    out=WQF[:],
                    in_=wqkvT[:, ot * 128:(ot + 1) * 128].rearrange("(i p) c -> p i c", p=128)),
                    waits=[qcast_tok[1 - bi]], tag=f"pq_wf{ot}")
                qcast_tok[bi] = P.op("s", lambda s, bi=bi: s.activation(
                    out=WQ[bi][:], in_=WQF[:], func=AF.Copy),
                    waits=[wqf_tok, q_w_free.get(bi)], tag=f"pq_wc{ot}")
                wq_tok[bi] = qcast_tok[bi]
                for th in range(2):
                    bq = (ot * 2 + th) % 4
                    mm = None
                    for i in range(32):
                        mm = P.op("t", lambda t, bi=bi, i=i, th=th, bq=bq: t.matmul(
                            bank(bq), WQ[bi][:, i, :], XNL[:, i, th * 512:(th + 1) * 512],
                            start=(i == 0), stop=(i == 31)),
                            waits=([wq_tok[bi], qb_tok.get(bq)] +
                                   (xnl_toks if ot == 0 and i == 0 else [])),
                            tag=f"pq_mm{ot}_{th}_{i}")
                    q_w_free[bi] = mm
                    ev = P.op("s", lambda s, bq=bq: s.activation(
                        out=KSB2, in_=bank(bq), func=AF.Copy),
                        waits=[mm, qb_tok.get("ksb")], tag=f"pq_ev{ot}_{th}")
                    qb_tok[bq] = ev
                    pm = P.op("t", lambda t: t.matmul(bank(6), p64_sb, KSB2,
                        start=True, stop=True), waits=[ev, qb_tok.get(6)], tag=f"pq_pm{ot}_{th}")
                    r1 = P.op("v", lambda v, th=th: v.tensor_tensor(
                        out=TMQ1, in0=KSB2, in1=CQ[:, th, :], op=ALU.mult),
                        waits=[ev], tag=f"pq_r1{ot}_{th}")
                    r2 = P.op("v", lambda v, th=th: v.tensor_tensor(
                        out=TMQ2, in0=bank(6), in1=SQ_[:, th, :], op=ALU.mult),
                        waits=[pm], tag=f"pq_r2{ot}_{th}")
                    qb_tok[6] = r2
                    r3 = P.op("v", lambda v, ot=ot, th=th: v.tensor_tensor(
                        out=QRES[:, ot, th * 512:(th + 1) * 512], in0=TMQ1, in1=TMQ2,
                        op=ALU.add), waits=[r1, r2], tag=f"pq_r3{ot}_{th}")
                    qb_tok["ksb"] = r3
            if "q_d" in debug:
                P.dma("y", "pqd", lambda y: y.dma_start(
                    out=q_d.rearrange("(i p) t -> p i t", p=128), in_=QRES[:]),
                    waits=[P.tok("v")], tag="pq_qdump")

        if "PA" in phases:
            P.start_phase()
            seg, cur = mkseg(CONST_END)
            QRES = A.view(QRES_OFF, BF16, (32, 1024))
            KW = [A.view(seg(8192), BF16, (4096,)) for _ in range(2)]
            VW = [A.view(seg(8192), BF16, (32, 128)) for _ in range(2)]
            MSK = A.view(seg(16384), BF16, (16, 512))
            ESB = [A.view(seg(1024), BF16, (512,)) for _ in range(3)]
            DENR = A.view(seg(2048), F32, (512,))[0:1, :]
            DENB = A.view(seg(1024), BF16, (512,))[0:1, :]
            REC = A.view(seg(2048), F32, (512,))
            AVN = A.view(seg(1024), BF16, (512,))

            kw_tok = [None, None]
            vw_tok = [None, None]
            kw_free = [None, None]
            vw_free = [None, None]
            esb_free = [None, None, None]
            SCB = (0, 1, 5)
            bankA = {}
            denb_tok = denr_tok = rec_tok = avn_tok = ast_tok = None
            msk_tok = None
            for chunk in range(2):
                bound = 16 if chunk == 0 else 32
                mstep0 = 0 if chunk == 0 else 16
                msk_tok = P.dma("y", "pam", lambda y, chunk=chunk: y.dma_start(
                    out=MSK[:], in_=masks[chunk * 16:(chunk + 1) * 16]
                    .rearrange("m p c -> p m c")),
                    waits=[P.tok("v")] if chunk == 1 else [], tag=f"pa_m{chunk}")
                if chunk == 0 and "mdbg" in debug:
                    P.dma("y", "pamd", lambda y: y.dma_start(
                        out=mdbg_d.rearrange("m p c -> p m c"), in_=MSK[:]),
                        waits=[msk_tok], tag="pa_mdump")
                for kvh in range(8):
                    bi = kvh % 2
                    kw_tok[bi] = P.dma("y", f"pak{bi}", lambda y, kvh=kvh, bound=bound, bi=bi:
                        y.dma_start(out=KW[bi][:, 0:bound * 128],
                                    in_=k_d[kvh * 128:(kvh + 1) * 128, 0:bound * 128]),
                        waits=[kw_free[bi]], tag=f"pa_kw{chunk}_{kvh}")
                    vw_tok[bi] = P.dma("y", f"pav{bi}", lambda y, kvh=kvh, bound=bound, bi=bi:
                        y.dma_start(out=VW[bi][:, 0:bound, :],
                                    in_=v_d[0:bound * 128, kvh * 128:(kvh + 1) * 128]
                                    .rearrange("(j p) c -> p j c", p=128)),
                        waits=[vw_free[bi]], tag=f"pa_vw{chunk}_{kvh}")
                    if chunk == 0 and kvh == 0 and "kvdbg" in debug:
                        DBGK = A.view(65536, BF16, (4096,))
                        DBGV = A.view(65536 + 8192, BF16, (32, 128))
                        dc1 = P.op("s", lambda s: s.activation(out=DBGK[:], in_=KW[0][:],
                            func=AF.Copy), waits=[kw_tok[0]], tag="dbg_k")
                        dc2 = P.op("s", lambda s: s.activation(out=DBGV[:], in_=VW[0][:],
                            func=AF.Copy), waits=[vw_tok[0]], tag="dbg_v")
                        P.dma("y", "padbg", lambda y: y.dma_start(
                            out=kvdbg_d[0].rearrange("p (a c) -> p a c", a=32),
                            in_=DBGK[:].rearrange("p (a c) -> p a c", a=32)),
                            waits=[dc1], tag="dbg_kst")
                        P.dma("y", "padbg", lambda y: y.dma_start(
                            out=kvdbg_d[1].rearrange("p (a c) -> p a c", a=32),
                            in_=DBGV[:].rearrange("p a c -> p a c")),
                            waits=[dc2], tag="dbg_vst")
                    for qh in range(4):
                        head = kvh * 4 + qh
                        sc_tok = {}

                        def plan_sc(st_, bi=bi, head=head, chunk=chunk):
                            sb = SCB[st_ % 3]
                            return P.op("t", lambda t, bi=bi, st_=st_, sb=sb,
                                        head=head, chunk=chunk: t.matmul(
                                bank(sb), KW[bi][:, st_ * 128:(st_ + 1) * 128],
                                QRES[:, head, chunk * 512:(chunk + 1) * 512],
                                start=True, stop=True),
                                waits=[kw_tok[bi], bankA.get(sb)],
                                tag=f"pa_sc{chunk}_{head}_{st_}")

                        sc_tok[0] = plan_sc(0)
                        sc_tok[1] = plan_sc(1)
                        for st_ in range(bound):
                            si = st_ % 3
                            sb = SCB[si]
                            ex = P.op("s", lambda s, si=si, sb=sb: s.activation(
                                out=ESB[si][:], in_=bank(sb), func=AF.Exp),
                                waits=[sc_tok[st_], esb_free[si]], tag=f"pa_ex{chunk}_{head}_{st_}")
                            bankA[sb] = ex
                            dep = ex
                            if st_ >= mstep0:
                                dep = P.op("v", lambda v, si=si, st_=st_, mstep0=mstep0:
                                    v.tensor_tensor(out=ESB[si][:], in0=ESB[si][:],
                                                    in1=MSK[:, st_ - mstep0, :], op=ALU.mult),
                                    waits=[ex, msk_tok], tag=f"pa_mk{chunk}_{head}_{st_}")
                            if st_ + 2 < bound:
                                sc_tok[st_ + 2] = plan_sc(st_ + 2)
                            av = P.op("t", lambda t, bi=bi, st_=st_, si=si, bound=bound:
                                t.matmul(bank(2), VW[bi][:, st_, :], ESB[si][:],
                                         start=(st_ == 0), stop=(st_ == bound - 1),
                                         skip_group_check=True),
                                waits=[dep, vw_tok[bi],
                                       bankA.get(2) if st_ == 0 else None],
                                tag=f"pa_av{chunk}_{head}_{st_}")
                            dn = P.op("t", lambda t, st_=st_, si=si, bound=bound:
                                t.matmul(bank(3, rows=1), ones128, ESB[si][:],
                                         start=(st_ == 0), stop=(st_ == bound - 1),
                                         skip_group_check=True),
                                waits=[dep, bankA.get(3) if st_ == 0 else None],
                                tag=f"pa_dn{chunk}_{head}_{st_}")
                            esb_free[si] = dn
                        kw_free[bi] = P.tok("t")
                        vw_free[bi] = P.tok("t")
                        denr_tok = P.op("s", lambda s: s.activation(
                            out=DENR, in_=bank(3, rows=1), func=AF.Copy),
                            waits=[dn, denb_tok], tag=f"pa_de{chunk}_{head}")
                        bankA[3] = denr_tok
                        denb_tok = P.op("v", lambda v: v.tensor_copy(DENB, DENR),
                            waits=[denr_tok, bankA.get("bc")], tag=f"pa_db{chunk}_{head}")
                        bc = P.op("t", lambda t: t.matmul(bank(4), onecol, DENB,
                            start=True, stop=True),
                            waits=[denb_tok, bankA.get(4)], tag=f"pa_bc{chunk}_{head}")
                        bankA["bc"] = bc
                        rec_tok = P.op("v", lambda v: v.reciprocal(REC[:], bank(4)),
                            waits=[bc, avn_tok], tag=f"pa_rc{chunk}_{head}")
                        bankA[4] = rec_tok
                        avn_tok = P.op("v", lambda v: v.tensor_tensor(
                            out=AVN[:], in0=bank(2), in1=REC[:], op=ALU.mult),
                            waits=[rec_tok, av, ast_tok], tag=f"pa_an{chunk}_{head}")
                        bankA[2] = avn_tok
                        ast_tok = P.dma("y", "pas", lambda y, head=head, chunk=chunk:
                            y.dma_start(out=attn_d[head * 128:(head + 1) * 128,
                                                   chunk * 512:(chunk + 1) * 512],
                                        in_=AVN[:]),
                            waits=[avn_tok], tag=f"pa_as{chunk}_{head}")

        if "PO" in phases:
            P.start_phase()
            seg, cur = mkseg(67584)
            ATT = A.view(2048, BF16, (32, 1024))
            X2B = A.view(QRES_OFF, BF16, (32, 1024))
            XN2 = A.view(2048, BF16, (32, 1024))
            WOF = A.view(seg(8192), F32, (16, 128))
            WO = [A.view(seg(8192), BF16, (32, 128)) for _ in range(2)]
            PRJ = [A.view(seg(1024), BF16, (512,)) for _ in range(2)]
            PSQ = A.view(seg(1024), BF16, (512,))
            PRL = [A.view(seg(1024), BF16, (512,)) for _ in range(2)]
            XLF = [A.view(seg(2048), F32, (512,)) for _ in range(2)]
            TMPO = A.view(seg(2048), F32, (512,))
            X2T = [A.view(seg(2048), F32, (512,)) for _ in range(2)]
            RSTB2 = [A.view(seg(2048), F32, (512,)) for _ in range(2)]
            MRB2 = [A.view(seg(2048), F32, (512,)) for _ in range(2)]
            RSTB3 = [A.view(seg(2048), F32, (512,)) for _ in range(2)]
            MRB3 = [A.view(seg(2048), F32, (512,)) for _ in range(2)]
            OROWM = A.view(seg(2048), F32, (512,))[0:1, :]
            OROWS2 = A.view(seg(2048), F32, (512,))[0:1, :]
            OROWT = A.view(seg(2048), F32, (512,))[0:1, :]
            OROWRS = A.view(seg(2048), F32, (512,))[0:1, :]
            OROWMR = A.view(seg(2048), F32, (512,))[0:1, :]
            OROWRSB = A.view(seg(1024), BF16, (512,))[0:1, :]
            OROWMRB = A.view(seg(1024), BF16, (512,))[0:1, :]
            assert cur[0] <= QRES_OFF, ("PO segs overlap X2B", cur[0], QRES_OFF)

            att_tok = P.dma("y", "poa", lambda y: y.dma_start(
                out=ATT[:], in_=attn_d.rearrange("(i p) t -> p i t", p=128)),
                tag="po_att")
            wo_tok = [None, None]
            wocast = [None, None]
            ob_tok = {}
            wo_free = {}
            prj_free = [None, None]
            for dt in range(32):
                bi = dt % 2
                for hf in range(2):
                    wof_tok = P.dma("g", "powf", lambda y, dt=dt, hf=hf: y.dma_start(
                        out=WOF[:],
                        in_=woT[hf * 2048:(hf + 1) * 2048, dt * 128:(dt + 1) * 128]
                        .rearrange("(i p) c -> p i c", p=128)),
                        waits=[wocast[1 - bi] if hf == 0 else wocast[bi]],
                        tag=f"po_wf{dt}_{hf}")
                    wocast[bi] = P.op("s", lambda s, bi=bi, hf=hf: s.activation(
                        out=WO[bi][:, hf * 16:(hf + 1) * 16, :], in_=WOF[:], func=AF.Copy),
                        waits=[wof_tok, wo_free.get(bi) if hf == 0 else None],
                        tag=f"po_wc{dt}_{hf}")
                wo_tok[bi] = wocast[bi]
                for th in range(2):
                    bq = (dt * 2 + th) % 4
                    ps_ = bq % 2
                    mm = None
                    for ot in range(32):
                        mm = P.op("t", lambda t, bi=bi, ot=ot, th=th, bq=bq: t.matmul(
                            bank(bq), WO[bi][:, ot, :], ATT[:, ot, th * 512:(th + 1) * 512],
                            start=(ot == 0), stop=(ot == 31)),
                            waits=([wo_tok[bi], ob_tok.get(bq)] +
                                   ([att_tok] if dt == 0 and ot == 0 else [])),
                            tag=f"po_mm{dt}_{th}_{ot}")
                    wo_free[bi] = mm
                    ev = P.op("s", lambda s, bq=bq, ps_=ps_: s.activation(
                        out=PRJ[ps_][:], in_=bank(bq), func=AF.Copy),
                        waits=[mm, prj_free[ps_], ob_tok.get(("prjt", ps_))],
                        tag=f"po_ev{dt}_{th}")
                    ob_tok[bq] = ev
                    pst = P.dma("y", f"pop{ps_}", lambda y, dt=dt, th=th, ps_=ps_:
                        y.dma_start(out=proj_d[dt * 128:(dt + 1) * 128,
                                               th * 512:(th + 1) * 512],
                                    in_=PRJ[ps_][:]),
                        waits=[ev], tag=f"po_ps{dt}_{th}")
                    sq = P.op("s", lambda s, ps_=ps_: s.activation(
                        out=PSQ[:], in_=PRJ[ps_][:], func=AF.Square),
                        waits=[ev, ob_tok.get("psq")], tag=f"po_sq{dt}_{th}")
                    m1 = P.op("t", lambda t, th=th, dt=dt, ps_=ps_: t.matmul(
                        bank(4 + th, rows=1), ones128, PRJ[ps_][:],
                        start=(dt == 0), stop=(dt == 31)),
                        waits=[ev, ob_tok.get(4 + th) if dt == 0 else None],
                        tag=f"po_s1{dt}_{th}")
                    m2 = P.op("t", lambda t, th=th, dt=dt: t.matmul(
                        bank(6 + th, rows=1), ones128, PSQ[:],
                        start=(dt == 0), stop=(dt == 31)),
                        waits=[sq, ob_tok.get(6 + th) if dt == 0 else None],
                        tag=f"po_s2{dt}_{th}")
                    ob_tok["psq"] = m2
                    prj_free[ps_] = pst
                    ob_tok[("prjt", ps_)] = P.tok("t")
                    ob_tok[("stat", th)] = (m1, m2)
            oproj_done = P.tok("t")

            def ln_rows(pfx, sum_bank, sq_bank, RSB, MRB_, waits_extra, nw=512):
                e1 = P.op("s", lambda s: s.activation(out=OROWM[:, 0:nw],
                    in_=bank(sum_bank, nw, rows=1), func=AF.Identity, scale=invS_sb),
                    waits=waits_extra, tag=f"{pfx}_m")
                e2 = P.op("s", lambda s: s.activation(out=OROWS2[:, 0:nw],
                    in_=bank(sq_bank, nw, rows=1), func=AF.Identity, scale=invS_sb),
                    waits=waits_extra, tag=f"{pfx}_s2")
                v1 = P.op("v", lambda v: v.tensor_tensor(out=OROWT[:, 0:nw],
                    in0=OROWM[:, 0:nw], in1=OROWM[:, 0:nw], op=ALU.mult),
                    waits=[e1], tag=f"{pfx}_mm")
                v2 = P.op("v", lambda v: v.tensor_tensor(out=OROWT[:, 0:nw],
                    in0=OROWS2[:, 0:nw], in1=OROWT[:, 0:nw], op=ALU.subtract),
                    waits=[v1, e2], tag=f"{pfx}_var")
                e3 = P.op("s", lambda s: s.activation(out=OROWRS[:, 0:nw],
                    in_=OROWT[:, 0:nw], func=AF.Sqrt, bias=eps_sb),
                    waits=[v2], tag=f"{pfx}_sqrt")
                v3 = P.op("v", lambda v: v.reciprocal(OROWRS[:, 0:nw], OROWRS[:, 0:nw]),
                    waits=[e3], tag=f"{pfx}_rcp")
                v4 = P.op("v", lambda v: v.tensor_tensor(out=OROWMR[:, 0:nw],
                    in0=OROWM[:, 0:nw], in1=OROWRS[:, 0:nw], op=ALU.mult),
                    waits=[v3], tag=f"{pfx}_mr")
                v5 = P.op("v", lambda v: v.tensor_copy(OROWRSB[:, 0:nw], OROWRS[:, 0:nw]),
                    waits=[v3], tag=f"{pfx}_cb1")
                v6 = P.op("v", lambda v: v.tensor_copy(OROWMRB[:, 0:nw], OROWMR[:, 0:nw]),
                    waits=[v4], tag=f"{pfx}_cb2")
                b1 = P.op("t", lambda t: t.matmul(bank(sum_bank, nw), onecol,
                    OROWRSB[:, 0:nw], start=True, stop=True),
                    waits=[v5, e1], tag=f"{pfx}_b1")
                b2 = P.op("t", lambda t: t.matmul(bank(sq_bank, nw), onecol,
                    OROWMRB[:, 0:nw], start=True, stop=True),
                    waits=[v6, e2], tag=f"{pfx}_b2")
                eb1 = P.op("s", lambda s: s.activation(out=RSB[:, 0:nw],
                    in_=bank(sum_bank, nw), func=AF.Copy), waits=[b1], tag=f"{pfx}_eb1")
                eb2 = P.op("s", lambda s: s.activation(out=MRB_[:, 0:nw],
                    in_=bank(sq_bank, nw), func=AF.Copy), waits=[b2], tag=f"{pfx}_eb2")
                return eb1, eb2

            ln2_toks = {}
            for th in range(2):
                m1, m2 = ob_tok[("stat", th)]
                ln2_toks[th] = ln_rows(f"po_l2{th}", 4 + th, 6 + th,
                                       RSTB2[th], MRB2[th], [m1, m2])

            x2_toks = {}
            prl_tok = [None, None]
            xl_tok = [None, None]
            x2st_tok = [None, None]
            tmpo_tok = None
            x2b_tok = {}
            for dt in range(32):
                for th in range(2):
                    sl = (dt * 2 + th) % 2
                    prl_tok[sl] = P.dma("y", f"por{sl}", lambda y, dt=dt, th=th, sl=sl:
                        y.dma_start(out=PRL[sl][:],
                                    in_=proj_d[dt * 128:(dt + 1) * 128,
                                               th * 512:(th + 1) * 512]),
                        waits=[x2b_tok.get(("prl", sl)),
                               P.tok("pop0"), P.tok("pop1")],
                        tag=f"po_rl{dt}_{th}")
                    xl_tok[sl] = P.dma("y", f"pox{sl}", lambda y, sl=sl, dt=dt, th=th:
                        y.dma_start(out=XLF[sl][:],
                                    in_=xtl[dt * 128:(dt + 1) * 128,
                                            th * 512:(th + 1) * 512]),
                        waits=[x2b_tok.get(("xlf", sl))],
                        tag=f"po_xl{dt}_{th}")
                    t1 = P.op("v", lambda v, sl=sl, th=th: v.tensor_tensor(
                        out=TMPO[:], in0=PRL[sl][:], in1=RSTB2[th][:], op=ALU.mult),
                        waits=[prl_tok[sl], ln2_toks[th][0], tmpo_tok], tag=f"po_t1{dt}_{th}")
                    t2 = P.op("v", lambda v, th=th: v.tensor_tensor(
                        out=TMPO[:], in0=TMPO[:], in1=MRB2[th][:], op=ALU.subtract),
                        waits=[t1, ln2_toks[th][1]], tag=f"po_t2{dt}_{th}")
                    x2v = P.op("v", lambda v, sl=sl: v.tensor_tensor(
                        out=X2T[sl][:], in0=TMPO[:], in1=XLF[sl][:], op=ALU.add),
                        waits=[t2, xl_tok[sl], x2st_tok[sl],
                               x2b_tok.get(("xlf", sl))], tag=f"po_x2{dt}_{th}")
                    tmpo_tok = x2v
                    x2b_tok[("prl", sl)] = x2v
                    x2st_tok[sl] = P.dma("y", f"po2{sl}", lambda y, dt=dt, th=th, sl=sl:
                        y.dma_start(out=x2_d[dt * 128:(dt + 1) * 128,
                                             th * 512:(th + 1) * 512],
                                    in_=X2T[sl][:]),
                        waits=[x2v], tag=f"po_2s{dt}_{th}")
                    cb = P.op("s", lambda s, dt=dt, th=th, sl=sl: s.activation(
                        out=X2B[:, dt, th * 512:(th + 1) * 512], in_=X2T[sl][:],
                        func=AF.Copy), waits=[x2v], tag=f"po_cb{dt}_{th}")
                    x2b_tok[("xlf", sl)] = cb
                    sq = P.op("s", lambda s, dt=dt, th=th: s.activation(
                        out=PSQ[:], in_=X2B[:, dt, th * 512:(th + 1) * 512],
                        func=AF.Square), waits=[cb, x2b_tok.get("psq3")],
                        tag=f"po_q3{dt}_{th}")
                    m1 = P.op("t", lambda t, th=th, dt=dt: t.matmul(
                        bank(4 + th, rows=1), ones128,
                        X2B[:, dt, th * 512:(th + 1) * 512],
                        start=(dt == 0), stop=(dt == 31)),
                        waits=[cb, ln2_toks[th][0] if dt == 0 else None],
                        tag=f"po_31{dt}_{th}")
                    m2 = P.op("t", lambda t, th=th, dt=dt: t.matmul(
                        bank(6 + th, rows=1), ones128, PSQ[:],
                        start=(dt == 0), stop=(dt == 31)),
                        waits=[sq, ln2_toks[th][1] if dt == 0 else None],
                        tag=f"po_32{dt}_{th}")
                    x2b_tok["psq3"] = m2
                    x2b_tok[("stat", th)] = (m1, m2)
            ln3_toks = {}
            for th in range(2):
                m1, m2 = x2b_tok[("stat", th)]
                ln3_toks[th] = ln_rows(f"po_l3{th}", 4 + th, 6 + th,
                                       RSTB3[th], MRB3[th], [m1, m2])
            xn2_tok = None
            for dt in range(32):
                for th in range(2):
                    t1 = P.op("v", lambda v, dt=dt, th=th: v.tensor_tensor(
                        out=TMPO[:], in0=X2B[:, dt, th * 512:(th + 1) * 512],
                        in1=RSTB3[th][:], op=ALU.mult),
                        waits=[ln3_toks[th][0], tmpo_tok,
                               oproj_done if dt == 0 and th == 0 else None],
                        tag=f"po_n1{dt}_{th}")
                    xn2_tok = P.op("v", lambda v, dt=dt, th=th: v.tensor_tensor(
                        out=XN2[:, dt, th * 512:(th + 1) * 512], in0=TMPO[:],
                        in1=MRB3[th][:], op=ALU.subtract),
                        waits=[t1, ln3_toks[th][1]], tag=f"po_n2{dt}_{th}")
                    tmpo_tok = xn2_tok
            if "xn2dbg" in debug:
                P.dma("y", "pod1", lambda y: y.dma_start(
                    out=xn2dbg_d.rearrange("(i p) t -> p i t", p=128), in_=XN2[:]),
                    waits=[P.tok("v")], tag="po_xn2dump")
                P.dma("y", "pod2", lambda y: y.dma_start(
                    out=r3dbg_d[0].rearrange("(z p) c -> p z c", p=128) if False else
                        r3dbg_d[0], in_=RSTB3[0][:]), waits=[P.tok("v")], tag="po_r3a")
                P.dma("y", "pod2", lambda y: y.dma_start(
                    out=r3dbg_d[1], in_=RSTB3[1][:]), waits=[P.tok("v")], tag="po_r3b")
                P.dma("y", "pod2", lambda y: y.dma_start(
                    out=r3dbg_d[2], in_=MRB3[0][:]), waits=[P.tok("v")], tag="po_r3c")
                P.dma("y", "pod2", lambda y: y.dma_start(
                    out=r3dbg_d[3], in_=MRB3[1][:]), waits=[P.tok("v")], tag="po_r3d")

        if "PF1" in phases:
            P.start_phase()
            seg, cur = mkseg(67584)
            XN2 = A.view(2048, BF16, (32, 1024))
            if "PO" not in phases or "force_xn2t" in debug:
                P.dma("y", "f1x", lambda y: y.dma_start(
                    out=XN2[:], in_=xn2t.rearrange("(i p) t -> p i t", p=128)),
                    tag="f1_xn2t")
                P.start_phase()
            WCAR = [A.view(seg(16384), BF16, (32, 256)) for _ in range(3)]
            OH32 = A.view(seg(2048), BF16, (32, 32))
            ZPG = A.view(seg(22016), BF16, (11008,))
            ZPU = A.view(seg(22016), BF16, (11008,))
            CSX = A.view(seg(2048), BF16, (2, 512))
            GS = [A.view(seg(1024), BF16, (512,)) for _ in range(2)]
            INTO = [A.view(seg(2048), BF16, (2, 512)) for _ in range(2)]

            oh_tok = P.dma("y", "f1o", lambda y: y.dma_start(
                out=OH32[:], in_=oh32.rearrange("m p c -> p m c")), tag="f1_oh")
            zpg_tok = P.dma("y", "f1z", lambda y: y.dma_start(
                out=ZPG[0:32, :], in_=zpg_d), tag="f1_zg")
            zpu_tok = P.dma("y", "f1z", lambda y: y.dma_start(
                out=ZPU[0:32, :], in_=zpu_d), tag="f1_zu")
            fb_tok = {}
            cs_ev = {}
            for th in range(2):
                mm = None
                for i in range(32):
                    mm = P.op("t", lambda t, i=i, th=th: t.matmul(
                        bank(4 + th, rows=32), OH32[:, i, :],
                        XN2[:, i, th * 512:(th + 1) * 512],
                        start=(i == 0), stop=(i == 31)),
                        waits=[oh_tok] if i == 0 else [], tag=f"f1_cs{th}_{i}")
                cs_ev[th] = P.op("s", lambda s, th=th: s.activation(
                    out=CSX[0:32, th, :], in_=bank(4 + th, rows=32), func=AF.Copy),
                    waits=[mm], tag=f"f1_cse{th}")
            spans = [(i0, 2) for i0 in range(0, NT_I, 2)]
            slot_free = [None, None, None]
            slot_tok = {}
            gs_free = [None, None]
            into_free = [None, None]
            bank_f = {}
            sidx = 0
            for (i0, ni) in spans:
                gslot = sidx % 3
                uslot = (sidx + 1) % 3
                sidx += 2
                gtok = P.dma("g", f"f1w{gslot}", lambda y, i0=i0, ni=ni, gslot=gslot:
                    y.dma_start(out=WCAR[gslot][:, :, 0:ni * 128],
                                in_=wtg_d[:, :, i0 * 128:(i0 + ni) * 128]
                                .rearrange("a p c -> p a c")),
                    waits=[slot_free[gslot]], tag=f"f1_wg{i0}")
                utok = P.dma("g", f"f1w{uslot}", lambda y, i0=i0, ni=ni, uslot=uslot:
                    y.dma_start(out=WCAR[uslot][:, :, 0:ni * 128],
                                in_=wtu_d[:, :, i0 * 128:(i0 + ni) * 128]
                                .rearrange("a p c -> p a c")),
                    waits=[slot_free[uslot]], tag=f"f1_wu{i0}")
                for ii in range(ni):
                    par = ii % 2
                    gb = par * 4
                    ub = par * 4 + 2
                    for th in range(2):
                        mm = None
                        for kt in range(32):
                            mm = P.op("t", lambda t, kt=kt, ii=ii, th=th, gb=gb, gslot=gslot:
                                t.matmul(bank(gb + th), WCAR[gslot][:, kt, ii * 128:(ii + 1) * 128],
                                         XN2[:, kt, th * 512:(th + 1) * 512],
                                         start=(kt == 0), stop=False),
                                waits=([gtok, bank_f.get(gb + th)] if kt == 0 else []),
                                tag=f"f1_gm{i0}_{ii}_{th}_{kt}")
                        gmm = P.op("t", lambda t, ii=ii, th=th, gb=gb, i0=i0: t.matmul(
                            bank(gb + th), ZPG[0:32, (i0 + ii) * 128:(i0 + ii + 1) * 128],
                            CSX[0:32, th, :], start=False, stop=True),
                            waits=[zpg_tok, cs_ev[th]], tag=f"f1_gz{i0}_{ii}_{th}")
                        mm = None
                        for kt in range(32):
                            mm = P.op("t", lambda t, kt=kt, ii=ii, th=th, ub=ub, uslot=uslot:
                                t.matmul(bank(ub + th), WCAR[uslot][:, kt, ii * 128:(ii + 1) * 128],
                                         XN2[:, kt, th * 512:(th + 1) * 512],
                                         start=(kt == 0), stop=False),
                                waits=([utok, bank_f.get(ub + th)] if kt == 0 else []),
                                tag=f"f1_um{i0}_{ii}_{th}_{kt}")
                        umm = P.op("t", lambda t, ii=ii, th=th, ub=ub, i0=i0: t.matmul(
                            bank(ub + th), ZPU[0:32, (i0 + ii) * 128:(i0 + ii + 1) * 128],
                            CSX[0:32, th, :], start=False, stop=True),
                            waits=[zpu_tok], tag=f"f1_uz{i0}_{ii}_{th}")
                        sil = P.op("s", lambda s, th=th, gb=gb, par=par: s.activation(
                            out=GS[par][:], in_=bank(gb + th), func=AF.Silu),
                            waits=[gmm, gs_free[par]], tag=f"f1_si{i0}_{ii}_{th}")
                        bank_f[gb + th] = sil
                        mul = P.op("v", lambda v, th=th, ub=ub, par=par: v.tensor_tensor(
                            out=INTO[par][:, th, :], in0=bank(ub + th), in1=GS[par][:],
                            op=ALU.mult),
                            waits=[umm, sil, into_free[par]], tag=f"f1_ml{i0}_{ii}_{th}")
                        bank_f[ub + th] = mul
                        gs_free[par] = mul
                        if th == 1:
                            into_free[par] = P.dma("y", f"f1i{par}",
                                lambda y, i0=i0, ii=ii, par=par: y.dma_start(
                                    out=inter_d[(i0 + ii) * 128:(i0 + ii + 1) * 128, :]
                                        .rearrange("p (a b) -> p a b", a=2),
                                    in_=INTO[par][:]),
                                waits=[mul], tag=f"f1_is{i0}_{ii}")
                    slot_tok[gslot] = P.tok("t")
                    slot_tok[uslot] = P.tok("t")
                slot_free[gslot] = slot_tok[gslot]
                slot_free[uslot] = slot_tok[uslot]
            pf1_done = P.tok("t")

        if "PF2" in phases:
            P.start_phase()
            seg, cur = mkseg(67584)
            D16 = A.view(2048, F16, (32, 1024))
            OH86 = A.view(seg(14792), BF16, (86, 86))
            ZPD = A.view(seg(8192), BF16, (4096,))
            CSI = A.view(seg(2048), BF16, (2, 512))
            INT2 = [A.view(seg(2048), BF16, (2, 512)) for _ in range(2)]
            WDT = [A.view(seg(1024), BF16, (512,)) for _ in range(2)]
            NRES = 43
            INTR = A.view(seg(NRES * 2048), BF16, (NRES, 2, 512))

            oh2_tok = P.dma("y", "f2o", lambda y: y.dma_start(
                out=OH86[:], in_=oh86.rearrange("m p c -> p m c")), tag="f2_oh")
            zpd_tok = P.dma("y", "f2z", lambda y: y.dma_start(
                out=ZPD[0:86, :], in_=zpd_d), tag="f2_zd")
            int_tok = [None, None]
            int_free = [None, None]
            res_gate = [None, None]
            intr_tok = {}
            csi_ev = {}
            mm_ = {0: None, 1: None}
            for i in range(NT_I):
                sl = i % 2
                if i < NRES:
                    intr_tok[i] = P.dma("g", f"f2i{sl}", lambda y, i=i: y.dma_start(
                        out=INTR[:, i, :, :], in_=inter_d[i * 128:(i + 1) * 128, :]
                        .rearrange("p (a b) -> p a b", a=2)),
                        waits=[res_gate[sl]], tag=f"f2_il{i}")
                    rhs_ = lambda th, i=i: INTR[:, i, th, :]
                    ld_tok = intr_tok[i]
                else:
                    int_tok[sl] = P.dma("g", f"f2i{sl}", lambda y, i=i, sl=sl: y.dma_start(
                        out=INT2[sl][:], in_=inter_d[i * 128:(i + 1) * 128, :]
                        .rearrange("p (a b) -> p a b", a=2)),
                        waits=[int_free[sl]], tag=f"f2_il{i}")
                    rhs_ = lambda th, sl=sl: INT2[sl][:, th, :]
                    ld_tok = int_tok[sl]
                for th in range(2):
                    mm_[th] = P.op("t", lambda t, i=i, th=th, rhs_=rhs_: t.matmul(
                        bank(4 + th, rows=86), OH86[:, i, :], rhs_(th),
                        start=(i == 0), stop=(i == NT_I - 1)),
                        waits=[ld_tok] + ([oh2_tok] if i == 0 else []),
                        tag=f"f2_cs{i}_{th}")
                res_gate[sl] = mm_[1]
                if i >= NRES:
                    int_free[sl] = mm_[1]
            for th in range(2):
                csi_ev[th] = P.op("s", lambda s, th=th: s.activation(
                    out=CSI[0:86, th, :], in_=bank(4 + th, rows=86), func=AF.Copy),
                    waits=[mm_[th]], tag=f"f2_cse{th}")
            wd_tok = [None, None]
            wd_free = [None, None]
            bank_g = {}
            for hg in range(8):
                mm_last = {}
                for i in range(NT_I):
                    sl = i % 2
                    if i >= NRES:
                        int_tok[sl] = P.dma("g", f"f2j{sl}", lambda y, i=i, sl=sl: y.dma_start(
                            out=INT2[sl][:], in_=inter_d[i * 128:(i + 1) * 128, :]
                            .rearrange("p (a b) -> p a b", a=2)),
                            waits=[int_free[sl]], tag=f"f2_jl{hg}_{i}")
                    wsl = i % 2
                    wd_tok[wsl] = P.dma("y", f"f2w{wsl}", lambda y, hg=hg, i=i, wsl=wsl:
                        y.dma_start(out=WDT[wsl][:],
                                    in_=wtd_d[i, :, hg * 512:(hg + 1) * 512]),
                        waits=[wd_free[wsl]], tag=f"f2_wl{hg}_{i}")
                    if i < NRES:
                        rhs2 = lambda th, i=i: INTR[:, i, th, :]
                        ld2 = intr_tok[i]
                    else:
                        rhs2 = lambda th, sl=sl: INT2[sl][:, th, :]
                        ld2 = int_tok[sl]
                    for ht in range(4):
                        for th in range(2):
                            bq = ht * 2 + th
                            mm_last[bq] = P.op("t", lambda t, i=i, ht=ht, th=th, bq=bq,
                                               rhs2=rhs2, wsl=wsl: t.matmul(
                                bank(bq), WDT[wsl][:, ht * 128:(ht + 1) * 128],
                                rhs2(th), start=(i == 0), stop=False),
                                waits=[ld2, wd_tok[wsl],
                                       bank_g.get(bq) if i == 0 else None],
                                tag=f"f2_dm{hg}_{i}_{bq}")
                    if i >= NRES:
                        int_free[sl] = mm_last[7]
                    wd_free[wsl] = mm_last[7]
                for ht in range(4):
                    for th in range(2):
                        bq = ht * 2 + th
                        zm = P.op("t", lambda t, hg=hg, ht=ht, th=th, bq=bq: t.matmul(
                            bank(bq), ZPD[0:86, (hg * 4 + ht) * 128:(hg * 4 + ht + 1) * 128],
                            CSI[0:86, th, :], start=False, stop=True),
                            waits=[zpd_tok, csi_ev[th], mm_last[bq]],
                            tag=f"f2_dz{hg}_{bq}")
                        bank_g[bq] = P.op("s", lambda s, hg=hg, ht=ht, th=th, bq=bq:
                            s.activation(out=D16[:, hg * 4 + ht, th * 512:(th + 1) * 512],
                                         in_=bank(bq), func=AF.Copy),
                            waits=[zm, pf1_done if hg == 0 and ht == 0 and th == 0 else None],
                            tag=f"f2_de{hg}_{bq}")

        if "PL" in phases:
            P.start_phase()
            seg, cur = mkseg(132 * 1024)
            D16 = A.view(2048, F16, (32, 1024))
            DBF = A.view(seg(1024), BF16, (512,))
            DSQ = A.view(seg(1024), BF16, (512,))
            X2L = [A.view(seg(2048), F32, (512,)) for _ in range(4)]
            OUTT = [A.view(seg(2048), F32, (512,)) for _ in range(4)]
            TMPL = A.view(seg(2048), F32, (512,))
            RSTB4 = [A.view(seg(2048), F32, (512,)) for _ in range(2)]
            MRB4 = [A.view(seg(2048), F32, (512,)) for _ in range(2)]
            LROWM = A.view(seg(2048), F32, (512,))[0:1, :]
            LROWS2 = A.view(seg(2048), F32, (512,))[0:1, :]
            LROWT = A.view(seg(2048), F32, (512,))[0:1, :]
            LROWRS = A.view(seg(2048), F32, (512,))[0:1, :]
            LROWMR = A.view(seg(2048), F32, (512,))[0:1, :]
            LROWRSB = A.view(seg(1024), BF16, (512,))[0:1, :]
            LROWMRB = A.view(seg(1024), BF16, (512,))[0:1, :]

            lb_tok = {}
            for dt in range(32):
                for th in range(2):
                    cb = P.op("s", lambda s, dt=dt, th=th: s.activation(
                        out=DBF[:], in_=D16[:, dt, th * 512:(th + 1) * 512], func=AF.Copy),
                        waits=[lb_tok.get("dbf")], tag=f"pl_cb{dt}_{th}")
                    sq = P.op("s", lambda s: s.activation(
                        out=DSQ[:], in_=DBF[:], func=AF.Square),
                        waits=[cb, lb_tok.get("dsq")], tag=f"pl_sq{dt}_{th}")
                    m1 = P.op("t", lambda t, th=th, dt=dt: t.matmul(
                        bank(0 + th, rows=1), ones128, DBF[:],
                        start=(dt == 0), stop=(dt == 31)),
                        waits=[cb, lb_tok.get(("b", th)) if dt == 0 else None],
                        tag=f"pl_s1{dt}_{th}")
                    lb_tok["dbf"] = m1
                    m2 = P.op("t", lambda t, th=th, dt=dt: t.matmul(
                        bank(2 + th, rows=1), ones128, DSQ[:],
                        start=(dt == 0), stop=(dt == 31)),
                        waits=[sq, lb_tok.get(("c", th)) if dt == 0 else None],
                        tag=f"pl_s2{dt}_{th}")
                    lb_tok["dsq"] = m2
                    lb_tok[("stat", th)] = (m1, m2)

            def ln_rows4(pfx, sum_bank, sq_bank, RSB, MRB_, waits_extra):
                e1 = P.op("s", lambda s: s.activation(out=LROWM,
                    in_=bank(sum_bank, 512, rows=1), func=AF.Identity, scale=invS_sb),
                    waits=waits_extra, tag=f"{pfx}_m")
                e2 = P.op("s", lambda s: s.activation(out=LROWS2,
                    in_=bank(sq_bank, 512, rows=1), func=AF.Identity, scale=invS_sb),
                    waits=waits_extra, tag=f"{pfx}_s2")
                v1 = P.op("v", lambda v: v.tensor_tensor(out=LROWT, in0=LROWM, in1=LROWM,
                    op=ALU.mult), waits=[e1], tag=f"{pfx}_mm")
                v2 = P.op("v", lambda v: v.tensor_tensor(out=LROWT, in0=LROWS2, in1=LROWT,
                    op=ALU.subtract), waits=[v1, e2], tag=f"{pfx}_var")
                e3 = P.op("s", lambda s: s.activation(out=LROWRS, in_=LROWT, func=AF.Sqrt,
                    bias=eps_sb), waits=[v2], tag=f"{pfx}_sqrt")
                v3 = P.op("v", lambda v: v.reciprocal(LROWRS, LROWRS), waits=[e3],
                    tag=f"{pfx}_rcp")
                v4 = P.op("v", lambda v: v.tensor_tensor(out=LROWMR, in0=LROWM, in1=LROWRS,
                    op=ALU.mult), waits=[v3], tag=f"{pfx}_mr")
                v5 = P.op("v", lambda v: v.tensor_copy(LROWRSB, LROWRS), waits=[v3],
                    tag=f"{pfx}_cb1")
                v6 = P.op("v", lambda v: v.tensor_copy(LROWMRB, LROWMR), waits=[v4],
                    tag=f"{pfx}_cb2")
                b1 = P.op("t", lambda t: t.matmul(bank(sum_bank), onecol, LROWRSB,
                    start=True, stop=True), waits=[v5, e1], tag=f"{pfx}_b1")
                b2 = P.op("t", lambda t: t.matmul(bank(sq_bank), onecol, LROWMRB,
                    start=True, stop=True), waits=[v6, e2], tag=f"{pfx}_b2")
                eb1 = P.op("s", lambda s: s.activation(out=RSB[:], in_=bank(sum_bank),
                    func=AF.Copy), waits=[b1], tag=f"{pfx}_eb1")
                eb2 = P.op("s", lambda s: s.activation(out=MRB_[:], in_=bank(sq_bank),
                    func=AF.Copy), waits=[b2], tag=f"{pfx}_eb2")
                return eb1, eb2

            ln4_toks = {}
            for th in range(2):
                m1, m2 = lb_tok[("stat", th)]
                ln4_toks[th] = ln_rows4(f"pl_l4{th}", 0 + th, 2 + th,
                                        RSTB4[th], MRB4[th], [m1, m2])
            x2l_tok = [None] * 4
            outt_free = [None] * 4
            ost_tok = [None] * 4
            tmpl_tok = None
            for dt in range(32):
                for th in range(2):
                    sl = (dt * 2 + th) % 4
                    x2l_tok[sl] = P.dma("y", f"plx{sl}", lambda y, dt=dt, th=th, sl=sl:
                        y.dma_start(out=X2L[sl][:],
                                    in_=x2_d[dt * 128:(dt + 1) * 128,
                                             th * 512:(th + 1) * 512]),
                        waits=[outt_free[sl]], tag=f"pl_xl{dt}_{th}")
                    t1 = P.op("v", lambda v, dt=dt, th=th: v.tensor_tensor(
                        out=TMPL[:], in0=D16[:, dt, th * 512:(th + 1) * 512],
                        in1=RSTB4[th][:], op=ALU.mult),
                        waits=[ln4_toks[th][0], tmpl_tok], tag=f"pl_t1{dt}_{th}")
                    t2 = P.op("v", lambda v, th=th: v.tensor_tensor(
                        out=TMPL[:], in0=TMPL[:], in1=MRB4[th][:], op=ALU.subtract),
                        waits=[t1, ln4_toks[th][1]], tag=f"pl_t2{dt}_{th}")
                    ov = P.op("v", lambda v, sl=sl: v.tensor_tensor(
                        out=OUTT[sl][:], in0=TMPL[:], in1=X2L[sl][:], op=ALU.add),
                        waits=[t2, x2l_tok[sl], ost_tok[sl]], tag=f"pl_o{dt}_{th}")
                    tmpl_tok = ov
                    outt_free[sl] = ov
                    ost_tok[sl] = P.dma("y", f"pls{sl}", lambda y, dt=dt, th=th, sl=sl:
                        y.dma_start(out=out_d[dt * 128:(dt + 1) * 128,
                                              th * 512:(th + 1) * 512],
                                    in_=OUTT[sl][:]),
                        waits=[ov], tag=f"pl_os{dt}_{th}")

        fin = [P.tok(s) for s in list(P.cnt.keys())]
        P.op("y", lambda y: y.nop(), waits=fin, tag="final_fence")

        emit_all(nc, P)

    return nc, P


def emit_all(nc, P):
    semh = {}
    with ExitStack() as es:
        for name in sorted(P.sem_names):
            semh[name] = es.enter_context(nc.semaphore(name))
        with nc.Block() as block:
            @block.tensor
            def _(t):
                _emit_one(P, "t", t, semh)
            @block.vector
            def _(v):
                _emit_one(P, "v", v, semh)
            @block.scalar
            def _(s):
                _emit_one(P, "s", s, semh)
            @block.gpsimd
            def _(g):
                _emit_one(P, "g", g, semh)
            @block.sync
            def _(y):
                _emit_one(P, "y", y, semh)


def _emit_one(P, name, eng, semh):
    last = {}
    for entry in P.ops[name]:
        for (sem, val) in entry["waits"]:
            if last.get(sem, -1) >= val:
                continue
            eng.wait_ge(semh[sem], val)
            last[sem] = val
        ins = entry["fn"](eng)
        if ins is None:
            raise RuntimeError(f"op emitted nothing: {entry['tag']}")
        ins.then_inc(semh[entry["inc"]], entry["amt"])


def kernel(**inputs):
    percore = stage_inputs(inputs)
    nc, _plan = build_nc()
    res = run_bass_kernel_spmd(nc, percore, core_ids=list(range(8)))
    out = np.empty((2, 4096, 4096), np.float32)
    for c in range(8):
        b, g = c // 4, c % 4
        o = np.asarray(res.results[c]["out"])
        out[b, 512 * g:512 * (g + 1)] = o[:, 0:512].T
        out[b, 512 * (7 - g):512 * (8 - g)] = o[:, 512:1024].T
    return out


# version 04 (final)
# speedup vs baseline: 1.5436x; 1.0544x over previous
import numpy as np
import ml_dtypes
import concourse.bass as bass
import concourse.mybir as mybir
from concourse.bass_utils import run_bass_kernel_spmd
from contextlib import ExitStack

F32 = mybir.dt.float32
F16 = mybir.dt.float16
BF16 = mybir.dt.bfloat16
U8 = mybir.dt.uint8
I32 = mybir.dt.int32
AF = mybir.ActivationFunctionType
ALU = mybir.AluOpType

D = 4096
S = 4096
B = 2
NH, NKV, HD = 32, 8, 128
QO, KO, VO = 4096, 1024, 1024
INTER = 11008
GS = 128
NT_I = INTER // 128
NT_D = D // 128
TLOC = 1024
ROPE_BASE = 500000.0


def rope_tables():
    inv_freq = (1.0 / ROPE_BASE ** (np.arange(0, HD, 2, dtype=np.float32) / np.float32(HD))).astype(np.float32)
    t = np.arange(S, dtype=np.float32)
    freqs = (t[:, None] * inv_freq[None, :]).astype(np.float32)
    cos = np.cos(freqs.astype(np.float64)).astype(np.float32)
    sin = np.sin(freqs.astype(np.float64)).astype(np.float32)
    cosF = np.concatenate([cos.T, cos.T], axis=0)
    sinF = np.concatenate([sin.T, sin.T], axis=0)
    return np.ascontiguousarray(cosF), np.ascontiguousarray(sinF)

def qweight_bytesT(qw):
    KD8, N = qw.shape
    b = qw.view(np.uint8).reshape(KD8, N, 4)
    return np.ascontiguousarray(b.transpose(1, 0, 2).reshape(N, KD8 * 4))

def perm_sign_P64T():
    P = np.zeros((128, 128), np.float32)
    for i in range(64):
        P[i, i + 64] = -1.0
        P[i + 64, i] = 1.0
    return np.ascontiguousarray(P.T).astype(ml_dtypes.bfloat16)

def build_masks(g):
    out = np.zeros((32, 128, 512), np.float32)
    qlo = 512 * g
    qhi = 512 * (7 - g)
    kt = np.arange(128)
    qt = np.arange(512)
    for s_ in range(16):
        out[s_] = (128 * s_ + kt[:, None]) <= (qlo + qt[None, :])
    for j, s_ in enumerate(range(16, 32)):
        out[16 + j] = (128 * s_ + kt[:, None]) <= (qhi + qt[None, :])
    return np.ascontiguousarray(out).astype(ml_dtypes.bfloat16)

def onehot_lhsT(n):
    oh = np.zeros((n, 128, n), np.float32)
    for j in range(n):
        oh[j, :, j] = 1.0
    return oh.astype(ml_dtypes.bfloat16)

def stage_inputs(inputs):
    x = np.asarray(inputs["L0_attn_x"])
    w_qkv = np.asarray(inputs["L0_attn_w_qkv"])
    w_o = np.asarray(inputs["L0_attn_w_o"])
    st = {}
    st["xt0"] = np.ascontiguousarray(x[0].T)
    st["xt1"] = np.ascontiguousarray(x[1].T)
    st["wqkvT"] = np.ascontiguousarray(w_qkv.T)
    st["woT"] = np.ascontiguousarray(w_o.T)
    for nm in ("gate", "up", "down"):
        st[f"hb_{nm}"] = qweight_bytesT(np.asarray(inputs[f"L0_ffn_{nm}_qweight"]))
        sc = np.asarray(inputs[f"L0_ffn_{nm}_scales"])
        st[f"sT_{nm}"] = np.ascontiguousarray(sc.T)
        st[f"s_{nm}"] = sc
        st[f"z_{nm}"] = np.ascontiguousarray(
            np.asarray(inputs[f"L0_ffn_{nm}_zeros"]).view(np.uint8))
    cosF, sinF = rope_tables()
    st["cosK"] = cosF
    st["sinK"] = sinF
    scale = np.float32(1.0) / np.sqrt(np.float64(HD)).astype(np.float32)
    st["cosQ"] = (cosF.astype(np.float64) * float(scale)).astype(np.float32)
    st["sinQ"] = (sinF.astype(np.float64) * float(scale)).astype(np.float32)
    st["p64t"] = perm_sign_P64T()
    st["i128"] = np.eye(128, dtype=np.float32)
    st["i128b"] = np.eye(128, dtype=np.float32).astype(ml_dtypes.bfloat16)
    st["oh32"] = onehot_lhsT(32)
    st["oh86"] = onehot_lhsT(86)
    percore = []
    for c in range(8):
        b, g = c // 4, c % 4
        d = dict(st)
        d["xt"] = st["xt0"] if b == 0 else st["xt1"]
        del d["xt0"], d["xt1"]
        d["masks"] = build_masks(g)
        loc = np.r_[512 * g:512 * (g + 1), 512 * (7 - g):512 * (8 - g)]
        d["xtl"] = np.ascontiguousarray(d["xt"][:, loc])
        d["xn2t"] = np.zeros((4096, 1024), ml_dtypes.bfloat16)
        d["offs"] = np.array([512 * g, 512 * (7 - g)], dtype=np.int32)
        percore.append(d)
    return percore


class Plan:
    ENGINES = ("t", "v", "s", "g", "y")

    def __init__(self):
        self.ops = {e: [] for e in self.ENGINES}
        self.cnt = {}
        self.sem_names = set()
        self.phase_barrier = None
        self.phase_seen = set()

    def tok(self, sem):
        return (sem, self.cnt.get(sem, 0))

    def start_phase(self):
        self.phase_barrier = [(s, v) for s, v in self.cnt.items()]
        self.phase_seen = set()

    def op(self, eng, fn, waits=(), inc=None, amt=1, tag=""):
        waits = [w for w in waits if w is not None]
        if self.phase_barrier is not None and eng not in self.phase_seen:
            waits = list(waits) + self.phase_barrier
            self.phase_seen.add(eng)
        if inc is None:
            inc = eng
        self.sem_names.add(inc)
        for w in waits:
            self.sem_names.add(w[0])
        self.cnt[inc] = self.cnt.get(inc, 0) + amt
        self.ops[eng].append({"fn": fn, "waits": waits, "inc": inc,
                              "amt": amt, "tag": tag})
        return (inc, self.cnt[inc])

    def dma(self, eng, q, fn, waits=(), tag=""):
        return self.op(eng, fn, waits=waits, inc=q, amt=16, tag=tag)

    def barrier_token_all(self):
        return [self.tok(e) for e in ("t", "v", "s")]

    def emit(self, engines, semh):
        for name, eng in engines.items():
            last = {}
            for entry in self.ops[name]:
                for (sem, val) in entry["waits"]:
                    if last.get(sem, -1) >= val:
                        continue
                    eng.wait_ge(semh[sem], val)
                    last[sem] = val
                ins = entry["fn"](eng)
                if ins is not None:
                    ins.then_inc(semh[entry["inc"]], entry["amt"])
                else:
                    raise RuntimeError(f"op returned no instruction: {entry['tag']}")


class Arena:
    def __init__(self, handle, nbytes):
        self.h = handle
        self.nbytes = nbytes

    def view(self, off, dtype, shape):
        import math
        isz = {F32: 4, BF16: 2, U8: 1, I32: 4, F16: 2}[dtype]
        n = math.prod(shape)
        assert off % 64 == 0, off
        assert off + n * isz <= self.nbytes, (off, n * isz, self.nbytes)
        ap = self.h[:, off:off + n * isz]
        ap = ap.bitcast(dtype)
        if len(shape) > 1:
            dims = " ".join(f"d{i}" for i in range(len(shape)))
            kw = {f"d{i}": shape[i] for i in range(len(shape))}
            ap = ap.rearrange(f"p ({dims}) -> p {dims}", **kw)
        return ap


import numpy as np
from contextlib import ExitStack


LNC = 128
SUPS = [(0, 1536), (1536, 1536), (3072, 1024)]


def build_nc(phases=("PD", "PK", "PQ", "PA", "PO", "PF1", "PF2", "PL"), debug=()):
    nc = bass.Bass("TRN2", target_bir_lowering=False, debug=False)

    def din(name, shape, dt):
        return nc.dram_tensor(name, shape, dt, kind="ExternalInput").ap()

    xt = din("xt", [D, S], F32)
    xtl = din("xtl", [D, TLOC], F32)
    xn2t = din("xn2t", [D, TLOC], BF16)
    wqkvT = din("wqkvT", [D, 6144], F32)
    woT = din("woT", [D, D], F32)
    hb_g = din("hb_gate", [INTER, D // 2], U8)
    hb_u = din("hb_up", [INTER, D // 2], U8)
    hb_d = din("hb_down", [D, INTER // 2], U8)
    sT_g = din("sT_gate", [INTER, 32], F16)
    sT_u = din("sT_up", [INTER, 32], F16)
    sT_d = din("sT_down", [D, 86], F16)
    s_g = din("s_gate", [32, INTER], F16)
    s_u = din("s_up", [32, INTER], F16)
    s_d = din("s_down", [86, D], F16)
    z_g = din("z_gate", [32, INTER // 2], U8)
    z_u = din("z_up", [32, INTER // 2], U8)
    z_d = din("z_down", [86, D // 2], U8)
    cosK = din("cosK", [128, S], F32)
    sinK = din("sinK", [128, S], F32)
    cosQ = din("cosQ", [128, S], F32)
    sinQ = din("sinQ", [128, S], F32)
    p64t = din("p64t", [128, 128], BF16)
    i128 = din("i128", [128, 128], F32)
    i128b = din("i128b", [128, 128], BF16)
    oh32 = din("oh32", [32, 128, 32], BF16)
    oh86 = din("oh86", [86, 128, 86], BF16)
    masks = din("masks", [32, 128, 512], BF16)
    offs = din("offs", [1, 2], I32)

    out_d = nc.dram_tensor("out", [D, TLOC], F32, kind="ExternalOutput").ap()

    def scratch(name, shape, dt):
        kind = "ExternalOutput" if name in debug else "Internal"
        return nc.dram_tensor(name, shape, dt, kind=kind).ap()

    xn_d = scratch("xn_d", [D, S], BF16)
    k_d = scratch("k_d", [1024, S], BF16)
    v_d = scratch("v_d", [S, 1024], BF16)
    x2_d = scratch("x2_d", [D, TLOC], F32)
    proj_d = scratch("proj_d", [D, TLOC], BF16)
    attn_d = scratch("attn_d", [D, TLOC], BF16)
    inter_d = scratch("inter_d", [INTER, TLOC], BF16)
    wtg_d = scratch("wtg_d", [32, 128, INTER], BF16)
    wtu_d = scratch("wtu_d", [32, 128, INTER], BF16)
    wtd_d = scratch("wtd_d", [86, 128, D], BF16)
    zpg_d = scratch("zpg_d", [32, INTER], BF16)
    zpu_d = scratch("zpu_d", [32, INTER], BF16)
    zpd_d = scratch("zpd_d", [86, D], BF16)
    q_d = scratch("q_d", [4096, TLOC], BF16)
    mdbg_d = scratch("mdbg", [16, 128, 512], BF16)
    wcdbg_d = scratch("wcdbg", [128, 8 * 384], BF16)
    kvdbg_d = scratch("kvdbg", [2, 128, 4096], BF16)
    xn2dbg_d = scratch("xn2dbg", [4096, 1024], BF16)
    r3dbg_d = scratch("r3dbg", [4, 128, 512], F32)

    P = Plan()
    AB = 200 * 1024

    stk = ExitStack()
    with stk:
        arena_t = stk.enter_context(nc.sbuf_tensor("arena", [128, AB], U8))
        A = Arena(arena_t, AB)
        psA = stk.enter_context(nc.psum_tensor("psA", [128, 4096], F32))

        def bank(i, cols=512, rows=128, row0=0):
            return psA[row0:row0 + rows, 512 * i:512 * i + cols]

        ones128 = A.view(0, BF16, (64,))[:, 0:1]
        onecol_t = A.view(128, BF16, (128,))
        onecol = onecol_t[0:1, :]
        p64_sb = A.view(384, BF16, (128,))
        i128_sb = A.view(640, F32, (128,))
        eps_sb = A.view(1152, F32, (16,))[0:1, 0:1]
        invS_sb = A.view(1216, F32, (16,))[0:1, 0:1]
        offs_sb = A.view(1280, I32, (16,))[0:1, 0:2]
        i128b_sb = A.view(1408, BF16, (128,))
        CONST_END = 2048

        P.op("g", lambda g: g.memset(ones128, 1.0), tag="c0")
        P.op("g", lambda g: g.memset(onecol_t[0:1, :], 1.0), tag="c1")
        P.op("g", lambda g: g.memset(eps_sb, 1e-5), tag="c2")
        P.op("g", lambda g: g.memset(invS_sb, 1.0 / 4096.0), tag="c3")
        P.dma("g", "gd0", lambda g: g.dma_start(out=p64_sb, in_=p64t), tag="c4")
        P.dma("g", "gd0", lambda g: g.dma_start(out=i128_sb, in_=i128), tag="c5")
        P.dma("g", "gd0", lambda g: g.dma_start(out=i128b_sb, in_=i128b), tag="c5b")
        P.dma("y", "yd0", lambda y: y.dma_start(out=offs_sb, in_=offs), tag="c6")

        def mkseg(base):
            cur = [base]
            def seg(nbytes):
                off = cur[0]
                cur[0] += (nbytes + 63) & ~63
                assert cur[0] <= AB, (off, nbytes, cur[0])
                return off
            return seg, cur

        if "PD" in phases:
            seg, cur = mkseg(CONST_END)
            HB = [A.view(seg(5504), U8, (5504,)) for _ in range(2)]
            NIB = [A.view(seg(11008), U8, (11008,)) for _ in range(2)]
            WBF = [A.view(seg(22016), BF16, (11008,)) for _ in range(2)]
            STS16 = A.view(seg(5504), F16, (2752,))
            STS = A.view(seg(11008), F32, (2752,))
            ZB = A.view(seg(5504), U8, (5504,))
            ZNIB = A.view(seg(11008), U8, (11008,))
            ZT = A.view(seg(22016), F32, (5504,))
            SN16 = A.view(seg(11008), F16, (5504,))
            SN = A.view(seg(22016), F32, (5504,))
            ZPB = A.view(seg(11008), BF16, (5504,))
            TEV = [A.view(seg(2048), BF16, (1024,)) for _ in range(4)]
            PSB = [psA[:, 512 * k:512 * (k + 1)].bitcast(BF16) for k in (4, 5, 6, 7)]

            hb_tok = [None, None]
            nib_tok = [None, None]
            wbf_tok = [None, None]
            tev_free = [None, None, None, None]
            psb_free = [None, None, None, None]
            last_sc = None
            st_cast = None
            for mi, (hb, sTp, wt, N, K) in enumerate([
                    (hb_g, sT_g, wtg_d, INTER, D),
                    (hb_u, sT_u, wtu_d, INTER, D),
                    (hb_d, sT_d, wtd_d, D, INTER)]):
                NT, G = N // 128, K // 128
                st_ld = P.dma("g", "pdt", lambda g, sTp=sTp, NT=NT, G=G: g.dma_start(
                    out=STS16[:, 0:NT * G].rearrange("p (n g) -> p n g", n=NT),
                    in_=sTp.rearrange("(n p) g -> p n g", p=128)),
                    waits=[st_cast], tag=f"pd_sT{mi}")
                st_cast = P.op("v", lambda v, NT=NT, G=G: v.tensor_copy(
                    STS[:, 0:NT * G], STS16[:, 0:NT * G]),
                    waits=[st_ld, last_sc], tag=f"pd_sTc{mi}")
                for nt in range(NT):
                    bi = nt % 2
                    hb_tok[bi] = P.dma("y", f"pdh{bi}",
                        lambda y, hb=hb, nt=nt, K=K, bi=bi: y.dma_start(
                            out=HB[bi][:, 0:K // 2], in_=hb[nt * 128:(nt + 1) * 128, :]),
                        waits=[nib_tok[bi]], tag=f"pd_hb{mi}_{nt}")
                    t1 = P.op("v", lambda v, bi=bi, K=K: v.tensor_scalar(
                        out=NIB[bi][:, 0:K:2], in0=HB[bi][:, 0:K // 2], scalar1=15,
                        scalar2=None, op0=ALU.bitwise_and),
                        waits=[hb_tok[bi], wbf_tok[bi]], tag=f"pd_lo{mi}_{nt}")
                    t2 = P.op("v", lambda v, bi=bi, K=K: v.tensor_scalar(
                        out=NIB[bi][:, 1:K:2], in0=HB[bi][:, 0:K // 2], scalar1=4,
                        scalar2=None, op0=ALU.logical_shift_right),
                        waits=[hb_tok[bi], wbf_tok[bi]], tag=f"pd_hi{mi}_{nt}")
                    nib_tok[bi] = t2
                    last_sc = P.op("v", lambda v, bi=bi, nt=nt, G=G, K=K, NT=NT:
                        v.tensor_tensor(
                            out=WBF[bi][:, 0:K].rearrange("p (g k) -> p g k", g=G),
                            in0=NIB[bi][:, 0:K].rearrange("p (g k) -> p g k", g=G),
                            in1=STS[:, 0:NT * G].rearrange("p (n g) -> p n g", n=NT)
                                [:, nt, :][:, :, None].broadcast_to([128, G, 128]),
                            op=ALU.mult),
                        waits=[t1, t2, st_cast], tag=f"pd_sc{mi}_{nt}")
                    ntr = K // 128
                    trt = None
                    for b in range((ntr + 7) // 8):
                        nk = min(8, ntr - b * 8)
                        sl4 = b % 4
                        for j in range(nk):
                            kt = b * 8 + j
                            trt = P.op("t", lambda t, bi=bi, kt=kt, j=j, sl4=sl4:
                                t.transpose(
                                    PSB[sl4][:, j * 128:(j + 1) * 128],
                                    WBF[bi][:, kt * 128:(kt + 1) * 128], i128b_sb),
                                waits=([last_sc, psb_free[sl4]] if j == 0 else []),
                                tag=f"pd_tr{mi}_{nt}_{kt}")
                        ev = P.op("s", lambda s, sl4=sl4, nk=nk: s.activation(
                            out=TEV[sl4][:, 0:nk * 128], in_=PSB[sl4][:, 0:nk * 128],
                            func=AF.Copy),
                            waits=[trt, tev_free[sl4]], tag=f"pd_te{mi}_{nt}_{b}")
                        psb_free[sl4] = ev
                        tev_free[sl4] = P.dma("s", f"pdw{sl4}",
                            lambda y, wt=wt, nt=nt, b=b, nk=nk, sl4=sl4: y.dma_start(
                                out=wt[b * 8:b * 8 + nk, :, nt * 128:(nt + 1) * 128]
                                    .rearrange("a p c -> p a c"),
                                in_=TEV[sl4][:, 0:nk * 128]
                                    .rearrange("p (a c) -> p a c", a=nk)),
                            waits=[ev], tag=f"pd_wst{mi}_{nt}_{b}")
                    wbf_tok[bi] = trt

            zb_tok = znib_tok = zt_tok = zpb_tok = None
            sn_cast = None
            for mi, (zby, sn, zp_o, N, G) in enumerate([
                    (z_g, s_g, zpg_d, INTER, 32),
                    (z_u, s_u, zpu_d, INTER, 32),
                    (z_d, s_d, zpd_d, D, 86)]):
                zb_tok = P.dma("y", "pdz", lambda y, zby=zby, G=G, N=N: y.dma_start(
                    out=ZB[0:G, 0:N // 2], in_=zby),
                    waits=[znib_tok], tag=f"pd_zb{mi}")
                e1 = P.op("v", lambda v, G=G, N=N: v.tensor_scalar(
                    out=ZNIB[0:G, 0:N:2], in0=ZB[0:G, 0:N // 2], scalar1=15,
                    scalar2=None, op0=ALU.bitwise_and),
                    waits=[zb_tok, zt_tok], tag=f"pd_zlo{mi}")
                e2 = P.op("v", lambda v, G=G, N=N: v.tensor_scalar(
                    out=ZNIB[0:G, 1:N:2], in0=ZB[0:G, 0:N // 2], scalar1=4,
                    scalar2=None, op0=ALU.logical_shift_right),
                    waits=[zb_tok, zt_tok], tag=f"pd_zhi{mi}")
                znib_tok = e2
                for h in range(2):
                    NH_ = N // 2
                    sn_ld = P.dma("g", "pdn", lambda g, sn=sn, G=G, NH_=NH_, h=h:
                        g.dma_start(out=SN16[0:G, 0:NH_], in_=sn[:, h * NH_:(h + 1) * NH_]),
                        waits=[sn_cast], tag=f"pd_sn{mi}_{h}")
                    sn_cast = P.op("v", lambda v, G=G, NH_=NH_: v.tensor_copy(
                        SN[0:G, 0:NH_], SN16[0:G, 0:NH_]),
                        waits=[sn_ld, zpb_tok], tag=f"pd_snc{mi}_{h}")
                    zt_tok = P.op("v", lambda v, G=G, NH_=NH_, h=h: v.tensor_scalar(
                        out=ZT[0:G, 0:NH_], in0=ZNIB[0:G, h * NH_:(h + 1) * NH_],
                        scalar1=-1.0, scalar2=-1.0, op0=ALU.mult, op1=ALU.add),
                        waits=[e1, e2, zpb_tok], tag=f"pd_zt{mi}_{h}")
                    zpv = P.op("v", lambda v, G=G, NH_=NH_: v.tensor_tensor(
                        out=ZPB[0:G, 0:NH_], in0=ZT[0:G, 0:NH_], in1=SN[0:G, 0:NH_],
                        op=ALU.mult),
                        waits=[zt_tok, sn_cast], tag=f"pd_zp{mi}_{h}")
                    zpb_tok = P.dma("y", "pdp", lambda y, zp_o=zp_o, G=G, NH_=NH_, h=h:
                        y.dma_start(out=zp_o[:, h * NH_:(h + 1) * NH_], in_=ZPB[0:G, 0:NH_]),
                        waits=[zpv], tag=f"pd_zpst{mi}_{h}")

        if "PK" in phases:
            P.start_phase()
            seg, cur = mkseg(CONST_END)
            XN = A.view(seg(98304), BF16, (32, 1536))
            XF = [A.view(seg(16384), F32, (32, LNC)) for _ in range(2)]
            WKVF = A.view(seg(16384), F32, (32, 128))
            WKV = [A.view(seg(8192), BF16, (32, 128)) for _ in range(2)]
            COSS = A.view(seg(6144), F32, (3, 512))
            SINS = A.view(seg(6144), F32, (3, 512))
            KSB = A.view(seg(1024), BF16, (512,))
            TMP1 = A.view(seg(2048), F32, (512,))
            TMP2 = A.view(seg(2048), F32, (512,))
            KR = A.view(seg(1024), BF16, (512,))
            VSB = A.view(seg(2048), F32, (512,))
            VT = A.view(seg(1024), BF16, (512,))
            RSTB = A.view(seg(512), F32, (LNC,))
            MRB = A.view(seg(512), F32, (LNC,))
            XSQ = A.view(seg(8192), BF16, (32, LNC))
            ROWM = A.view(seg(512), F32, (LNC,))[0:1, :]
            ROWS2 = A.view(seg(512), F32, (LNC,))[0:1, :]
            ROWT = A.view(seg(512), F32, (LNC,))[0:1, :]
            ROWRS = A.view(seg(512), F32, (LNC,))[0:1, :]
            ROWMR = A.view(seg(512), F32, (LNC,))[0:1, :]
            ROWRSB = A.view(seg(256), BF16, (LNC,))[0:1, :]
            ROWMRB = A.view(seg(256), BF16, (LNC,))[0:1, :]

            bank_tok = {}
            wkv_tok = [None, None]
            wcast_tok = [None, None]
            kv_w_free = {}
            sup_end_tok = []
            xf_free = [None, None]
            cs_use = []
            for sup, (s0, SW) in enumerate(SUPS):
                n_tt = SW // 512
                cs1 = P.dma("y", "pkc", lambda y, s0=s0, SW=SW, n_tt=n_tt: y.dma_start(
                    out=COSS[:, 0:n_tt, :], in_=cosK[:, s0:s0 + SW]
                    .rearrange("p (a b) -> p a b", b=512)),
                    waits=cs_use, tag=f"pk_cos{sup}")
                cs2 = P.dma("y", "pkc", lambda y, s0=s0, SW=SW, n_tt=n_tt: y.dma_start(
                    out=SINS[:, 0:n_tt, :], in_=sinK[:, s0:s0 + SW]
                    .rearrange("p (a b) -> p a b", b=512)),
                    waits=cs_use, tag=f"pk_sin{sup}")

                n_ch = SW // LNC
                m2_carry = [None, None]
                apply_last = {}
                for ch in range(n_ch):
                    c0 = ch * LNC
                    bi = ch % 2
                    xld = P.dma("g", f"pkx{bi}", lambda g, s0=s0, c0=c0, bi=bi: g.dma_start(
                        out=XF[bi][:],
                        in_=xt[:, s0 + c0:s0 + c0 + LNC].rearrange("(i p) t -> p i t", p=128)),
                        waits=[xf_free[bi]], tag=f"pk_x{sup}_{ch}")
                    cast = P.op("s", lambda s, bi=bi, c0=c0: s.activation(
                        out=XN[:, :, c0:c0 + LNC], in_=XF[bi][:], func=AF.Copy),
                        waits=[xld] + sup_end_tok +
                              ([apply_last[ch - 2]] if ch >= 2 else []),
                        tag=f"pk_cast{sup}_{ch}")
                    sq = P.op("s", lambda s, c0=c0: s.activation(
                        out=XSQ[:], in_=XN[:, :, c0:c0 + LNC], func=AF.Square),
                        waits=[cast, m2_carry[bi]], tag=f"pk_sq{sup}_{ch}")
                    mm_tok = None
                    for i in range(32):
                        P.op("t", lambda t, i=i, c0=c0: t.matmul(
                            bank(4, LNC, rows=1), ones128, XN[:, i, c0:c0 + LNC],
                            start=(i == 0), stop=(i == 31)),
                            waits=[cast, bank_tok.get(4) if i == 0 else None],
                            tag=f"pk_s1_{sup}_{ch}_{i}")
                        mm_tok = P.op("t", lambda t, i=i: t.matmul(
                            bank(5, LNC, rows=1), ones128, XSQ[:, i, :],
                            start=(i == 0), stop=(i == 31)),
                            waits=[sq, bank_tok.get(5) if i == 0 else None],
                            tag=f"pk_s2_{sup}_{ch}_{i}")
                    m2_carry[bi] = mm_tok
                    mm1_tok = P.tok("t")
                    e1 = P.op("s", lambda s: s.activation(
                        out=ROWM, in_=bank(4, LNC, rows=1), func=AF.Identity, scale=invS_sb),
                        waits=[mm_tok], tag=f"pk_m{sup}_{ch}")
                    e2 = P.op("s", lambda s: s.activation(
                        out=ROWS2, in_=bank(5, LNC, rows=1), func=AF.Identity, scale=invS_sb),
                        waits=[mm_tok], tag=f"pk_s2r{sup}_{ch}")
                    bank_tok[4] = e1
                    bank_tok[5] = e2
                    v1 = P.op("v", lambda v: v.tensor_tensor(out=ROWT, in0=ROWM, in1=ROWM,
                        op=ALU.mult), waits=[e1], tag=f"pk_mm{sup}_{ch}")
                    v2 = P.op("v", lambda v: v.tensor_tensor(out=ROWT, in0=ROWS2, in1=ROWT,
                        op=ALU.subtract), waits=[v1, e2], tag=f"pk_var{sup}_{ch}")
                    e3 = P.op("s", lambda s: s.activation(out=ROWRS, in_=ROWT, func=AF.Sqrt,
                        bias=eps_sb), waits=[v2], tag=f"pk_sqrt{sup}_{ch}")
                    v3 = P.op("v", lambda v: v.reciprocal(ROWRS, ROWRS), waits=[e3],
                        tag=f"pk_rcp{sup}_{ch}")
                    v4 = P.op("v", lambda v: v.tensor_tensor(out=ROWMR, in0=ROWM, in1=ROWRS,
                        op=ALU.mult), waits=[v3], tag=f"pk_mr{sup}_{ch}")
                    v5 = P.op("v", lambda v: v.tensor_copy(ROWRSB, ROWRS), waits=[v3],
                        tag=f"pk_cb1{sup}_{ch}")
                    v6 = P.op("v", lambda v: v.tensor_copy(ROWMRB, ROWMR), waits=[v4],
                        tag=f"pk_cb2{sup}_{ch}")
                    b1 = P.op("t", lambda t: t.matmul(bank(4, LNC), onecol, ROWRSB,
                        start=True, stop=True), waits=[v5, bank_tok.get(4)], tag=f"pk_b1{sup}_{ch}")
                    b2 = P.op("t", lambda t: t.matmul(bank(5, LNC), onecol, ROWMRB,
                        start=True, stop=True), waits=[v6, bank_tok.get(5)], tag=f"pk_b2{sup}_{ch}")
                    eb1 = P.op("s", lambda s: s.activation(out=RSTB, in_=bank(4, LNC),
                        func=AF.Copy), waits=[b1], tag=f"pk_eb1{sup}_{ch}")
                    eb2 = P.op("s", lambda s: s.activation(out=MRB, in_=bank(5, LNC),
                        func=AF.Copy), waits=[b2], tag=f"pk_eb2{sup}_{ch}")
                    bank_tok[4] = eb1
                    bank_tok[5] = eb2
                    a1 = P.op("v", lambda v, bi=bi, c0=c0: v.tensor_tensor(
                        out=XF[bi][:], in0=XN[:, :, c0:c0 + LNC],
                        in1=RSTB[:, None, :].broadcast_to([128, 32, LNC]), op=ALU.mult),
                        waits=[eb1, mm1_tok, sq], tag=f"pk_a1{sup}_{ch}")
                    a2 = P.op("v", lambda v, bi=bi, c0=c0: v.tensor_tensor(
                        out=XN[:, :, c0:c0 + LNC], in0=XF[bi][:],
                        in1=MRB[:, None, :].broadcast_to([128, 32, LNC]), op=ALU.subtract),
                        waits=[a1, eb2], tag=f"pk_a2{sup}_{ch}")
                    apply_last[ch] = a2
                    xf_free[bi] = a2
                    P.dma("y", "pkn", lambda y, s0=s0, c0=c0: y.dma_start(
                        out=xn_d[:, s0 + c0:s0 + c0 + LNC].rearrange("(i p) t -> p i t", p=128),
                        in_=XN[:, :, c0:c0 + LNC]),
                        waits=[a2], tag=f"pk_xnst{sup}_{ch}")
                apply_toks = [apply_last[n_ch - 1], apply_last[n_ch - 2]]

                for ot in range(16):
                    bi = ot % 2
                    wkvf_tok = P.dma("g", "pkwf", lambda y, ot=ot: y.dma_start(
                        out=WKVF[:],
                        in_=wqkvT[:, 4096 + ot * 128:4096 + (ot + 1) * 128]
                            .rearrange("(i p) c -> p i c", p=128)),
                        waits=[wcast_tok[1 - bi]], tag=f"pk_wf{sup}_{ot}")
                    wcast_tok[bi] = P.op("s", lambda s, bi=bi: s.activation(
                        out=WKV[bi][:], in_=WKVF[:], func=AF.Copy),
                        waits=[wkvf_tok, kv_w_free.get(bi)], tag=f"pk_wc{sup}_{ot}")
                    wkv_tok[bi] = wcast_tok[bi]
                    for tt in range(n_tt):
                        bq = (ot * n_tt + tt) % 4
                        mm = None
                        for i in range(32):
                            mm = P.op("t", lambda t, bi=bi, i=i, tt=tt, bq=bq: t.matmul(
                                bank(bq), WKV[bi][:, i, :], XN[:, i, tt * 512:(tt + 1) * 512],
                                start=(i == 0), stop=(i == 31)),
                                waits=([wkv_tok[bi], bank_tok.get(bq)] +
                                       (apply_toks if i == 0 else [])),
                                tag=f"pk_mm{sup}_{ot}_{tt}_{i}")
                        kv_w_free[bi] = mm
                        if ot < 8:
                            ev = P.op("s", lambda s, bq=bq: s.activation(
                                out=KSB, in_=bank(bq), func=AF.Copy),
                                waits=[mm, bank_tok.get("ksb")], tag=f"pk_kev{sup}_{ot}_{tt}")
                            bank_tok[bq] = ev
                            pm = P.op("t", lambda t: t.matmul(bank(6), p64_sb, KSB,
                                start=True, stop=True),
                                waits=[ev, bank_tok.get(6)], tag=f"pk_pm{sup}_{ot}_{tt}")
                            r1 = P.op("v", lambda v, tt=tt: v.tensor_tensor(
                                out=TMP1, in0=KSB, in1=COSS[:, tt, :], op=ALU.mult),
                                waits=[ev, cs1], tag=f"pk_r1{sup}_{ot}_{tt}")
                            r2 = P.op("v", lambda v, tt=tt: v.tensor_tensor(
                                out=TMP2, in0=bank(6), in1=SINS[:, tt, :], op=ALU.mult),
                                waits=[pm, cs2], tag=f"pk_r2{sup}_{ot}_{tt}")
                            bank_tok[6] = r2
                            r3 = P.op("v", lambda v: v.tensor_tensor(
                                out=KR, in0=TMP1, in1=TMP2, op=ALU.add),
                                waits=[r1, r2, bank_tok.get("kr")], tag=f"pk_r3{sup}_{ot}_{tt}")
                            bank_tok["ksb"] = r3
                            kst = P.dma("y", "pkk", lambda y, ot=ot, s0=s0, tt=tt: y.dma_start(
                                out=k_d[ot * 128:(ot + 1) * 128, s0 + tt * 512:s0 + (tt + 1) * 512],
                                in_=KR), waits=[r3], tag=f"pk_kst{sup}_{ot}_{tt}")
                            bank_tok["kr"] = kst
                        else:
                            ev = P.op("s", lambda s, bq=bq: s.activation(
                                out=VSB, in_=bank(bq), func=AF.Copy),
                                waits=[mm, bank_tok.get("vsb")], tag=f"pk_vev{sup}_{ot}_{tt}")
                            bank_tok[bq] = ev
                            trt = None
                            for j in range(4):
                                trt = P.op("t", lambda t, j=j: t.transpose(
                                    bank(7)[:, j * 128:(j + 1) * 128],
                                    VSB[:, j * 128:(j + 1) * 128], i128_sb),
                                    waits=[ev, bank_tok.get(7) if j == 0 else None],
                                    tag=f"pk_tr{sup}_{ot}_{tt}_{j}")
                            vt = P.op("s", lambda s: s.activation(
                                out=VT, in_=bank(7), func=AF.Copy),
                                waits=[trt, bank_tok.get("vt")], tag=f"pk_vt{sup}_{ot}_{tt}")
                            bank_tok[7] = vt
                            bank_tok["vsb"] = vt
                            vst = P.dma("y", "pkv", lambda y, ot=ot, s0=s0, tt=tt: y.dma_start(
                                out=v_d[s0 + tt * 512:s0 + (tt + 1) * 512,
                                        (ot - 8) * 128:(ot - 7) * 128]
                                    .rearrange("(j p) c -> p j c", p=128),
                                in_=VT[:].rearrange("p (j c) -> p j c", j=4)),
                                waits=[vt], tag=f"pk_vst{sup}_{ot}_{tt}")
                            bank_tok["vt"] = vst
                cs_use = [P.tok("v")]
                sup_end_tok = [P.tok("t"), P.tok("pkn")]

        QRES_OFF = AB - 65536
        if "PQ" in phases:
            P.start_phase()
            seg, cur = mkseg(CONST_END)
            QRES = A.view(QRES_OFF, BF16, (32, 1024))
            XNL = A.view(seg(65536), BF16, (32, 1024))
            WQF = A.view(seg(16384), F32, (32, 128))
            WQ = [A.view(seg(8192), BF16, (32, 128)) for _ in range(2)]
            CQ = A.view(seg(4096), F32, (2, 512))
            SQ_ = A.view(seg(4096), F32, (2, 512))
            KSB2 = A.view(seg(1024), BF16, (512,))
            TMQ1 = A.view(seg(2048), F32, (512,))
            TMQ2 = A.view(seg(2048), F32, (512,))

            rlo = nc.sync.alloc_register("rlo")
            rhi = nc.sync.alloc_register("rhi")
            rg1 = P.op("y", lambda y: y.reg_load(rlo, offs_sb[0:1, 0:1]), tag="pq_rl")
            rg2 = P.op("y", lambda y: y.reg_load(rhi, offs_sb[0:1, 1:2]), tag="pq_rh")
            xnl_toks = []
            for hf, reg, tok in ((0, rlo, rg1), (1, rhi, rg2)):
                xnl_toks.append(P.dma("y", "pql", lambda y, hf=hf, reg=reg: y.dma_start(
                    out=XNL[:, :, hf * 512:(hf + 1) * 512],
                    in_=bass.AP(xn_d.tensor, reg, [[S, 128], [128 * S, 32], [1, 512]])),
                    waits=[tok], tag=f"pq_xnl{hf}"))
                xnl_toks.append(P.dma("y", "pql", lambda y, hf=hf, reg=reg: y.dma_start(
                    out=CQ[:, hf, :],
                    in_=bass.AP(cosQ.tensor, reg, [[S, 128], [1, 512]])),
                    waits=[tok], tag=f"pq_cq{hf}"))
                xnl_toks.append(P.dma("y", "pql", lambda y, hf=hf, reg=reg: y.dma_start(
                    out=SQ_[:, hf, :],
                    in_=bass.AP(sinQ.tensor, reg, [[S, 128], [1, 512]])),
                    waits=[tok], tag=f"pq_sq{hf}"))
            wq_tok = [None, None]
            qcast_tok = [None, None]
            qb_tok = {}
            q_w_free = {}
            for ot in range(32):
                bi = ot % 2
                wqf_tok = P.dma("g", "pqwf", lambda y, ot=ot: y.dma_start(
                    out=WQF[:],
                    in_=wqkvT[:, ot * 128:(ot + 1) * 128].rearrange("(i p) c -> p i c", p=128)),
                    waits=[qcast_tok[1 - bi]], tag=f"pq_wf{ot}")
                qcast_tok[bi] = P.op("s", lambda s, bi=bi: s.activation(
                    out=WQ[bi][:], in_=WQF[:], func=AF.Copy),
                    waits=[wqf_tok, q_w_free.get(bi)], tag=f"pq_wc{ot}")
                wq_tok[bi] = qcast_tok[bi]
                for th in range(2):
                    bq = (ot * 2 + th) % 4
                    mm = None
                    for i in range(32):
                        mm = P.op("t", lambda t, bi=bi, i=i, th=th, bq=bq: t.matmul(
                            bank(bq), WQ[bi][:, i, :], XNL[:, i, th * 512:(th + 1) * 512],
                            start=(i == 0), stop=(i == 31)),
                            waits=([wq_tok[bi], qb_tok.get(bq)] +
                                   (xnl_toks if ot == 0 and i == 0 else [])),
                            tag=f"pq_mm{ot}_{th}_{i}")
                    q_w_free[bi] = mm
                    ev = P.op("s", lambda s, bq=bq: s.activation(
                        out=KSB2, in_=bank(bq), func=AF.Copy),
                        waits=[mm, qb_tok.get("ksb")], tag=f"pq_ev{ot}_{th}")
                    qb_tok[bq] = ev
                    pm = P.op("t", lambda t: t.matmul(bank(6), p64_sb, KSB2,
                        start=True, stop=True), waits=[ev, qb_tok.get(6)], tag=f"pq_pm{ot}_{th}")
                    r1 = P.op("v", lambda v, th=th: v.tensor_tensor(
                        out=TMQ1, in0=KSB2, in1=CQ[:, th, :], op=ALU.mult),
                        waits=[ev], tag=f"pq_r1{ot}_{th}")
                    r2 = P.op("v", lambda v, th=th: v.tensor_tensor(
                        out=TMQ2, in0=bank(6), in1=SQ_[:, th, :], op=ALU.mult),
                        waits=[pm], tag=f"pq_r2{ot}_{th}")
                    qb_tok[6] = r2
                    r3 = P.op("v", lambda v, ot=ot, th=th: v.tensor_tensor(
                        out=QRES[:, ot, th * 512:(th + 1) * 512], in0=TMQ1, in1=TMQ2,
                        op=ALU.add), waits=[r1, r2], tag=f"pq_r3{ot}_{th}")
                    qb_tok["ksb"] = r3
            if "q_d" in debug:
                P.dma("y", "pqd", lambda y: y.dma_start(
                    out=q_d.rearrange("(i p) t -> p i t", p=128), in_=QRES[:]),
                    waits=[P.tok("v")], tag="pq_qdump")

        if "PA" in phases:
            P.start_phase()
            seg, cur = mkseg(CONST_END)
            QRES = A.view(QRES_OFF, BF16, (32, 1024))
            KW = [A.view(seg(8192), BF16, (4096,)) for _ in range(2)]
            VW = [A.view(seg(8192), BF16, (32, 128)) for _ in range(2)]
            MSK = A.view(seg(16384), BF16, (16, 512))
            ESB = [A.view(seg(1024), BF16, (512,)) for _ in range(3)]
            DENR = A.view(seg(2048), F32, (512,))[0:1, :]
            DENB = A.view(seg(1024), BF16, (512,))[0:1, :]
            REC = A.view(seg(2048), F32, (512,))
            AVN = A.view(seg(1024), BF16, (512,))

            kw_tok = [None, None]
            vw_tok = [None, None]
            kw_free = [None, None]
            vw_free = [None, None]
            esb_free = [None, None, None]
            SCB = (0, 1, 5)
            bankA = {}
            denb_tok = denr_tok = rec_tok = avn_tok = ast_tok = None
            msk_tok = None
            for chunk in range(2):
                bound = 16 if chunk == 0 else 32
                mstep0 = 0 if chunk == 0 else 16
                msk_tok = P.dma("y", "pam", lambda y, chunk=chunk: y.dma_start(
                    out=MSK[:], in_=masks[chunk * 16:(chunk + 1) * 16]
                    .rearrange("m p c -> p m c")),
                    waits=[P.tok("v")] if chunk == 1 else [], tag=f"pa_m{chunk}")
                if chunk == 0 and "mdbg" in debug:
                    P.dma("y", "pamd", lambda y: y.dma_start(
                        out=mdbg_d.rearrange("m p c -> p m c"), in_=MSK[:]),
                        waits=[msk_tok], tag="pa_mdump")
                for kvh in range(8):
                    bi = kvh % 2
                    kw_tok[bi] = P.dma("y", f"pak{bi}", lambda y, kvh=kvh, bound=bound, bi=bi:
                        y.dma_start(out=KW[bi][:, 0:bound * 128],
                                    in_=k_d[kvh * 128:(kvh + 1) * 128, 0:bound * 128]),
                        waits=[kw_free[bi]], tag=f"pa_kw{chunk}_{kvh}")
                    vw_tok[bi] = P.dma("y", f"pav{bi}", lambda y, kvh=kvh, bound=bound, bi=bi:
                        y.dma_start(out=VW[bi][:, 0:bound, :],
                                    in_=v_d[0:bound * 128, kvh * 128:(kvh + 1) * 128]
                                    .rearrange("(j p) c -> p j c", p=128)),
                        waits=[vw_free[bi]], tag=f"pa_vw{chunk}_{kvh}")
                    if chunk == 0 and kvh == 0 and "kvdbg" in debug:
                        DBGK = A.view(65536, BF16, (4096,))
                        DBGV = A.view(65536 + 8192, BF16, (32, 128))
                        dc1 = P.op("s", lambda s: s.activation(out=DBGK[:], in_=KW[0][:],
                            func=AF.Copy), waits=[kw_tok[0]], tag="dbg_k")
                        dc2 = P.op("s", lambda s: s.activation(out=DBGV[:], in_=VW[0][:],
                            func=AF.Copy), waits=[vw_tok[0]], tag="dbg_v")
                        P.dma("y", "padbg", lambda y: y.dma_start(
                            out=kvdbg_d[0].rearrange("p (a c) -> p a c", a=32),
                            in_=DBGK[:].rearrange("p (a c) -> p a c", a=32)),
                            waits=[dc1], tag="dbg_kst")
                        P.dma("y", "padbg", lambda y: y.dma_start(
                            out=kvdbg_d[1].rearrange("p (a c) -> p a c", a=32),
                            in_=DBGV[:].rearrange("p a c -> p a c")),
                            waits=[dc2], tag="dbg_vst")
                    for qh in range(4):
                        head = kvh * 4 + qh
                        sc_tok = {}

                        def plan_sc(st_, bi=bi, head=head, chunk=chunk):
                            sb = SCB[st_ % 3]
                            return P.op("t", lambda t, bi=bi, st_=st_, sb=sb,
                                        head=head, chunk=chunk: t.matmul(
                                bank(sb), KW[bi][:, st_ * 128:(st_ + 1) * 128],
                                QRES[:, head, chunk * 512:(chunk + 1) * 512],
                                start=True, stop=True),
                                waits=[kw_tok[bi], bankA.get(sb)],
                                tag=f"pa_sc{chunk}_{head}_{st_}")

                        sc_tok[0] = plan_sc(0)
                        sc_tok[1] = plan_sc(1)
                        for st_ in range(bound):
                            si = st_ % 3
                            sb = SCB[si]
                            ex = P.op("s", lambda s, si=si, sb=sb: s.activation(
                                out=ESB[si][:], in_=bank(sb), func=AF.Exp),
                                waits=[sc_tok[st_], esb_free[si]], tag=f"pa_ex{chunk}_{head}_{st_}")
                            bankA[sb] = ex
                            dep = ex
                            if st_ >= mstep0:
                                dep = P.op("v", lambda v, si=si, st_=st_, mstep0=mstep0:
                                    v.tensor_tensor(out=ESB[si][:], in0=ESB[si][:],
                                                    in1=MSK[:, st_ - mstep0, :], op=ALU.mult),
                                    waits=[ex, msk_tok], tag=f"pa_mk{chunk}_{head}_{st_}")
                            if st_ + 2 < bound:
                                sc_tok[st_ + 2] = plan_sc(st_ + 2)
                            av = P.op("t", lambda t, bi=bi, st_=st_, si=si, bound=bound:
                                t.matmul(bank(2), VW[bi][:, st_, :], ESB[si][:],
                                         start=(st_ == 0), stop=(st_ == bound - 1),
                                         skip_group_check=True),
                                waits=[dep, vw_tok[bi],
                                       bankA.get(2) if st_ == 0 else None],
                                tag=f"pa_av{chunk}_{head}_{st_}")
                            dn = P.op("t", lambda t, st_=st_, si=si, bound=bound:
                                t.matmul(bank(3, rows=1), ones128, ESB[si][:],
                                         start=(st_ == 0), stop=(st_ == bound - 1),
                                         skip_group_check=True),
                                waits=[dep, bankA.get(3) if st_ == 0 else None],
                                tag=f"pa_dn{chunk}_{head}_{st_}")
                            esb_free[si] = dn
                        kw_free[bi] = P.tok("t")
                        vw_free[bi] = P.tok("t")
                        denr_tok = P.op("s", lambda s: s.activation(
                            out=DENR, in_=bank(3, rows=1), func=AF.Copy),
                            waits=[dn, denb_tok], tag=f"pa_de{chunk}_{head}")
                        bankA[3] = denr_tok
                        denb_tok = P.op("v", lambda v: v.tensor_copy(DENB, DENR),
                            waits=[denr_tok, bankA.get("bc")], tag=f"pa_db{chunk}_{head}")
                        bc = P.op("t", lambda t: t.matmul(bank(4), onecol, DENB,
                            start=True, stop=True),
                            waits=[denb_tok, bankA.get(4)], tag=f"pa_bc{chunk}_{head}")
                        bankA["bc"] = bc
                        rec_tok = P.op("v", lambda v: v.reciprocal(REC[:], bank(4)),
                            waits=[bc, avn_tok], tag=f"pa_rc{chunk}_{head}")
                        bankA[4] = rec_tok
                        avn_tok = P.op("v", lambda v: v.tensor_tensor(
                            out=AVN[:], in0=bank(2), in1=REC[:], op=ALU.mult),
                            waits=[rec_tok, av, ast_tok], tag=f"pa_an{chunk}_{head}")
                        bankA[2] = avn_tok
                        ast_tok = P.dma("y", "pas", lambda y, head=head, chunk=chunk:
                            y.dma_start(out=attn_d[head * 128:(head + 1) * 128,
                                                   chunk * 512:(chunk + 1) * 512],
                                        in_=AVN[:]),
                            waits=[avn_tok], tag=f"pa_as{chunk}_{head}")

        if "PO" in phases:
            P.start_phase()
            seg, cur = mkseg(67584)
            ATT = A.view(2048, BF16, (32, 1024))
            X2B = A.view(QRES_OFF, BF16, (32, 1024))
            XN2 = A.view(2048, BF16, (32, 1024))
            WOF = A.view(seg(8192), F32, (16, 128))
            WO = [A.view(seg(8192), BF16, (32, 128)) for _ in range(2)]
            PRJ = [A.view(seg(1024), BF16, (512,)) for _ in range(2)]
            PSQ = A.view(seg(1024), BF16, (512,))
            PRL = [A.view(seg(1024), BF16, (512,)) for _ in range(2)]
            XLF = [A.view(seg(2048), F32, (512,)) for _ in range(2)]
            TMPO = A.view(seg(2048), F32, (512,))
            X2T = [A.view(seg(2048), F32, (512,)) for _ in range(2)]
            RSTB2 = [A.view(seg(2048), F32, (512,)) for _ in range(2)]
            MRB2 = [A.view(seg(2048), F32, (512,)) for _ in range(2)]
            RSTB3 = [A.view(seg(2048), F32, (512,)) for _ in range(2)]
            MRB3 = [A.view(seg(2048), F32, (512,)) for _ in range(2)]
            OROWM = A.view(seg(2048), F32, (512,))[0:1, :]
            OROWS2 = A.view(seg(2048), F32, (512,))[0:1, :]
            OROWT = A.view(seg(2048), F32, (512,))[0:1, :]
            OROWRS = A.view(seg(2048), F32, (512,))[0:1, :]
            OROWMR = A.view(seg(2048), F32, (512,))[0:1, :]
            OROWRSB = A.view(seg(1024), BF16, (512,))[0:1, :]
            OROWMRB = A.view(seg(1024), BF16, (512,))[0:1, :]
            assert cur[0] <= QRES_OFF, ("PO segs overlap X2B", cur[0], QRES_OFF)

            att_tok = {}
            for ot in range(32):
                att_tok[ot] = P.dma("y", f"poa{ot % 4}", lambda y, ot=ot: y.dma_start(
                    out=ATT[:, ot, :], in_=attn_d[ot * 128:(ot + 1) * 128, :]),
                    tag=f"po_att{ot}")
            wo_tok = [None, None]
            wocast = [None, None]
            ob_tok = {}
            wo_free = {}
            prj_free = [None, None]
            for dt in range(32):
                bi = dt % 2
                for hf in range(2):
                    wof_tok = P.dma("g", "powf", lambda y, dt=dt, hf=hf: y.dma_start(
                        out=WOF[:],
                        in_=woT[hf * 2048:(hf + 1) * 2048, dt * 128:(dt + 1) * 128]
                        .rearrange("(i p) c -> p i c", p=128)),
                        waits=[wocast[1 - bi] if hf == 0 else wocast[bi]],
                        tag=f"po_wf{dt}_{hf}")
                    wocast[bi] = P.op("s", lambda s, bi=bi, hf=hf: s.activation(
                        out=WO[bi][:, hf * 16:(hf + 1) * 16, :], in_=WOF[:], func=AF.Copy),
                        waits=[wof_tok, wo_free.get(bi) if hf == 0 else None],
                        tag=f"po_wc{dt}_{hf}")
                wo_tok[bi] = wocast[bi]
                for th in range(2):
                    bq = (dt * 2 + th) % 4
                    ps_ = bq % 2
                    mm = None
                    for ot in range(32):
                        mm = P.op("t", lambda t, bi=bi, ot=ot, th=th, bq=bq: t.matmul(
                            bank(bq), WO[bi][:, ot, :], ATT[:, ot, th * 512:(th + 1) * 512],
                            start=(ot == 0), stop=(ot == 31)),
                            waits=([wo_tok[bi], ob_tok.get(bq)] +
                                   ([att_tok[ot]] if dt == 0 else [])),
                            tag=f"po_mm{dt}_{th}_{ot}")
                    wo_free[bi] = mm
                    ev = P.op("s", lambda s, bq=bq, ps_=ps_: s.activation(
                        out=PRJ[ps_][:], in_=bank(bq), func=AF.Copy),
                        waits=[mm, prj_free[ps_], ob_tok.get(("prjt", ps_))],
                        tag=f"po_ev{dt}_{th}")
                    ob_tok[bq] = ev
                    pst = P.dma("y", f"pop{ps_}", lambda y, dt=dt, th=th, ps_=ps_:
                        y.dma_start(out=proj_d[dt * 128:(dt + 1) * 128,
                                               th * 512:(th + 1) * 512],
                                    in_=PRJ[ps_][:]),
                        waits=[ev], tag=f"po_ps{dt}_{th}")
                    sq = P.op("s", lambda s, ps_=ps_: s.activation(
                        out=PSQ[:], in_=PRJ[ps_][:], func=AF.Square),
                        waits=[ev, ob_tok.get("psq")], tag=f"po_sq{dt}_{th}")
                    m1 = P.op("t", lambda t, th=th, dt=dt, ps_=ps_: t.matmul(
                        bank(4 + th, rows=1), ones128, PRJ[ps_][:],
                        start=(dt == 0), stop=(dt == 31)),
                        waits=[ev, ob_tok.get(4 + th) if dt == 0 else None],
                        tag=f"po_s1{dt}_{th}")
                    m2 = P.op("t", lambda t, th=th, dt=dt: t.matmul(
                        bank(6 + th, rows=1), ones128, PSQ[:],
                        start=(dt == 0), stop=(dt == 31)),
                        waits=[sq, ob_tok.get(6 + th) if dt == 0 else None],
                        tag=f"po_s2{dt}_{th}")
                    ob_tok["psq"] = m2
                    prj_free[ps_] = pst
                    ob_tok[("prjt", ps_)] = P.tok("t")
                    ob_tok[("stat", th)] = (m1, m2)
            oproj_done = P.tok("t")

            def ln_rows(pfx, sum_bank, sq_bank, RSB, MRB_, waits_extra, nw=512):
                e1 = P.op("s", lambda s: s.activation(out=OROWM[:, 0:nw],
                    in_=bank(sum_bank, nw, rows=1), func=AF.Identity, scale=invS_sb),
                    waits=waits_extra, tag=f"{pfx}_m")
                e2 = P.op("s", lambda s: s.activation(out=OROWS2[:, 0:nw],
                    in_=bank(sq_bank, nw, rows=1), func=AF.Identity, scale=invS_sb),
                    waits=waits_extra, tag=f"{pfx}_s2")
                v1 = P.op("v", lambda v: v.tensor_tensor(out=OROWT[:, 0:nw],
                    in0=OROWM[:, 0:nw], in1=OROWM[:, 0:nw], op=ALU.mult),
                    waits=[e1], tag=f"{pfx}_mm")
                v2 = P.op("v", lambda v: v.tensor_tensor(out=OROWT[:, 0:nw],
                    in0=OROWS2[:, 0:nw], in1=OROWT[:, 0:nw], op=ALU.subtract),
                    waits=[v1, e2], tag=f"{pfx}_var")
                e3 = P.op("s", lambda s: s.activation(out=OROWRS[:, 0:nw],
                    in_=OROWT[:, 0:nw], func=AF.Sqrt, bias=eps_sb),
                    waits=[v2], tag=f"{pfx}_sqrt")
                v3 = P.op("v", lambda v: v.reciprocal(OROWRS[:, 0:nw], OROWRS[:, 0:nw]),
                    waits=[e3], tag=f"{pfx}_rcp")
                v4 = P.op("v", lambda v: v.tensor_tensor(out=OROWMR[:, 0:nw],
                    in0=OROWM[:, 0:nw], in1=OROWRS[:, 0:nw], op=ALU.mult),
                    waits=[v3], tag=f"{pfx}_mr")
                v5 = P.op("v", lambda v: v.tensor_copy(OROWRSB[:, 0:nw], OROWRS[:, 0:nw]),
                    waits=[v3], tag=f"{pfx}_cb1")
                v6 = P.op("v", lambda v: v.tensor_copy(OROWMRB[:, 0:nw], OROWMR[:, 0:nw]),
                    waits=[v4], tag=f"{pfx}_cb2")
                b1 = P.op("t", lambda t: t.matmul(bank(sum_bank, nw), onecol,
                    OROWRSB[:, 0:nw], start=True, stop=True),
                    waits=[v5, e1], tag=f"{pfx}_b1")
                b2 = P.op("t", lambda t: t.matmul(bank(sq_bank, nw), onecol,
                    OROWMRB[:, 0:nw], start=True, stop=True),
                    waits=[v6, e2], tag=f"{pfx}_b2")
                eb1 = P.op("s", lambda s: s.activation(out=RSB[:, 0:nw],
                    in_=bank(sum_bank, nw), func=AF.Copy), waits=[b1], tag=f"{pfx}_eb1")
                eb2 = P.op("s", lambda s: s.activation(out=MRB_[:, 0:nw],
                    in_=bank(sq_bank, nw), func=AF.Copy), waits=[b2], tag=f"{pfx}_eb2")
                return eb1, eb2

            ln2_toks = {}
            for th in range(2):
                m1, m2 = ob_tok[("stat", th)]
                ln2_toks[th] = ln_rows(f"po_l2{th}", 4 + th, 6 + th,
                                       RSTB2[th], MRB2[th], [m1, m2])

            x2_toks = {}
            prl_tok = [None, None]
            xl_tok = [None, None]
            x2st_tok = [None, None]
            tmpo_tok = None
            x2b_tok = {}
            for dt in range(32):
                for th in range(2):
                    sl = (dt * 2 + th) % 2
                    prl_tok[sl] = P.dma("y", f"por{sl}", lambda y, dt=dt, th=th, sl=sl:
                        y.dma_start(out=PRL[sl][:],
                                    in_=proj_d[dt * 128:(dt + 1) * 128,
                                               th * 512:(th + 1) * 512]),
                        waits=[x2b_tok.get(("prl", sl)),
                               P.tok("pop0"), P.tok("pop1")],
                        tag=f"po_rl{dt}_{th}")
                    xl_tok[sl] = P.dma("y", f"pox{sl}", lambda y, sl=sl, dt=dt, th=th:
                        y.dma_start(out=XLF[sl][:],
                                    in_=xtl[dt * 128:(dt + 1) * 128,
                                            th * 512:(th + 1) * 512]),
                        waits=[x2b_tok.get(("xlf", sl))],
                        tag=f"po_xl{dt}_{th}")
                    t1 = P.op("v", lambda v, sl=sl, th=th: v.tensor_tensor(
                        out=TMPO[:], in0=PRL[sl][:], in1=RSTB2[th][:], op=ALU.mult),
                        waits=[prl_tok[sl], ln2_toks[th][0], tmpo_tok], tag=f"po_t1{dt}_{th}")
                    t2 = P.op("v", lambda v, th=th: v.tensor_tensor(
                        out=TMPO[:], in0=TMPO[:], in1=MRB2[th][:], op=ALU.subtract),
                        waits=[t1, ln2_toks[th][1]], tag=f"po_t2{dt}_{th}")
                    x2v = P.op("v", lambda v, sl=sl: v.tensor_tensor(
                        out=X2T[sl][:], in0=TMPO[:], in1=XLF[sl][:], op=ALU.add),
                        waits=[t2, xl_tok[sl], x2st_tok[sl],
                               x2b_tok.get(("xlf", sl))], tag=f"po_x2{dt}_{th}")
                    tmpo_tok = x2v
                    x2b_tok[("prl", sl)] = x2v
                    x2st_tok[sl] = P.dma("y", f"po2{sl}", lambda y, dt=dt, th=th, sl=sl:
                        y.dma_start(out=x2_d[dt * 128:(dt + 1) * 128,
                                             th * 512:(th + 1) * 512],
                                    in_=X2T[sl][:]),
                        waits=[x2v], tag=f"po_2s{dt}_{th}")
                    cb = P.op("s", lambda s, dt=dt, th=th, sl=sl: s.activation(
                        out=X2B[:, dt, th * 512:(th + 1) * 512], in_=X2T[sl][:],
                        func=AF.Copy), waits=[x2v], tag=f"po_cb{dt}_{th}")
                    x2b_tok[("xlf", sl)] = cb
                    sq = P.op("s", lambda s, dt=dt, th=th: s.activation(
                        out=PSQ[:], in_=X2B[:, dt, th * 512:(th + 1) * 512],
                        func=AF.Square), waits=[cb, x2b_tok.get("psq3")],
                        tag=f"po_q3{dt}_{th}")
                    m1 = P.op("t", lambda t, th=th, dt=dt: t.matmul(
                        bank(4 + th, rows=1), ones128,
                        X2B[:, dt, th * 512:(th + 1) * 512],
                        start=(dt == 0), stop=(dt == 31)),
                        waits=[cb, ln2_toks[th][0] if dt == 0 else None],
                        tag=f"po_31{dt}_{th}")
                    m2 = P.op("t", lambda t, th=th, dt=dt: t.matmul(
                        bank(6 + th, rows=1), ones128, PSQ[:],
                        start=(dt == 0), stop=(dt == 31)),
                        waits=[sq, ln2_toks[th][1] if dt == 0 else None],
                        tag=f"po_32{dt}_{th}")
                    x2b_tok["psq3"] = m2
                    x2b_tok[("stat", th)] = (m1, m2)
            ln3_toks = {}
            for th in range(2):
                m1, m2 = x2b_tok[("stat", th)]
                ln3_toks[th] = ln_rows(f"po_l3{th}", 4 + th, 6 + th,
                                       RSTB3[th], MRB3[th], [m1, m2])
            xn2_tok = None
            for dt in range(32):
                for th in range(2):
                    t1 = P.op("v", lambda v, dt=dt, th=th: v.tensor_tensor(
                        out=TMPO[:], in0=X2B[:, dt, th * 512:(th + 1) * 512],
                        in1=RSTB3[th][:], op=ALU.mult),
                        waits=[ln3_toks[th][0], tmpo_tok,
                               oproj_done if dt == 0 and th == 0 else None],
                        tag=f"po_n1{dt}_{th}")
                    xn2_tok = P.op("v", lambda v, dt=dt, th=th: v.tensor_tensor(
                        out=XN2[:, dt, th * 512:(th + 1) * 512], in0=TMPO[:],
                        in1=MRB3[th][:], op=ALU.subtract),
                        waits=[t1, ln3_toks[th][1]], tag=f"po_n2{dt}_{th}")
                    tmpo_tok = xn2_tok
            if "xn2dbg" in debug:
                P.dma("y", "pod1", lambda y: y.dma_start(
                    out=xn2dbg_d.rearrange("(i p) t -> p i t", p=128), in_=XN2[:]),
                    waits=[P.tok("v")], tag="po_xn2dump")
                P.dma("y", "pod2", lambda y: y.dma_start(
                    out=r3dbg_d[0].rearrange("(z p) c -> p z c", p=128) if False else
                        r3dbg_d[0], in_=RSTB3[0][:]), waits=[P.tok("v")], tag="po_r3a")
                P.dma("y", "pod2", lambda y: y.dma_start(
                    out=r3dbg_d[1], in_=RSTB3[1][:]), waits=[P.tok("v")], tag="po_r3b")
                P.dma("y", "pod2", lambda y: y.dma_start(
                    out=r3dbg_d[2], in_=MRB3[0][:]), waits=[P.tok("v")], tag="po_r3c")
                P.dma("y", "pod2", lambda y: y.dma_start(
                    out=r3dbg_d[3], in_=MRB3[1][:]), waits=[P.tok("v")], tag="po_r3d")

        if "PF1" in phases:
            P.start_phase()
            seg, cur = mkseg(67584)
            XN2 = A.view(2048, BF16, (32, 1024))
            if "PO" not in phases or "force_xn2t" in debug:
                P.dma("y", "f1x", lambda y: y.dma_start(
                    out=XN2[:], in_=xn2t.rearrange("(i p) t -> p i t", p=128)),
                    tag="f1_xn2t")
                P.start_phase()
            WCAR = [A.view(seg(16384), BF16, (32, 256)) for _ in range(3)]
            OH32 = A.view(seg(2048), BF16, (32, 32))
            ZPG = A.view(seg(22016), BF16, (11008,))
            ZPU = A.view(seg(22016), BF16, (11008,))
            CSX = A.view(seg(2048), BF16, (2, 512))
            GS = [A.view(seg(1024), BF16, (512,)) for _ in range(2)]
            INTO = [A.view(seg(2048), BF16, (2, 512)) for _ in range(2)]

            oh_tok = P.dma("y", "f1o", lambda y: y.dma_start(
                out=OH32[:], in_=oh32.rearrange("m p c -> p m c")), tag="f1_oh")
            zpg_tok = P.dma("y", "f1z", lambda y: y.dma_start(
                out=ZPG[0:32, :], in_=zpg_d), tag="f1_zg")
            zpu_tok = P.dma("y", "f1z", lambda y: y.dma_start(
                out=ZPU[0:32, :], in_=zpu_d), tag="f1_zu")
            fb_tok = {}
            cs_ev = {}
            for th in range(2):
                mm = None
                for i in range(32):
                    mm = P.op("t", lambda t, i=i, th=th: t.matmul(
                        bank(4 + th, rows=32), OH32[:, i, :],
                        XN2[:, i, th * 512:(th + 1) * 512],
                        start=(i == 0), stop=(i == 31)),
                        waits=[oh_tok] if i == 0 else [], tag=f"f1_cs{th}_{i}")
                cs_ev[th] = P.op("s", lambda s, th=th: s.activation(
                    out=CSX[0:32, th, :], in_=bank(4 + th, rows=32), func=AF.Copy),
                    waits=[mm], tag=f"f1_cse{th}")
            spans = [(i0, 2) for i0 in range(0, NT_I, 2)]
            slot_free = [None, None, None]
            slot_tok = {}
            gs_free = [None, None]
            into_free = [None, None]
            bank_f = {}
            sidx = 0
            for (i0, ni) in spans:
                gslot = sidx % 3
                uslot = (sidx + 1) % 3
                sidx += 2
                gtok = P.dma("g", f"f1w{gslot}", lambda y, i0=i0, ni=ni, gslot=gslot:
                    y.dma_start(out=WCAR[gslot][:, :, 0:ni * 128],
                                in_=wtg_d[:, :, i0 * 128:(i0 + ni) * 128]
                                .rearrange("a p c -> p a c")),
                    waits=[slot_free[gslot]], tag=f"f1_wg{i0}")
                utok = P.dma("g", f"f1w{uslot}", lambda y, i0=i0, ni=ni, uslot=uslot:
                    y.dma_start(out=WCAR[uslot][:, :, 0:ni * 128],
                                in_=wtu_d[:, :, i0 * 128:(i0 + ni) * 128]
                                .rearrange("a p c -> p a c")),
                    waits=[slot_free[uslot]], tag=f"f1_wu{i0}")
                for ii in range(ni):
                    par = ii % 2
                    gb = par * 4
                    ub = par * 4 + 2
                    for th in range(2):
                        mm = None
                        for kt in range(32):
                            mm = P.op("t", lambda t, kt=kt, ii=ii, th=th, gb=gb, gslot=gslot:
                                t.matmul(bank(gb + th), WCAR[gslot][:, kt, ii * 128:(ii + 1) * 128],
                                         XN2[:, kt, th * 512:(th + 1) * 512],
                                         start=(kt == 0), stop=False),
                                waits=([gtok, bank_f.get(gb + th)] if kt == 0 else []),
                                tag=f"f1_gm{i0}_{ii}_{th}_{kt}")
                        gmm = P.op("t", lambda t, ii=ii, th=th, gb=gb, i0=i0: t.matmul(
                            bank(gb + th), ZPG[0:32, (i0 + ii) * 128:(i0 + ii + 1) * 128],
                            CSX[0:32, th, :], start=False, stop=True),
                            waits=[zpg_tok, cs_ev[th]], tag=f"f1_gz{i0}_{ii}_{th}")
                        mm = None
                        for kt in range(32):
                            mm = P.op("t", lambda t, kt=kt, ii=ii, th=th, ub=ub, uslot=uslot:
                                t.matmul(bank(ub + th), WCAR[uslot][:, kt, ii * 128:(ii + 1) * 128],
                                         XN2[:, kt, th * 512:(th + 1) * 512],
                                         start=(kt == 0), stop=False),
                                waits=([utok, bank_f.get(ub + th)] if kt == 0 else []),
                                tag=f"f1_um{i0}_{ii}_{th}_{kt}")
                        umm = P.op("t", lambda t, ii=ii, th=th, ub=ub, i0=i0: t.matmul(
                            bank(ub + th), ZPU[0:32, (i0 + ii) * 128:(i0 + ii + 1) * 128],
                            CSX[0:32, th, :], start=False, stop=True),
                            waits=[zpu_tok], tag=f"f1_uz{i0}_{ii}_{th}")
                        sil = P.op("s", lambda s, th=th, gb=gb, par=par: s.activation(
                            out=GS[par][:], in_=bank(gb + th), func=AF.Silu),
                            waits=[gmm, gs_free[par]], tag=f"f1_si{i0}_{ii}_{th}")
                        bank_f[gb + th] = sil
                        mul = P.op("v", lambda v, th=th, ub=ub, par=par: v.tensor_tensor(
                            out=INTO[par][:, th, :], in0=bank(ub + th), in1=GS[par][:],
                            op=ALU.mult),
                            waits=[umm, sil, into_free[par]], tag=f"f1_ml{i0}_{ii}_{th}")
                        bank_f[ub + th] = mul
                        gs_free[par] = mul
                        if th == 1:
                            into_free[par] = P.dma("y", f"f1i{par}",
                                lambda y, i0=i0, ii=ii, par=par: y.dma_start(
                                    out=inter_d[(i0 + ii) * 128:(i0 + ii + 1) * 128, :]
                                        .rearrange("p (a b) -> p a b", a=2),
                                    in_=INTO[par][:]),
                                waits=[mul], tag=f"f1_is{i0}_{ii}")
                    slot_tok[gslot] = P.tok("t")
                    slot_tok[uslot] = P.tok("t")
                slot_free[gslot] = slot_tok[gslot]
                slot_free[uslot] = slot_tok[uslot]
            pf1_done = P.tok("t")

        if "PF2" in phases:
            P.start_phase()
            seg, cur = mkseg(67584)
            D16 = A.view(2048, F16, (32, 1024))
            OH86 = A.view(seg(14792), BF16, (86, 86))
            ZPD = A.view(seg(8192), BF16, (4096,))
            CSI = A.view(seg(2048), BF16, (2, 512))
            INT2 = [A.view(seg(2048), BF16, (2, 512)) for _ in range(2)]
            WDT = [A.view(seg(1024), BF16, (512,)) for _ in range(2)]
            NRES = 43
            INTR = A.view(seg(NRES * 2048), BF16, (NRES, 2, 512))

            oh2_tok = P.dma("y", "f2o", lambda y: y.dma_start(
                out=OH86[:], in_=oh86.rearrange("m p c -> p m c")), tag="f2_oh")
            zpd_tok = P.dma("y", "f2z", lambda y: y.dma_start(
                out=ZPD[0:86, :], in_=zpd_d), tag="f2_zd")
            int_tok = [None, None]
            int_free = [None, None]
            res_gate = [None, None]
            intr_tok = {}
            csi_ev = {}
            mm_ = {0: None, 1: None}
            for i in range(NT_I):
                sl = i % 2
                if i < NRES:
                    intr_tok[i] = P.dma("g", f"f2i{sl}", lambda y, i=i: y.dma_start(
                        out=INTR[:, i, :, :], in_=inter_d[i * 128:(i + 1) * 128, :]
                        .rearrange("p (a b) -> p a b", a=2)),
                        waits=[res_gate[sl]], tag=f"f2_il{i}")
                    rhs_ = lambda th, i=i: INTR[:, i, th, :]
                    ld_tok = intr_tok[i]
                else:
                    int_tok[sl] = P.dma("g", f"f2i{sl}", lambda y, i=i, sl=sl: y.dma_start(
                        out=INT2[sl][:], in_=inter_d[i * 128:(i + 1) * 128, :]
                        .rearrange("p (a b) -> p a b", a=2)),
                        waits=[int_free[sl]], tag=f"f2_il{i}")
                    rhs_ = lambda th, sl=sl: INT2[sl][:, th, :]
                    ld_tok = int_tok[sl]
                for th in range(2):
                    mm_[th] = P.op("t", lambda t, i=i, th=th, rhs_=rhs_: t.matmul(
                        bank(4 + th, rows=86), OH86[:, i, :], rhs_(th),
                        start=(i == 0), stop=(i == NT_I - 1)),
                        waits=[ld_tok] + ([oh2_tok] if i == 0 else []),
                        tag=f"f2_cs{i}_{th}")
                res_gate[sl] = mm_[1]
                if i >= NRES:
                    int_free[sl] = mm_[1]
            for th in range(2):
                csi_ev[th] = P.op("s", lambda s, th=th: s.activation(
                    out=CSI[0:86, th, :], in_=bank(4 + th, rows=86), func=AF.Copy),
                    waits=[mm_[th]], tag=f"f2_cse{th}")
            wd_tok = [None, None]
            wd_free = [None, None]
            bank_g = {}
            for hg in range(8):
                mm_last = {}
                for i in range(NT_I):
                    sl = i % 2
                    if i >= NRES:
                        int_tok[sl] = P.dma("g", f"f2j{sl}", lambda y, i=i, sl=sl: y.dma_start(
                            out=INT2[sl][:], in_=inter_d[i * 128:(i + 1) * 128, :]
                            .rearrange("p (a b) -> p a b", a=2)),
                            waits=[int_free[sl]], tag=f"f2_jl{hg}_{i}")
                    wsl = i % 2
                    wd_tok[wsl] = P.dma("y", f"f2w{wsl}", lambda y, hg=hg, i=i, wsl=wsl:
                        y.dma_start(out=WDT[wsl][:],
                                    in_=wtd_d[i, :, hg * 512:(hg + 1) * 512]),
                        waits=[wd_free[wsl]], tag=f"f2_wl{hg}_{i}")
                    if i < NRES:
                        rhs2 = lambda th, i=i: INTR[:, i, th, :]
                        ld2 = intr_tok[i]
                    else:
                        rhs2 = lambda th, sl=sl: INT2[sl][:, th, :]
                        ld2 = int_tok[sl]
                    for ht in range(4):
                        for th in range(2):
                            bq = ht * 2 + th
                            mm_last[bq] = P.op("t", lambda t, i=i, ht=ht, th=th, bq=bq,
                                               rhs2=rhs2, wsl=wsl: t.matmul(
                                bank(bq), WDT[wsl][:, ht * 128:(ht + 1) * 128],
                                rhs2(th), start=(i == 0), stop=False),
                                waits=[ld2, wd_tok[wsl],
                                       bank_g.get(bq) if i == 0 else None],
                                tag=f"f2_dm{hg}_{i}_{bq}")
                    if i >= NRES:
                        int_free[sl] = mm_last[7]
                    wd_free[wsl] = mm_last[7]
                for ht in range(4):
                    for th in range(2):
                        bq = ht * 2 + th
                        zm = P.op("t", lambda t, hg=hg, ht=ht, th=th, bq=bq: t.matmul(
                            bank(bq), ZPD[0:86, (hg * 4 + ht) * 128:(hg * 4 + ht + 1) * 128],
                            CSI[0:86, th, :], start=False, stop=True),
                            waits=[zpd_tok, csi_ev[th], mm_last[bq]],
                            tag=f"f2_dz{hg}_{bq}")
                        bank_g[bq] = P.op("s", lambda s, hg=hg, ht=ht, th=th, bq=bq:
                            s.activation(out=D16[:, hg * 4 + ht, th * 512:(th + 1) * 512],
                                         in_=bank(bq), func=AF.Copy),
                            waits=[zm, pf1_done if hg == 0 and ht == 0 and th == 0 else None],
                            tag=f"f2_de{hg}_{bq}")

        if "PL" in phases:
            P.start_phase()
            seg, cur = mkseg(132 * 1024)
            D16 = A.view(2048, F16, (32, 1024))
            DBF = A.view(seg(1024), BF16, (512,))
            DSQ = A.view(seg(1024), BF16, (512,))
            X2L = [A.view(seg(2048), F32, (512,)) for _ in range(4)]
            OUTT = [A.view(seg(2048), F32, (512,)) for _ in range(4)]
            TMPL = A.view(seg(2048), F32, (512,))
            RSTB4 = [A.view(seg(2048), F32, (512,)) for _ in range(2)]
            MRB4 = [A.view(seg(2048), F32, (512,)) for _ in range(2)]
            LROWM = A.view(seg(2048), F32, (512,))[0:1, :]
            LROWS2 = A.view(seg(2048), F32, (512,))[0:1, :]
            LROWT = A.view(seg(2048), F32, (512,))[0:1, :]
            LROWRS = A.view(seg(2048), F32, (512,))[0:1, :]
            LROWMR = A.view(seg(2048), F32, (512,))[0:1, :]
            LROWRSB = A.view(seg(1024), BF16, (512,))[0:1, :]
            LROWMRB = A.view(seg(1024), BF16, (512,))[0:1, :]

            lb_tok = {}
            for dt in range(32):
                for th in range(2):
                    cb = P.op("s", lambda s, dt=dt, th=th: s.activation(
                        out=DBF[:], in_=D16[:, dt, th * 512:(th + 1) * 512], func=AF.Copy),
                        waits=[lb_tok.get("dbf")], tag=f"pl_cb{dt}_{th}")
                    sq = P.op("s", lambda s: s.activation(
                        out=DSQ[:], in_=DBF[:], func=AF.Square),
                        waits=[cb, lb_tok.get("dsq")], tag=f"pl_sq{dt}_{th}")
                    m1 = P.op("t", lambda t, th=th, dt=dt: t.matmul(
                        bank(0 + th, rows=1), ones128, DBF[:],
                        start=(dt == 0), stop=(dt == 31)),
                        waits=[cb, lb_tok.get(("b", th)) if dt == 0 else None],
                        tag=f"pl_s1{dt}_{th}")
                    lb_tok["dbf"] = m1
                    m2 = P.op("t", lambda t, th=th, dt=dt: t.matmul(
                        bank(2 + th, rows=1), ones128, DSQ[:],
                        start=(dt == 0), stop=(dt == 31)),
                        waits=[sq, lb_tok.get(("c", th)) if dt == 0 else None],
                        tag=f"pl_s2{dt}_{th}")
                    lb_tok["dsq"] = m2
                    lb_tok[("stat", th)] = (m1, m2)

            def ln_rows4(pfx, sum_bank, sq_bank, RSB, MRB_, waits_extra):
                e1 = P.op("s", lambda s: s.activation(out=LROWM,
                    in_=bank(sum_bank, 512, rows=1), func=AF.Identity, scale=invS_sb),
                    waits=waits_extra, tag=f"{pfx}_m")
                e2 = P.op("s", lambda s: s.activation(out=LROWS2,
                    in_=bank(sq_bank, 512, rows=1), func=AF.Identity, scale=invS_sb),
                    waits=waits_extra, tag=f"{pfx}_s2")
                v1 = P.op("v", lambda v: v.tensor_tensor(out=LROWT, in0=LROWM, in1=LROWM,
                    op=ALU.mult), waits=[e1], tag=f"{pfx}_mm")
                v2 = P.op("v", lambda v: v.tensor_tensor(out=LROWT, in0=LROWS2, in1=LROWT,
                    op=ALU.subtract), waits=[v1, e2], tag=f"{pfx}_var")
                e3 = P.op("s", lambda s: s.activation(out=LROWRS, in_=LROWT, func=AF.Sqrt,
                    bias=eps_sb), waits=[v2], tag=f"{pfx}_sqrt")
                v3 = P.op("v", lambda v: v.reciprocal(LROWRS, LROWRS), waits=[e3],
                    tag=f"{pfx}_rcp")
                v4 = P.op("v", lambda v: v.tensor_tensor(out=LROWMR, in0=LROWM, in1=LROWRS,
                    op=ALU.mult), waits=[v3], tag=f"{pfx}_mr")
                v5 = P.op("v", lambda v: v.tensor_copy(LROWRSB, LROWRS), waits=[v3],
                    tag=f"{pfx}_cb1")
                v6 = P.op("v", lambda v: v.tensor_copy(LROWMRB, LROWMR), waits=[v4],
                    tag=f"{pfx}_cb2")
                b1 = P.op("t", lambda t: t.matmul(bank(sum_bank), onecol, LROWRSB,
                    start=True, stop=True), waits=[v5, e1], tag=f"{pfx}_b1")
                b2 = P.op("t", lambda t: t.matmul(bank(sq_bank), onecol, LROWMRB,
                    start=True, stop=True), waits=[v6, e2], tag=f"{pfx}_b2")
                eb1 = P.op("s", lambda s: s.activation(out=RSB[:], in_=bank(sum_bank),
                    func=AF.Copy), waits=[b1], tag=f"{pfx}_eb1")
                eb2 = P.op("s", lambda s: s.activation(out=MRB_[:], in_=bank(sq_bank),
                    func=AF.Copy), waits=[b2], tag=f"{pfx}_eb2")
                return eb1, eb2

            ln4_toks = {}
            for th in range(2):
                m1, m2 = lb_tok[("stat", th)]
                ln4_toks[th] = ln_rows4(f"pl_l4{th}", 0 + th, 2 + th,
                                        RSTB4[th], MRB4[th], [m1, m2])
            x2l_tok = [None] * 4
            outt_free = [None] * 4
            ost_tok = [None] * 4
            tmpl_tok = None
            for dt in range(32):
                for th in range(2):
                    sl = (dt * 2 + th) % 4
                    x2l_tok[sl] = P.dma("y", f"plx{sl}", lambda y, dt=dt, th=th, sl=sl:
                        y.dma_start(out=X2L[sl][:],
                                    in_=x2_d[dt * 128:(dt + 1) * 128,
                                             th * 512:(th + 1) * 512]),
                        waits=[outt_free[sl]], tag=f"pl_xl{dt}_{th}")
                    t1 = P.op("v", lambda v, dt=dt, th=th: v.tensor_tensor(
                        out=TMPL[:], in0=D16[:, dt, th * 512:(th + 1) * 512],
                        in1=RSTB4[th][:], op=ALU.mult),
                        waits=[ln4_toks[th][0], tmpl_tok], tag=f"pl_t1{dt}_{th}")
                    t2 = P.op("v", lambda v, th=th: v.tensor_tensor(
                        out=TMPL[:], in0=TMPL[:], in1=MRB4[th][:], op=ALU.subtract),
                        waits=[t1, ln4_toks[th][1]], tag=f"pl_t2{dt}_{th}")
                    ov = P.op("v", lambda v, sl=sl: v.tensor_tensor(
                        out=OUTT[sl][:], in0=TMPL[:], in1=X2L[sl][:], op=ALU.add),
                        waits=[t2, x2l_tok[sl], ost_tok[sl]], tag=f"pl_o{dt}_{th}")
                    tmpl_tok = ov
                    outt_free[sl] = ov
                    ost_tok[sl] = P.dma("y", f"pls{sl}", lambda y, dt=dt, th=th, sl=sl:
                        y.dma_start(out=out_d[dt * 128:(dt + 1) * 128,
                                              th * 512:(th + 1) * 512],
                                    in_=OUTT[sl][:]),
                        waits=[ov], tag=f"pl_os{dt}_{th}")

        fin = [P.tok(s) for s in list(P.cnt.keys())]
        P.op("y", lambda y: y.nop(), waits=fin, tag="final_fence")

        emit_all(nc, P)

    return nc, P


def emit_all(nc, P):
    semh = {}
    with ExitStack() as es:
        for name in sorted(P.sem_names):
            semh[name] = es.enter_context(nc.semaphore(name))
        with nc.Block() as block:
            @block.tensor
            def _(t):
                _emit_one(P, "t", t, semh)
            @block.vector
            def _(v):
                _emit_one(P, "v", v, semh)
            @block.scalar
            def _(s):
                _emit_one(P, "s", s, semh)
            @block.gpsimd
            def _(g):
                _emit_one(P, "g", g, semh)
            @block.sync
            def _(y):
                _emit_one(P, "y", y, semh)


def _emit_one(P, name, eng, semh):
    last = {}
    for entry in P.ops[name]:
        for (sem, val) in entry["waits"]:
            if last.get(sem, -1) >= val:
                continue
            eng.wait_ge(semh[sem], val)
            last[sem] = val
        ins = entry["fn"](eng)
        if ins is None:
            raise RuntimeError(f"op emitted nothing: {entry['tag']}")
        ins.then_inc(semh[entry["inc"]], entry["amt"])


def kernel(**inputs):
    percore = stage_inputs(inputs)
    nc, _plan = build_nc()
    res = run_bass_kernel_spmd(nc, percore, core_ids=list(range(8)))
    out = np.empty((2, 4096, 4096), np.float32)
    for c in range(8):
        b, g = c // 4, c % 4
        o = np.asarray(res.results[c]["out"])
        out[b, 512 * g:512 * (g + 1)] = o[:, 0:512].T
        out[b, 512 * (7 - g):512 * (8 - g)] = o[:, 512:1024].T
    return out
